# version 02
# baseline (speedup 1.0000x reference)
import contextlib
import math

import numpy as np
import ml_dtypes

import concourse.bass as bass
import concourse.mybir as mybir
from concourse.bass_utils import run_bass_kernel_spmd

F32 = mybir.dt.float32
BF16 = mybir.dt.bfloat16
I32 = mybir.dt.int32
AF = mybir.ActivationFunctionType
ALU = mybir.AluOpType
AX = mybir.AxisListType
bfdt = ml_dtypes.bfloat16

T = 4096
D = 1024
NT = 32
NDT = 8
NEG = -1.0e30

ENGINES = ("sync", "gpsimd", "vector", "scalar", "tensor")


class Sem:
    __slots__ = ("h", "n", "name")

    def __init__(self, h, name):
        self.h = h
        self.n = 0
        self.name = name


class Prog:

    def __init__(self, nc, stack):
        self.nc = nc
        self.stack = stack
        self.streams = {e: [] for e in ENGINES}
        self.nsem = 0
        self._selfsem = {}

    def sem(self, name=None):
        name = name or f"s{self.nsem}"
        self.nsem += 1
        h = self.stack.enter_context(self.nc.semaphore(name))
        return Sem(h, name)

    def _self(self, eng):
        if eng not in self._selfsem:
            self._selfsem[eng] = self.sem(f"self_{eng}")
        return self._selfsem[eng]

    def selfwait(self, eng):
        s = self._self(eng)
        return (s, s.n)

    def add(self, eng, fn, wait=(), inc=(), selfsync=False):
        wl = []
        if eng == "vector" and eng in self._selfsem:
            s = self._selfsem[eng]
            if s.n > 0:
                wl.append((s, s.n))
        for s, v in wait:
            assert v <= s.n, f"wait on future count of {s.name}: {v} > {s.n}"
            if v > 0:
                wl.append((s, v))
        il = []
        for s, k in inc:
            s.n += k
            il.append((s, k))
        if selfsync:
            s = self._self(eng)
            s.n += 1
            il.append((s, 1))
        assert len(il) <= 1, "at most one sync update per instruction"
        self.streams[eng].append((fn, wl, il))

    def emit(self):
        with self.nc.Block() as block:
            for e in ENGINES:
                ops = self.streams[e]
                if not ops:
                    continue

                def body(engine, ops=ops):
                    lastw = {}
                    for fn, wl, il in ops:
                        for s, v in wl:
                            if lastw.get(id(s), 0) >= v:
                                continue
                            engine.wait_ge(s.h, v)
                            lastw[id(s)] = v
                        ins = fn(engine) if fn is not None else None
                        for s, k in il:
                            assert ins is not None, "inc requires an instruction"
                            ins = ins.then_inc(s.h, k)

                getattr(block, e)(body)


def w(sem):
    return (sem, sem.n)


class Arena:

    def __init__(self, nc, stack, size_bytes, name="arena"):
        self.nc = nc
        base_t = stack.enter_context(nc.sbuf_tensor(name, [128, size_bytes], mybir.dt.uint8))
        self.base = nc.lookup_mloc(base_t).addr
        self.size = size_bytes

    def at(self, off_kb, name, shape, dtype):
        off = int(off_kb * 1024)
        nbytes = int(np.prod(shape[1:])) * (2 if dtype == BF16 else 4)
        assert off + nbytes <= self.size, f"{name} overflows arena"
        return self.nc.alloc_sbuf_tensor_at(name, shape, dtype, offset=self.base + off)



def hT(a):
    return np.ascontiguousarray(np.asarray(a).T)


def hC(a):
    return np.ascontiguousarray(np.asarray(a))



class Common:

    def __init__(self, p: Prog, x_cols=4 * D, lnbf_cols=2 * D):
        self.p = p
        nc = p.nc
        st = p.stack
        self.x_sb = st.enter_context(nc.sbuf_tensor("x_sb", [128, x_cols], F32))
        self.lnbf = st.enter_context(nc.sbuf_tensor("lnbf", [128, lnbf_cols], BF16))
        self.sq_scr = None
        self.stat = st.enter_context(nc.sbuf_tensor("stat", [128, 16], F32))
        self.tr_ps = st.enter_context(nc.psum_tensor("tr_ps", [128, 1024], BF16))
        self.ident = st.enter_context(nc.sbuf_tensor("ident", [128, 128], BF16))
        self.ones1 = st.enter_context(nc.sbuf_tensor("ones1", [1, 128], BF16))
        self.s_ssq = p.sem("ssq")
        self.s_var = p.sem("var")
        self.s_rstd = p.sem("rstd")
        self.s_lnbf = p.sem("lnbf")
        self.s_tr = p.sem("tr")
        self.s_lnT = p.sem("lnT")
        self.ident_dram = nc.inline_tensor(np.eye(128, dtype=bfdt), name="identc")
        self.ones1_dram = nc.inline_tensor(np.ones((1, 128), dtype=bfdt), name="ones1c")
        self.cbuf = st.enter_context(nc.sbuf_tensor("cbuf", [128, 16], F32))
        self.cvals = {}
        self.s_cst = p.sem("cst")
        self.cb(1e-5)

    def cb(self, val):
        if val not in self.cvals:
            assert len(self.cvals) < 16
            self.cvals[val] = len(self.cvals)
        c = self.cvals[val]
        return self.cbuf[:, c:c + 1]

    def load_consts(self, s_dma):
        p = self.p
        p.add("sync", lambda e: e.dma_start(self.ident[:, :], self.ident_dram.ap()[:, :]),
              inc=[(s_dma, 16)])
        p.add("sync", lambda e: e.dma_start(self.ones1[:, :], self.ones1_dram.ap()[:, :]),
              inc=[(s_dma, 16)])

    def finalize_consts(self):
        p = self.p
        vals = np.zeros((1, 16), np.float32)
        for v, c in self.cvals.items():
            vals[0, c] = v
        cd = p.nc.inline_tensor(np.tile(vals, (128, 1)), name="cbank")
        op = (lambda e: e.dma_start(self.cbuf[:, :], cd.ap()[:, :]), [], [(self.s_cst, 16)])
        self.s_cst.n += 16
        p.streams["gpsimd"].insert(0, op)
        for e in ENGINES:
            if e == "gpsimd" or not p.streams[e]:
                continue
            p.streams[e].insert(0, (None, [(self.s_cst, 16)], []))
        if len(p.streams["gpsimd"]) > 1:
            p.streams["gpsimd"].insert(1, (None, [(self.s_cst, 16)], []))

    def ln_tile(self, i, out_ap_fn, deps=(), slot=None, src_col=None, out_wait=()):
        p = self.p
        if self.sq_scr is None:
            self.sq_scr = self.p.stack.enter_context(
                self.p.nc.sbuf_tensor("sq_scr", [128, D], F32))
        c0 = (i * D) if src_col is None else src_col
        xt = lambda: self.x_sb[:, c0:c0 + D]
        st = self.stat
        p.add("scalar",
              lambda e: e.activation(self.sq_scr[:, :], xt(), AF.Square,
                                     accum_out=st[:, 1:2]),
              wait=list(deps) + [w(self.s_var)],
              inc=[(self.s_ssq, 1)])
        p.add("vector",
              lambda e: e.reduce_sum(st[:, 0:1], xt(), axis=AX.X),
              wait=list(deps), selfsync=True)
        p.add("vector", lambda e: e.tensor_scalar(st[:, 2:3], st[:, 0:1], 1.0 / D, None, op0=ALU.mult),
              selfsync=True)
        p.add("vector", lambda e: e.tensor_mul(st[:, 5:6], st[:, 2:3], st[:, 2:3]),
              selfsync=True)
        p.add("vector",
              lambda e: e.tensor_scalar(st[:, 6:7], st[:, 1:2], 1.0 / D, None, op0=ALU.mult),
              wait=[w(self.s_ssq)], selfsync=True)
        p.add("vector", lambda e: e.tensor_sub(st[:, 3:4], st[:, 6:7], st[:, 5:6]),
              wait=[w(self.s_rstd)],
              inc=[(self.s_var, 1)])
        p.add("scalar", lambda e: e.activation(st[:, 7:8], st[:, 3:4], AF.Ln, bias=self.cb(1e-5)),
              wait=[w(self.s_var)])
        p.add("scalar", lambda e: e.activation(st[:, 4:5], st[:, 7:8], AF.Exp, scale=-0.5),
              inc=[(self.s_rstd, 1)])
        p.add("vector",
              lambda e: e.tensor_scalar(out_ap_fn(), xt(), st[:, 2:3], st[:, 4:5],
                                        op0=ALU.subtract, op1=ALU.mult),
              wait=[w(self.s_rstd), (self.s_tr, max(0, self.s_tr.n - 1))] + list(out_wait),
              inc=[(self.s_lnbf, 1)])

    def transpose_tile(self, src_ap_fn, dst_ap_fn, wait_src, evac_wait=()):
        p = self.p

        def tr(e, j):
            return e.transpose(self.tr_ps[:, j * 128:(j + 1) * 128],
                               src_ap_fn()[:, j * 128:(j + 1) * 128],
                               self.ident[:, :])

        for j in range(8):
            p.add("tensor", lambda e, j=j: tr(e, j),
                  wait=(list(wait_src) + [w(self.s_lnT)]) if j == 0 else (),
                  inc=[(self.s_tr, 1)] if j == 7 else ())
        p.add("vector", lambda e: e.tensor_copy(dst_ap_fn(), self.tr_ps[:, :]),
              wait=[w(self.s_tr)] + list(evac_wait), inc=[(self.s_lnT, 1)])



def build_L1_attn(debug=False):
    nc = bass.Bass("TRN2", target_bir_lowering=False)
    x_in = nc.dram_tensor("x_in", [T, D], F32, kind="ExternalInput")
    x_out = nc.dram_tensor("x_out", [T, D], F32, kind="ExternalOutput")
    wq = nc.dram_tensor("wqT", [D, D], F32, kind="ExternalInput")
    wk = nc.dram_tensor("wkT", [D, D], F32, kind="ExternalInput")
    wv = nc.dram_tensor("wvT", [D, D], F32, kind="ExternalInput")
    wo = nc.dram_tensor("woT", [D, D], F32, kind="ExternalInput")

    H = 16
    DH = 64
    VW = DH + 1

    mask_diag = np.where(np.arange(128)[:, None] <= np.arange(128)[None, :], 0.0, NEG).astype(np.float32)
    mask_prev = np.where(np.arange(128)[:, None] >= np.arange(128)[None, :], 0.0, NEG).astype(np.float32)

    if debug:
        dbg_ln = nc.dram_tensor("dbg_ln", [128, D], F32, kind="ExternalOutput")
        dbg_qT = nc.dram_tensor("dbg_qT", [128, 8 * 512], F32, kind="ExternalOutput")
        dbg_kT = nc.dram_tensor("dbg_kT", [128, 8 * 512], F32, kind="ExternalOutput")
        dbg_v = nc.dram_tensor("dbg_v", [128, 4 * 16 * 65], F32, kind="ExternalOutput")
        dbg_at = nc.dram_tensor("dbg_at", [128, D], F32, kind="ExternalOutput")
    with contextlib.ExitStack() as stack:
        p = Prog(nc, stack)
        cm = Common(p)
        sb = lambda name, shape, dt=BF16: stack.enter_context(nc.sbuf_tensor(name, shape, dt))
        ps = lambda name, shape, dt=F32: stack.enter_context(nc.psum_tensor(name, shape, dt))
        dbg32 = sb("dbg32", [128, 8 * 512], F32) if debug else None

        wq_sb = sb("wq_sb", [128, 8 * D])
        wk_sb = sb("wk_sb", [128, 8 * D])
        wv_sb = sb("wv_sb", [128, 8 * D])
        wo_sb = sb("wo_sb", [128, 8 * D])
        lnT4 = sb("lnT4", [128, 4096])
        qT4 = sb("qT4", [128, 8 * 512])
        kT4 = sb("kT4", [128, 8 * 512])
        kTprev = sb("kTprev", [128, 8 * 128])
        v4 = sb("v4", [128, 4 * H * VW])
        vprev = sb("vprev", [128, H * VW])
        attn = sb("attn", [128, D])
        attnT_sb = sb("attnT_sb", [128, D])
        attnT_ps = ps("attnT_ps", [128, 1024], BF16)
        mdiag = sb("mdiag", [128, 128], F32)
        mprev = sb("mprev", [128, 128], F32)
        pt2 = sb("pt2", [128, 2 * 128])

        qkv_ps = [ps("qkv_ps0", [128, 512]), ps("qkv_ps1", [128, 512])]
        sc_ps = ps("sc_ps", [128, 128])
        o_ps = ps("o_ps", [128, 128])
        pr_ps = ps("pr_ps", [128, 1024])

        md = nc.inline_tensor(mask_diag, name="mdiagc")
        mp = nc.inline_tensor(mask_prev, name="mprevc")

        s_dma = p.sem("dma")
        s_wgt = p.sem("wgt")
        s_qkvm = p.sem("qkvm")
        s_qkve = p.sem("qkve")
        s_prev = p.sem("prev")
        s_scm = p.sem("scm")
        s_msk = p.sem("msk")
        s_exp = p.sem("exp")
        s_om = p.sem("om")
        s_oev = p.sem("oev")
        s_atr = p.sem("atr")
        s_ate = p.sem("ate")
        s_prm = p.sem("prm")
        s_res = p.sem("res")
        s_out = p.sem("out")

        cm.load_consts(s_dma)
        p.add("sync", lambda e: e.dma_start(mdiag[:, :], md.ap()[:, :]), inc=[(s_dma, 16)])
        p.add("sync", lambda e: e.dma_start(mprev[:, :], mp.ap()[:, :]), inc=[(s_dma, 16)])
        for _ge in ("tensor", "vector", "scalar"):
            p.add(_ge, None, wait=[w(s_dma)])
        xr = x_in.ap().rearrange("(i p) d -> p i d", p=128)
        xwr = x_out.ap().rearrange("(i p) d -> p i d", p=128)
        s_xc = p.sem("xc")
        s_xo = p.sem("xo")
        s_dbg = p.sem("dbg")
        for wdr, wsb in ((wq, wq_sb), (wk, wk_sb), (wv, wv_sb), (wo, wo_sb)):
            wsrc = wdr.ap().rearrange("(j p) c -> p j c", p=128)
            p.add("gpsimd", lambda e, a=wsb, b=wsrc: e.dma_start(
                      a[:, :].rearrange("p (j c) -> p j c", j=8), b),
                  inc=[(s_wgt, 16)])
        for slot in range(4):
            for h in range(H):
                col = slot * H * VW + h * VW + DH
                p.add("vector", lambda e, c=col: e.memset(v4[:, c:c + 1], 1.0))
        for h in range(H):
            col = h * VW + DH
            p.add("vector", lambda e, c=col: e.memset(vprev[:, c:c + 1], 1.0))

        wdep = [(s_wgt, s_wgt.n)]

        for c4 in range(8):
            tiles = [c4 * 4 + k for k in range(4)]
            p.add("sync",
                  lambda e, c4=c4: e.dma_start(
                      cm.x_sb[:, :].rearrange("p (k d) -> p k d", k=4),
                      xr[:, c4 * 4:(c4 + 1) * 4, :]),
                  wait=[w(s_res), w(s_xo)], inc=[(s_xc, 16)])
            xdep = [w(s_xc)]
            for k, i in enumerate(tiles):
                slot = i % 2
                cm.ln_tile(i, (lambda s=slot: cm.lnbf[:, s * D:(s + 1) * D]), deps=xdep,
                           src_col=k * D)
                if debug and i == 4:
                    p.add("gpsimd", lambda e, s=slot: e.dma_start(dbg_ln.ap()[:, :], cm.lnbf[:, s * D:(s + 1) * D]),
                          wait=[w(cm.s_lnbf)], inc=[(s_dbg, 16)])
                dst = lambda k=k: lnT4[:, :].rearrange("p (j t) -> p j t", j=8)[:, :, k * 128:(k + 1) * 128]
                cm.transpose_tile((lambda s=slot: cm.lnbf[:, s * D:(s + 1) * D]),
                                  dst, wait_src=[w(cm.s_lnbf)],
                                  evac_wait=[w(s_qkvm)])
            lnT_ready = [w(cm.s_lnT)]

            for which, wsb, dstbuf, scale in (("q", wq_sb, qT4, 0.125), ("k", wk_sb, kT4, None)):
                for ct in range(8):
                    bank = qkv_ps[ct % 2]
                    def mm(e, j, ct=ct, wsb=wsb, bank=bank):
                        return e.matmul(bank[:, :],
                                        wsb[:, j * D + ct * 128: j * D + ct * 128 + 128],
                                        lnT4[:, j * 512:(j + 1) * 512],
                                        start=(j == 0), stop=(j == 7))
                    for j in range(8):
                        p.add("tensor", lambda e, j=j, f=mm: f(e, j),
                              wait=(lnT_ready + wdep + [(s_qkve, max(0, s_qkvm.n - 1))]) if j == 0 else (),
                              inc=[(s_qkvm, 1)] if j == 7 else ())
                    if scale is not None:
                        p.add("scalar",
                              lambda e, ct=ct, bank=bank, dstbuf=dstbuf, sc2=scale: e.activation(
                                  dstbuf[:, ct * 512:(ct + 1) * 512], bank[:, :], AF.Copy, scale=sc2),
                              wait=[w(s_qkvm), w(s_scm)], inc=[(s_qkve, 1)])
                    else:
                        p.add("vector",
                              lambda e, ct=ct, bank=bank, dstbuf=dstbuf: e.tensor_copy(
                                  dstbuf[:, ct * 512:(ct + 1) * 512], bank[:, :]),
                              wait=[w(s_qkvm), w(s_scm)], inc=[(s_qkve, 1)])
            for k in range(4):
                for half in range(2):
                    bank = qkv_ps[half]
                    def mmv(e, j, k=k, half=half, bank=bank):
                        return e.matmul(bank[:, :],
                                        lnT4[:, j * 512 + k * 128: j * 512 + k * 128 + 128],
                                        wv_sb[:, j * D + half * 512: j * D + (half + 1) * 512],
                                        start=(j == 0), stop=(j == 7))
                    for j in range(8):
                        p.add("tensor", lambda e, j=j, f=mmv: f(e, j),
                              wait=([(s_qkve, max(0, s_qkvm.n - 1))]) if j == 0 else (),
                              inc=[(s_qkvm, 1)] if j == 7 else ())
                    def ev(e, k=k, half=half, bank=bank):
                        dst = v4[:, :].rearrange("p (s h c) -> p s h c", s=4, h=H)[
                            :, k, half * 8:(half + 1) * 8, 0:DH]
                        return e.tensor_copy(dst, bank[:, :].rearrange("p (h c) -> p h c", h=8))
                    p.add("vector", lambda e, f=ev: f(e),
                          wait=[w(s_qkvm), w(s_om)], inc=[(s_qkve, 1)])

            if debug and c4 == 1:
                p.add("gpsimd", lambda e: e.dma_start(dbg_qT.ap()[:, :], qT4[:, :]), wait=[w(s_qkve)], inc=[(s_dbg, 16)])
                p.add("gpsimd", lambda e: e.dma_start(dbg_kT.ap()[:, :], kT4[:, :]), wait=[w(s_qkve)], inc=[(s_dbg, 16)])
                p.add("gpsimd", lambda e: e.dma_start(dbg_v.ap()[:, :], v4[:, :]), wait=[w(s_qkve)], inc=[(s_dbg, 16)])
            for k, i in enumerate(tiles):
                for h in range(H):
                    ct, prow = h // 2, (h % 2) * 64
                    jts = ([-1] if i > 0 else []) + [0]
                    nj = len(jts)
                    for idx, jt in enumerate(jts):
                        if jt == 0:
                            kt_ap = lambda ct=ct, prow=prow, k=k: kT4[prow:prow + 64, ct * 512 + k * 128: ct * 512 + k * 128 + 128]
                        elif k > 0:
                            kt_ap = lambda ct=ct, prow=prow, k=k: kT4[prow:prow + 64, ct * 512 + (k - 1) * 128: ct * 512 + (k - 1) * 128 + 128]
                        else:
                            kt_ap = lambda ct=ct, prow=prow: kTprev[prow:prow + 64, ct * 128:(ct + 1) * 128]
                        qt_ap = lambda ct=ct, prow=prow, k=k: qT4[prow:prow + 64, ct * 512 + k * 128: ct * 512 + k * 128 + 128]
                        p.add("tensor",
                              lambda e, ka=kt_ap, qa=qt_ap: e.matmul(sc_ps[:, :], ka(), qa(),
                                                                     start=True, stop=True),
                              wait=[w(s_exp), w(s_qkve), w(s_prev)],
                              inc=[(s_scm, 1)])
                        msk = mprev if jt == -1 else mdiag
                        p.add("vector",
                              lambda e, m=msk: e.tensor_add(sc_ps[:, :], sc_ps[:, :], m[:, :]),
                              wait=[w(s_scm)], inc=[(s_msk, 1)])
                        p.add("scalar",
                              lambda e, idx=idx: e.activation(pt2[:, idx * 128:(idx + 1) * 128],
                                                              sc_ps[:, :], AF.Exp, bias=cm.cb(-20.0)),
                              wait=[w(s_msk), w(s_om)], inc=[(s_exp, 1)])
                        if jt == 0:
                            v_ap = lambda k=k, h=h: v4[:, k * H * VW + h * VW: k * H * VW + (h + 1) * VW]
                        elif k > 0:
                            v_ap = lambda k=k, h=h: v4[:, (k - 1) * H * VW + h * VW: (k - 1) * H * VW + (h + 1) * VW]
                        else:
                            v_ap = lambda h=h: vprev[:, h * VW:(h + 1) * VW]
                        p.add("tensor",
                              lambda e, idx=idx, va=v_ap, st=(idx == 0), sp=(idx == nj - 1):
                                  e.matmul(o_ps[:, 0:VW], pt2[:, idx * 128:(idx + 1) * 128],
                                           va(), start=st, stop=sp),
                              wait=[w(s_exp)] + ([w(s_oev)] if idx == 0 else []),
                              inc=[(s_om, 1)])
                    p.add("vector", lambda e: e.reciprocal(cm.stat[:, 8:9], o_ps[:, DH:DH + 1]),
                          wait=[w(s_om), w(s_atr)], selfsync=True)
                    p.add("vector",
                          lambda e, h=h: e.tensor_scalar(attn[:, h * DH:(h + 1) * DH],
                                                         o_ps[:, 0:DH], cm.stat[:, 8:9], None,
                                                         op0=ALU.mult),
                          inc=[(s_oev, 1)])
                if debug and i == 4:
                    p.add("gpsimd", lambda e: e.dma_start(dbg_at.ap()[:, :], attn[:, :]), wait=[w(s_oev)], inc=[(s_dbg, 16)])
                for j in range(8):
                    p.add("tensor",
                          lambda e, j=j: e.transpose(attnT_ps[:, j * 128:(j + 1) * 128],
                                                     attn[:, j * 128:(j + 1) * 128], cm.ident[:, :]),
                          wait=([w(s_oev), w(s_ate)]) if j == 0 else (),
                          inc=[(s_atr, 1)] if j == 7 else ())
                p.add("vector", lambda e: e.tensor_copy(attnT_sb[:, :], attnT_ps[:, :]),
                      wait=[w(s_atr), w(s_prm)], inc=[(s_ate, 1)])
                def mmp(e, j, half):
                    return e.matmul(pr_ps[:, half * 512:(half + 1) * 512],
                                    attnT_sb[:, j * 128:(j + 1) * 128],
                                    wo_sb[:, j * D + half * 512: j * D + (half + 1) * 512],
                                    start=(j == 0), stop=(j == 7))
                for half in range(2):
                    for j in range(8):
                        p.add("tensor", lambda e, j=j, h2=half: mmp(e, j, h2),
                              wait=([w(s_ate), w(s_res)] + wdep) if (j == 0 and half == 0) else (),
                              inc=[(s_prm, 1)] if j == 7 else ())
                p.add("vector",
                      lambda e, k=k: e.tensor_add(cm.x_sb[:, k * D:(k + 1) * D],
                                                  cm.x_sb[:, k * D:(k + 1) * D], pr_ps[:, :]),
                      wait=[w(s_prm)], inc=[(s_res, 1)])
                p.add("sync",
                      lambda e, i=i, k=k: e.dma_start(xwr[:, i, :],
                                                      cm.x_sb[:, k * D:(k + 1) * D]),
                      wait=[w(s_res)], inc=[(s_xo, 16)])

            def savek(e):
                src = kT4[:, :].rearrange("p (ct t) -> p ct t", ct=8)[:, :, 3 * 128:4 * 128]
                return e.tensor_copy(kTprev[:, :].rearrange("p (ct t) -> p ct t", ct=8), src)
            p.add("vector", lambda e: savek(e), wait=[w(s_qkve), w(s_scm)], inc=[(s_prev, 1)])
            p.add("vector",
                  lambda e: e.tensor_copy(vprev[:, :], v4[:, 3 * H * VW: 4 * H * VW]),
                  wait=[w(s_qkve), w(s_om)], inc=[(s_prev, 1)])

        p.add("sync", lambda e: e.wait_ge(s_xo.h, s_xo.n))
        if s_dbg.n:
            p.add("sync", lambda e: e.wait_ge(s_dbg.h, s_dbg.n))

        cm.finalize_consts()
        p.emit()
    return nc


def l1_attn_inmap(inputs, b, x):
    return {
        "x_in": np.ascontiguousarray(x[b]),
        "wqT": hT(inputs["L1_attn_Wq"]),
        "wkT": hT(inputs["L1_attn_Wk"]),
        "wvT": hT(inputs["L1_attn_Wv"]),
        "woT": hT(inputs["L1_attn_Wo"]),
    }





def build_L0_ffn():
    nc = bass.Bass("TRN2", target_bir_lowering=False)
    x_in = nc.dram_tensor("x_in", [T, D], F32, kind="ExternalInput")
    x_out = nc.dram_tensor("x_out", [T, D], F32, kind="ExternalOutput")
    gwT = nc.dram_tensor("gwT", [D, 16], F32, kind="ExternalInput")
    w1 = nc.dram_tensor("w1", [16, D, 512], F32, kind="ExternalInput")
    w3 = nc.dram_tensor("w3", [16, D, 512], F32, kind="ExternalInput")
    w2 = nc.dram_tensor("w2", [16, 512, D], F32, kind="ExternalInput")
    ws1 = nc.dram_tensor("ws1", [D, D], F32, kind="ExternalInput")
    ws3 = nc.dram_tensor("ws3", [D, D], F32, kind="ExternalInput")
    ws2 = nc.dram_tensor("ws2", [D, D], F32, kind="ExternalInput")
    lnT_scr = nc.dram_tensor("lnT_scr", [8, 128, 4096], BF16, kind="Internal")

    GROUPS = [[0, 1, 2, 3, 4], [5, 6, 7, 8, 9], [10, 11, 12, 13, 14], [15, "s0", "s1"]]
    NSLOT = 5

    with contextlib.ExitStack() as stack:
        p = Prog(nc, stack)
        cm = Common(p)
        sb = lambda name, shape, dt=BF16: stack.enter_context(nc.sbuf_tensor(name, shape, dt))
        ps = lambda name, shape, dt=F32: stack.enter_context(nc.psum_tensor(name, shape, dt))

        w1g = sb("w1g", [128, NSLOT * 4096])
        w3g = sb("w3g", [128, NSLOT * 4096])
        w2g = sb("w2g", [128, NSLOT * 4096])
        lnT4 = sb("lnT4", [128, 4096])
        gsb = sb("gsb", [128, 8 * 16])
        s1T = sb("s1T", [128, 2048])
        h3T = sb("h3T", [128, 2048])
        aT = sb("aT", [128, 2048])
        y_sb = sb("y_sb", [128, 4096], F32)
        c_all = sb("c_all", [128, NT * 16], F32)
        rb = sb("rb", [128, 128], F32)
        gsb32 = sb("gsb32", [128, 8 * 16], F32)
        glo = sb("glo", [128, 8 * 16])
        lnf_sc = sb("lnf_sc", [128, D], F32)
        lo_bf = sb("lo_bf", [128, D])
        loT4 = sb("loT4", [128, 4096])

        rt_ps = ps("rt_ps", [128, 16])
        hps = [ps("hps0", [128, 512]), ps("hps1", [128, 512])]
        y_ps = ps("y_ps", [128, 1024])

        s_dma = p.sem("dma")
        s_wg = p.sem("wg")
        s_ltio = p.sem("ltio")
        s_rtm = p.sem("rtm")
        s_rte = p.sem("rte")
        s_c = p.sem("c")
        s_hm = p.sem("hm")
        s_he = p.sem("he")
        s_aT = p.sem("aT")
        s_ym = p.sem("ym")
        s_ya = p.sem("ya")
        s_xc = p.sem("xc")
        s_xo = p.sem("xo")
        s_xadd = p.sem("xadd")
        s_glo = p.sem("glo")
        s_lo = p.sem("lo")
        s_lo2 = p.sem("lo2")

        cm.load_consts(s_dma)
        for _ge in ("tensor", "vector", "scalar"):
            p.add(_ge, None, wait=[w(s_dma)])
        gsrc = gwT.ap().rearrange("(j p) e -> p j e", p=128)
        p.add("gpsimd", lambda e: e.dma_start(gsb[:, :].rearrange("p (j e) -> p j e", j=8), gsrc),
              inc=[(s_wg, 16)])
        p.add("sync", lambda e: e.dma_start(gsb32[:, :].rearrange("p (j e) -> p j e", j=8), gsrc),
              inc=[(s_glo, 16)])
        p.add("vector", lambda e: e.tensor_sub(glo[:, :], gsb32[:, :], gsb[:, :]),
              wait=[w(s_wg), w(s_glo)], inc=[(s_glo, 1)])

        xr = x_in.ap().rearrange("(i p) d -> p i d", p=128)
        xwr = x_out.ap().rearrange("(i p) d -> p i d", p=128)

        def wslot_srcs(e):
            if e == "s0":
                return (ws1.ap()[:, 0:512], ws3.ap()[:, 0:512], ws2.ap()[0:512, :])
            if e == "s1":
                return (ws1.ap()[:, 512:1024], ws3.ap()[:, 512:1024], ws2.ap()[512:1024, :])
            return (w1.ap()[e], w3.ap()[e], w2.ap()[e])

        for g, grp in enumerate(GROUPS):
            for slot, e in enumerate(grp):
                a1, a3, a2 = wslot_srcs(e)
                war = [w(s_hm), w(s_ym)]
                p.add("gpsimd", lambda e_, a=a1, s=slot: e_.dma_start(
                    w1g[:, s * 4096:(s + 1) * 4096].rearrange("p (j f) -> p j f", j=8),
                    a.rearrange("(j p) f -> p j f", p=128)), wait=war, inc=[(s_wg, 16)])
                p.add("gpsimd", lambda e_, a=a3, s=slot: e_.dma_start(
                    w3g[:, s * 4096:(s + 1) * 4096].rearrange("p (j f) -> p j f", j=8),
                    a.rearrange("(j p) f -> p j f", p=128)), wait=war, inc=[(s_wg, 16)])
                p.add("gpsimd", lambda e_, a=a2, s=slot: e_.dma_start(
                    w2g[:, s * 4096:(s + 1) * 4096].rearrange("p (ft d) -> p ft d", ft=4),
                    a.rearrange("(ft p) d -> p ft d", p=128)), wait=war, inc=[(s_wg, 16)])
            wdep = [w(s_wg)]

            for c4 in range(8):
                tiles = [c4 * 4 + k for k in range(4)]
                xsrc = xr if g == 0 else xwr
                p.add("sync",
                      lambda e, c4=c4, xsrc=xsrc: e.dma_start(
                          cm.x_sb[:, :].rearrange("p (k d) -> p k d", k=4),
                          xsrc[:, c4 * 4:(c4 + 1) * 4, :]),
                      wait=[w(s_xadd), w(s_xo), w(s_lo)], inc=[(s_xc, 16)])
                xdep = [w(s_xc)]

                if g == 0:
                    for k, i in enumerate(tiles):
                        slot = i % 2
                        cm.ln_tile(i, (lambda s=slot: cm.lnbf[:, s * D:(s + 1) * D]),
                                   deps=xdep, src_col=k * D)
                        dst = lambda k=k: lnT4[:, :].rearrange("p (j t) -> p j t", j=8)[:, :, k * 128:(k + 1) * 128]
                        cm.transpose_tile((lambda s=slot: cm.lnbf[:, s * D:(s + 1) * D]),
                                          dst, wait_src=[w(cm.s_lnbf)],
                                          evac_wait=[w(s_hm), w(s_rtm), w(s_ltio)])
                        st = cm.stat
                        p.add("vector", lambda e, k=k: e.tensor_scalar(
                            lnf_sc[:, :], cm.x_sb[:, k * D:(k + 1) * D], st[:, 2:3], st[:, 4:5],
                            op0=ALU.subtract, op1=ALU.mult),
                            wait=[w(cm.s_rstd)], inc=[(s_lo, 1)])
                        p.add("vector", lambda e, s=slot: e.tensor_sub(
                            lo_bf[:, :], lnf_sc[:, :], cm.lnbf[:, s * D:(s + 1) * D]),
                            wait=[w(s_lo)], inc=[(s_lo2, 1)])
                        dstlo = lambda k=k: loT4[:, :].rearrange("p (j t) -> p j t", j=8)[:, :, k * 128:(k + 1) * 128]
                        cm.transpose_tile((lambda: lo_bf[:, :]), dstlo,
                                          wait_src=[w(s_lo2)],
                                          evac_wait=[w(s_rtm)])
                    p.add("sync", lambda e, c4=c4: e.dma_start(lnT_scr.ap()[c4], lnT4[:, :]),
                          wait=[w(cm.s_lnT)], inc=[(s_ltio, 16)])
                    for k, i in enumerate(tiles):
                        for pi, (lsrc, rsrc) in enumerate(((lnT4, gsb), (lnT4, glo), (loT4, gsb))):
                            for j in range(8):
                                p.add("tensor",
                                      lambda e, j=j, k=k, pi=pi, lsrc=lsrc, rsrc=rsrc: e.matmul(
                                          rt_ps[:, :], lsrc[:, j * 512 + k * 128: j * 512 + k * 128 + 128],
                                          rsrc[:, j * 16:(j + 1) * 16],
                                          start=(pi == 0 and j == 0), stop=(pi == 2 and j == 7)),
                                      wait=([w(cm.s_lnT), w(s_wg), w(s_rte), w(s_glo)]) if (pi == 0 and j == 0) else (),
                                      inc=[(s_rtm, 1)] if (pi == 2 and j == 7) else ())
                        p.add("vector", lambda e: e.reduce_max(rb[:, 0:1], rt_ps[:, :], axis=AX.X),
                              wait=[w(s_rtm)], selfsync=True)
                        p.add("vector", lambda e: e.tensor_scalar(rb[:, 1:2], rb[:, 0:1], -1.0, None, op0=ALU.mult),
                              selfsync=True)
                        p.add("scalar",
                              lambda e: e.activation(rb[:, 16:32], rt_ps[:, :], AF.Exp,
                                                     bias=rb[:, 1:2], accum_out=rb[:, 2:3]),
                              wait=[p.selfwait("vector")], inc=[(s_rte, 1)])
                        p.add("vector", lambda e: e.reciprocal(rb[:, 3:4], rb[:, 2:3]),
                              wait=[w(s_rte)], selfsync=True)
                        p.add("vector", lambda e: e.tensor_scalar(rb[:, 32:48], rb[:, 16:32], rb[:, 3:4], None, op0=ALU.mult),
                              selfsync=True)
                        p.add("vector", lambda e: e.reduce_max(
                            rb[:, 48:52].rearrange("p (g o) -> p g o", g=4),
                            rb[:, 32:48].rearrange("p (g f) -> p g f", g=4), axis=AX.X), selfsync=True)
                        p.add("vector", lambda e: e.reduce_max(rb[:, 52:53], rb[:, 48:52], axis=AX.X), selfsync=True)
                        p.add("vector", lambda e: e.tensor_scalar(rb[:, 54:58], rb[:, 48:52], rb[:, 52:53], None, op0=ALU.is_ge), selfsync=True)
                        p.add("vector", lambda e: e.scalar_tensor_tensor(rb[:, 58:62], rb[:, 54:58], -10.0, rb[:, 48:52], op0=ALU.mult, op1=ALU.add), selfsync=True)
                        p.add("vector", lambda e: e.reduce_max(rb[:, 53:54], rb[:, 58:62], axis=AX.X), selfsync=True)
                        p.add("vector", lambda e: e.tensor_scalar(rb[:, 54:58], rb[:, 48:52], rb[:, 53:54], None, op0=ALU.is_ge), selfsync=True)
                        for jj in range(4):
                            p.add("vector", lambda e, jj=jj: e.tensor_copy(
                                rb[:, 60:76].rearrange("p (g f) -> p g f", g=4)[:, :, jj:jj + 1],
                                rb[:, 54:58].rearrange("p (g o) -> p g o", g=4)),
                                selfsync=(jj == 3))
                        p.add("vector", lambda e: e.tensor_mul(rb[:, 76:92], rb[:, 32:48], rb[:, 60:76]), selfsync=True)
                        p.add("vector", lambda e: e.tensor_copy(rb[:, 92:108], rb[:, 76:92]), selfsync=True)
                        for r in range(3):
                            p.add("vector", lambda e: e.reduce_max(rb[:, 108:109], rb[:, 92:108], axis=AX.X), selfsync=True)
                            p.add("vector", lambda e: e.tensor_scalar(rb[:, 109:125], rb[:, 92:108], rb[:, 108:109], None, op0=ALU.is_ge), selfsync=True)
                            p.add("vector", lambda e: e.scalar_tensor_tensor(rb[:, 92:108], rb[:, 109:125], -10.0, rb[:, 92:108], op0=ALU.mult, op1=ALU.add), selfsync=True)
                        p.add("vector", lambda e: e.reduce_max(rb[:, 108:109], rb[:, 92:108], axis=AX.X), selfsync=True)
                        p.add("vector", lambda e: e.tensor_scalar(rb[:, 109:125], rb[:, 76:92], rb[:, 108:109], None, op0=ALU.is_ge), selfsync=True)
                        p.add("vector", lambda e, i=i: e.tensor_mul(c_all[:, i * 16:(i + 1) * 16], rb[:, 32:48], rb[:, 109:125]),
                              selfsync=True)
                else:
                    p.add("sync", lambda e, c4=c4: e.dma_start(lnT4[:, :], lnT_scr.ap()[c4]),
                          wait=[w(s_hm)], inc=[(s_ltio, 16)])
                ltdep = [w(s_ltio)] if g > 0 else [w(cm.s_lnT)]

                for slot, e_id in enumerate(grp):
                    shared = isinstance(e_id, str)
                    for which, wg_, dst_, act in ((0, w1g, s1T, True), (1, w3g, h3T, False)):
                        for ft in range(4):
                            bank = hps[ft % 2]
                            for j in range(8):
                                p.add("tensor",
                                      lambda e, j=j, ft=ft, slot=slot, wg_=wg_, bank=bank: e.matmul(
                                          bank[:, :],
                                          wg_[:, slot * 4096 + j * 512 + ft * 128: slot * 4096 + j * 512 + ft * 128 + 128],
                                          lnT4[:, j * 512:(j + 1) * 512],
                                          start=(j == 0), stop=(j == 7)),
                                      wait=(ltdep + wdep + [(s_he, max(0, s_hm.n - 1))]) if j == 0 else (),
                                      inc=[(s_hm, 1)] if j == 7 else ())
                            if act:
                                p.add("scalar",
                                      lambda e, ft=ft, bank=bank, dst_=dst_: e.activation(
                                          dst_[:, ft * 512:(ft + 1) * 512], bank[:, :], AF.Silu),
                                      wait=[w(s_hm), w(s_aT)], inc=[(s_he, 1)])
                            else:
                                p.add("vector",
                                      lambda e, ft=ft, bank=bank, dst_=dst_: e.tensor_copy(
                                          dst_[:, ft * 512:(ft + 1) * 512], bank[:, :]),
                                      wait=[w(s_hm), w(s_aT)], inc=[(s_he, 1)])
                    p.add("vector", lambda e: e.tensor_mul(aT[:, :], s1T[:, :], h3T[:, :]),
                          wait=[w(s_he), w(s_ym)], inc=[(s_aT, 1)])
                    for tt in range(4):
                        for ft in range(4):
                            for dh in range(2):
                                p.add("tensor",
                                      lambda e, tt=tt, ft=ft, dh=dh, slot=slot: e.matmul(
                                          y_ps[:, dh * 512:(dh + 1) * 512],
                                          aT[:, ft * 512 + tt * 128: ft * 512 + tt * 128 + 128],
                                          w2g[:, slot * 4096 + ft * 1024 + dh * 512: slot * 4096 + ft * 1024 + (dh + 1) * 512],
                                          start=(ft == 0), stop=(ft == 3)),
                                      wait=([w(s_aT), w(s_ya)]) if (ft == 0 and dh == 0) else (),
                                      inc=[(s_ym, 1)] if (ft == 3 and dh == 1) else ())
                        first = (slot == 0)
                        if shared:
                            if first:
                                p.add("vector", lambda e, tt=tt: e.tensor_copy(
                                    y_sb[:, tt * D:(tt + 1) * D], y_ps[:, :]),
                                    wait=[w(s_ym), w(s_xadd)], inc=[(s_ya, 1)])
                            else:
                                p.add("vector", lambda e, tt=tt: e.tensor_add(
                                    y_sb[:, tt * D:(tt + 1) * D], y_sb[:, tt * D:(tt + 1) * D], y_ps[:, :]),
                                    wait=[w(s_ym)], inc=[(s_ya, 1)])
                        else:
                            cap = lambda tt=tt, e_id=e_id, c4=c4: c_all[:, (c4 * 4 + tt) * 16 + e_id: (c4 * 4 + tt) * 16 + e_id + 1]
                            if first:
                                p.add("vector", lambda e, tt=tt, cap=cap: e.tensor_scalar(
                                    y_sb[:, tt * D:(tt + 1) * D], y_ps[:, :], cap(), None, op0=ALU.mult),
                                    wait=[w(s_ym), w(s_xadd)], inc=[(s_ya, 1)])
                            else:
                                p.add("vector", lambda e, tt=tt, cap=cap: e.scalar_tensor_tensor(
                                    y_sb[:, tt * D:(tt + 1) * D], y_ps[:, :], cap(),
                                    y_sb[:, tt * D:(tt + 1) * D], op0=ALU.mult, op1=ALU.add),
                                    wait=[w(s_ym)], inc=[(s_ya, 1)])

                p.add("vector", lambda e: e.tensor_add(cm.x_sb[:, :], cm.x_sb[:, :], y_sb[:, :]),
                      wait=[w(s_ya)] + xdep, inc=[(s_xadd, 1)])
                p.add("sync",
                      lambda e, c4=c4: e.dma_start(xwr[:, c4 * 4:(c4 + 1) * 4, :],
                                                   cm.x_sb[:, :].rearrange("p (k d) -> p k d", k=4)),
                      wait=[w(s_xadd)], inc=[(s_xo, 16)])

        p.add("sync", lambda e: e.wait_ge(s_xo.h, s_xo.n))
        cm.finalize_consts()
        p.emit()
    return nc


def l0_ffn_inmap(inputs, b, x):
    return {
        "x_in": np.ascontiguousarray(x[b]),
        "gwT": hT(inputs["L0_ffn_gate_w"]),
        "w1": hC(inputs["L0_ffn_w1"]),
        "w3": hC(inputs["L0_ffn_w3"]),
        "w2": hC(inputs["L0_ffn_w2"]),
        "ws1": hC(inputs["L0_ffn_ws1"]),
        "ws3": hC(inputs["L0_ffn_ws3"]),
        "ws2": hC(inputs["L0_ffn_ws2"]),
    }





def build_L3_ffn():
    nc = bass.Bass("TRN2", target_bir_lowering=False)
    x_in = nc.dram_tensor("x_in", [T, D], F32, kind="ExternalInput")
    x_out = nc.dram_tensor("x_out", [T, D], F32, kind="ExternalOutput")
    gwT = nc.dram_tensor("gwT", [D, 8], F32, kind="ExternalInput")
    w1 = nc.dram_tensor("w1", [8, D, D], F32, kind="ExternalInput")
    w2 = nc.dram_tensor("w2", [8, D, D], F32, kind="ExternalInput")
    w3 = nc.dram_tensor("w3", [8, D, D], F32, kind="ExternalInput")
    w1s = nc.dram_tensor("w1s", [D, D], F32, kind="ExternalInput")
    w2s = nc.dram_tensor("w2s", [D, D], F32, kind="ExternalInput")
    w3s = nc.dram_tensor("w3s", [D, D], F32, kind="ExternalInput")
    lnx_scr = nc.dram_tensor("lnx_scr", [8, 128, 4096], BF16, kind="Internal")

    GROUPS = [[0, 1], [2, 3], [4, 5], [6, 7], ["sh"]]

    with contextlib.ExitStack() as stack:
        p = Prog(nc, stack)
        cm = Common(p)
        sb = lambda name, shape, dt=BF16: stack.enter_context(nc.sbuf_tensor(name, shape, dt))
        ps = lambda name, shape, dt=F32: stack.enter_context(nc.psum_tensor(name, shape, dt))

        w1g = sb("w1g", [128, 2 * 8192])
        w3g = sb("w3g", [128, 2 * 8192])
        w2g = sb("w2g", [128, 2 * 8192])
        lnx4 = sb("lnx4", [128, 4096])
        xeT = sb("xeT", [128, 8 * 512])
        s1T = sb("s1T", [128, 8 * 512])
        h3T = sb("h3T", [128, 8 * 512])
        aT = sb("aT", [128, 8 * 512])
        diag4 = sb("diag4", [128, 4 * 128])
        c_all = sb("c_all", [128, NT * 8], F32)
        rb = sb("rb", [128, 64], F32)

        rt_ps = ps("rt_ps", [128, 8])
        xe_ps = ps("xe_ps", [128, 512])
        hps = [ps("hps0", [128, 512]), ps("hps1", [128, 512])]
        y_ps = ps("y_ps", [128, 1024])

        s_dma = p.sem("dma")
        s_wg = p.sem("wg")
        s_lxio = p.sem("lxio")
        s_rtm = p.sem("rtm")
        s_rte = p.sem("rte")
        s_dg = p.sem("dg")
        s_xem = p.sem("xem")
        s_xee = p.sem("xee")
        s_hm = p.sem("hm")
        s_he = p.sem("he")
        s_aT = p.sem("aT")
        s_ym = p.sem("ym")
        s_ya = p.sem("ya")
        s_xc = p.sem("xc")
        s_xo = p.sem("xo")

        cm.load_consts(s_dma)
        for _ge in ("tensor", "vector", "scalar"):
            p.add(_ge, None, wait=[w(s_dma)])
        gsrc = gwT.ap().rearrange("(j p) e -> p j e", p=128)
        gsb = sb("gsb", [128, 8 * 8])
        p.add("gpsimd", lambda e: e.dma_start(gsb[:, :].rearrange("p (j e) -> p j e", j=8), gsrc),
              inc=[(s_wg, 16)])

        xr = x_in.ap().rearrange("(i p) d -> p i d", p=128)
        xwr = x_out.ap().rearrange("(i p) d -> p i d", p=128)

        def wsrcs(e):
            if e == "sh":
                return (w1s.ap(), w3s.ap(), w2s.ap())
            return (w1.ap()[e], w3.ap()[e], w2.ap()[e])

        for g, grp in enumerate(GROUPS):
            for slot, e_id in enumerate(grp):
                a1, a3, a2 = wsrcs(e_id)
                war = [w(s_hm), w(s_ym)]
                for a, wg_ in ((a1, w1g), (a3, w3g), (a2, w2g)):
                    p.add("gpsimd", lambda e_, a=a, wg_=wg_, s=slot: e_.dma_start(
                        wg_[:, s * 8192:(s + 1) * 8192].rearrange("p (j f) -> p j f", j=8),
                        a.rearrange("(j p) f -> p j f", p=128)), wait=war, inc=[(s_wg, 16)])
            wdep = [w(s_wg)]

            for c4 in range(8):
                tiles = [c4 * 4 + k for k in range(4)]
                xsrc = xr if g == 0 else xwr
                p.add("sync",
                      lambda e, c4=c4, xsrc=xsrc: e.dma_start(
                          cm.x_sb[:, :].rearrange("p (k d) -> p k d", k=4),
                          xsrc[:, c4 * 4:(c4 + 1) * 4, :]),
                      wait=[w(s_ya), w(s_xo), w(s_lxio)], inc=[(s_xc, 16)])
                xdep = [w(s_xc)]

                if g == 0:
                    for k, i in enumerate(tiles):
                        cm.ln_tile(i, (lambda k=k: lnx4[:, k * D:(k + 1) * D]), deps=xdep,
                                   src_col=k * D)
                    p.add("sync", lambda e, c4=c4: e.dma_start(lnx_scr.ap()[c4], lnx4[:, :]),
                          wait=[w(cm.s_lnbf)], inc=[(s_lxio, 16)])
                    for k, i in enumerate(tiles):
                        for j in range(8):
                            p.add("tensor",
                                  lambda e, j=j, k=k: e.transpose(
                                      cm.tr_ps[:, j * 128:(j + 1) * 128],
                                      lnx4[:, k * D + j * 128: k * D + (j + 1) * 128],
                                      cm.ident[:, :]),
                                  wait=([w(cm.s_lnbf), w(cm.s_lnT), w(s_lxio)]) if j == 0 else (),
                                  inc=[(cm.s_tr, 1)] if j == 7 else ())
                        p.add("vector", lambda e: e.tensor_copy(cm.lnbf[:, 0:D], cm.tr_ps[:, :]),
                              wait=[w(cm.s_tr), w(s_rtm)], inc=[(cm.s_lnT, 1)])
                        for j in range(8):
                            p.add("tensor",
                                  lambda e, j=j: e.matmul(
                                      rt_ps[:, :], cm.lnbf[:, j * 128:(j + 1) * 128],
                                      gsb[:, j * 8:(j + 1) * 8], start=(j == 0), stop=(j == 7)),
                                  wait=([w(cm.s_lnT), w(s_wg), w(s_rte)]) if j == 0 else (),
                                  inc=[(s_rtm, 1)] if j == 7 else ())
                        p.add("vector", lambda e: e.reduce_max(rb[:, 0:1], rt_ps[:, :], axis=AX.X),
                              wait=[w(s_rtm)], selfsync=True)
                        p.add("vector", lambda e: e.tensor_scalar(rb[:, 1:2], rb[:, 0:1], -1.0, None, op0=ALU.mult),
                              selfsync=True)
                        p.add("scalar",
                              lambda e: e.activation(rb[:, 16:24], rt_ps[:, :], AF.Exp,
                                                     bias=rb[:, 1:2], accum_out=rb[:, 2:3]),
                              wait=[p.selfwait("vector")], inc=[(s_rte, 1)])
                        p.add("vector", lambda e: e.reciprocal(rb[:, 3:4], rb[:, 2:3]),
                              wait=[w(s_rte)], selfsync=True)
                        p.add("vector", lambda e: e.tensor_scalar(rb[:, 24:32], rb[:, 16:24], rb[:, 3:4], None, op0=ALU.mult),
                              selfsync=True)
                        p.add("vector", lambda e: e.reduce_max(rb[:, 4:5], rb[:, 24:32], axis=AX.X), selfsync=True)
                        p.add("vector", lambda e: e.tensor_scalar(rb[:, 32:40], rb[:, 24:32], rb[:, 4:5], None, op0=ALU.is_ge), selfsync=True)
                        p.add("vector", lambda e: e.scalar_tensor_tensor(rb[:, 40:48], rb[:, 32:40], -10.0, rb[:, 24:32], op0=ALU.mult, op1=ALU.add), selfsync=True)
                        p.add("vector", lambda e: e.reduce_max(rb[:, 5:6], rb[:, 40:48], axis=AX.X), selfsync=True)
                        p.add("vector", lambda e: e.tensor_scalar(rb[:, 32:40], rb[:, 24:32], rb[:, 5:6], None, op0=ALU.is_ge), selfsync=True)
                        p.add("vector", lambda e, i=i: e.tensor_mul(c_all[:, i * 8:(i + 1) * 8], rb[:, 24:32], rb[:, 32:40]),
                              selfsync=True)
                else:
                    p.add("sync", lambda e, c4=c4: e.dma_start(lnx4[:, :], lnx_scr.ap()[c4]),
                          wait=[w(s_xem), w(s_rtm)], inc=[(s_lxio, 16)])
                lxdep = [w(s_lxio)] if g > 0 else [w(cm.s_lnbf)]

                for slot, e_id in enumerate(grp):
                    shared = (e_id == "sh")
                    if not shared:
                        for tt in range(4):
                            cap = lambda tt=tt, e_id=e_id, c4=c4: c_all[:, (c4 * 4 + tt) * 8 + e_id: (c4 * 4 + tt) * 8 + e_id + 1]
                            p.add("vector",
                                  lambda e, tt=tt, cap=cap: e.tensor_scalar(
                                      diag4[:, tt * 128:(tt + 1) * 128], cm.ident[:, :], cap(), None, op0=ALU.mult),
                                  wait=[w(s_xem)] if tt == 0 else (),
                                  inc=[(s_dg, 1)] if tt == 3 else ())
                    for dt in range(8):
                        for tt in range(4):
                            rhs_ap = (lambda: cm.ident[:, :]) if shared else (lambda tt=tt: diag4[:, tt * 128:(tt + 1) * 128])
                            p.add("tensor",
                                  lambda e, dt=dt, tt=tt, r=rhs_ap: e.matmul(
                                      xe_ps[:, tt * 128:(tt + 1) * 128],
                                      lnx4[:, tt * D + dt * 128: tt * D + (dt + 1) * 128],
                                      r(), start=True, stop=True),
                                  wait=(lxdep + [w(s_dg), w(s_xee)]) if (dt == 0 and tt == 0)
                                       else ([w(s_xee)] if tt == 0 else ()),
                                  inc=[(s_xem, 1)] if tt == 3 else ())
                        p.add("vector", lambda e, dt=dt: e.tensor_copy(xeT[:, dt * 512:(dt + 1) * 512], xe_ps[:, :]),
                              wait=[w(s_xem), w(s_hm)], inc=[(s_xee, 1)])
                    for which, wg_, dst_, act in ((0, w1g, s1T, True), (1, w3g, h3T, False)):
                        for ft in range(8):
                            bank = hps[ft % 2]
                            for j in range(8):
                                p.add("tensor",
                                      lambda e, j=j, ft=ft, slot=slot, wg_=wg_, bank=bank: e.matmul(
                                          bank[:, :],
                                          wg_[:, slot * 8192 + j * 1024 + ft * 128: slot * 8192 + j * 1024 + ft * 128 + 128],
                                          xeT[:, j * 512:(j + 1) * 512],
                                          start=(j == 0), stop=(j == 7)),
                                      wait=(wdep + [w(s_xee), (s_he, max(0, s_hm.n - 1))]) if j == 0 else (),
                                      inc=[(s_hm, 1)] if j == 7 else ())
                            if act:
                                p.add("scalar",
                                      lambda e, ft=ft, bank=bank, dst_=dst_: e.activation(
                                          dst_[:, ft * 512:(ft + 1) * 512], bank[:, :], AF.Silu),
                                      wait=[w(s_hm), w(s_aT)], inc=[(s_he, 1)])
                            else:
                                p.add("vector",
                                      lambda e, ft=ft, bank=bank, dst_=dst_: e.tensor_copy(
                                          dst_[:, ft * 512:(ft + 1) * 512], bank[:, :]),
                                      wait=[w(s_hm), w(s_aT)], inc=[(s_he, 1)])
                    p.add("vector", lambda e: e.tensor_mul(aT[:, :], s1T[:, :], h3T[:, :]),
                          wait=[w(s_he), w(s_ym)], inc=[(s_aT, 1)])
                    for tt in range(4):
                        for ft in range(8):
                            for dh in range(2):
                                p.add("tensor",
                                      lambda e, tt=tt, ft=ft, dh=dh, slot=slot: e.matmul(
                                          y_ps[:, dh * 512:(dh + 1) * 512],
                                          aT[:, ft * 512 + tt * 128: ft * 512 + tt * 128 + 128],
                                          w2g[:, slot * 8192 + ft * 1024 + dh * 512: slot * 8192 + ft * 1024 + (dh + 1) * 512],
                                          start=(ft == 0), stop=(ft == 7)),
                                      wait=([w(s_aT), w(s_ya)]) if (ft == 0 and dh == 0) else (),
                                      inc=[(s_ym, 1)] if (ft == 7 and dh == 1) else ())
                        p.add("vector", lambda e, tt=tt: e.tensor_add(
                            cm.x_sb[:, tt * D:(tt + 1) * D], cm.x_sb[:, tt * D:(tt + 1) * D], y_ps[:, :]),
                            wait=[w(s_ym)] + xdep, inc=[(s_ya, 1)])

                p.add("sync",
                      lambda e, c4=c4: e.dma_start(xwr[:, c4 * 4:(c4 + 1) * 4, :],
                                                   cm.x_sb[:, :].rearrange("p (k d) -> p k d", k=4)),
                      wait=[w(s_ya)], inc=[(s_xo, 16)])

        p.add("sync", lambda e: e.wait_ge(s_xo.h, s_xo.n))
        cm.finalize_consts()
        p.emit()
    return nc


def l3_ffn_inmap(inputs, b, x):
    return {
        "x_in": np.ascontiguousarray(x[b]),
        "gwT": hT(inputs["L3_ffn_gate_w"]),
        "w1": hC(inputs["L3_ffn_w1"]),
        "w2": hC(inputs["L3_ffn_w2"]),
        "w3": hC(inputs["L3_ffn_w3"]),
        "w1s": hC(inputs["L3_ffn_w1s"][0]),
        "w2s": hC(inputs["L3_ffn_w2s"][0]),
        "w3s": hC(inputs["L3_ffn_w3s"][0]),
    }





def build_L1_ffn():
    nc = bass.Bass("TRN2", target_bir_lowering=False)
    x_in = nc.dram_tensor("x_in", [T, D], F32, kind="ExternalInput")
    x_out = nc.dram_tensor("x_out", [T, D], F32, kind="ExternalOutput")
    qwp = nc.dram_tensor("qwp", [D, D], F32, kind="ExternalInput")
    kwp = nc.dram_tensor("kwp", [D, D], F32, kind="ExternalInput")
    vw = nc.dram_tensor("vw", [D, 512], F32, kind="ExternalInput")
    qbp = nc.dram_tensor("qbp", [128, 8], F32, kind="ExternalInput")
    kbp = nc.dram_tensor("kbp", [128, 8], F32, kind="ExternalInput")
    vbd = nc.dram_tensor("vbd", [1, 512], F32, kind="ExternalInput")
    owp = nc.dram_tensor("owp", [D, D], F32, kind="ExternalInput")
    obd = nc.dram_tensor("obd", [128, 8], F32, kind="ExternalInput")
    rwd = nc.dram_tensor("rwd", [D, 8], F32, kind="ExternalInput")
    rbd = nc.dram_tensor("rbd", [1, 8], F32, kind="ExternalInput")
    w1 = nc.dram_tensor("w1", [8, D, 2048], F32, kind="ExternalInput")
    b1d = nc.dram_tensor("b1d", [8, 128, 16], F32, kind="ExternalInput")
    w2 = nc.dram_tensor("w2", [8, 2048, D], F32, kind="ExternalInput")
    b2d = nc.dram_tensor("b2d", [8, 1, D], F32, kind="ExternalInput")
    qT_scr = nc.dram_tensor("qT_scr", [128, 8 * 4096], BF16, kind="Internal")
    lnx_scr = nc.dram_tensor("lnx_scr", [8, 128, 4096], BF16, kind="Internal")
    mla_scr = nc.dram_tensor("mla_scr", [128, 8 * 4096], BF16, kind="Internal")

    H = 16
    SC = 1.0 / math.sqrt(32.0)

    with contextlib.ExitStack() as stack:
        p = Prog(nc, stack)
        cm = Common(p)
        sb = lambda name, shape, dt=BF16: stack.enter_context(nc.sbuf_tensor(name, shape, dt))
        ps = lambda name, shape, dt=F32: stack.enter_context(nc.psum_tensor(name, shape, dt))

        bias_sb = sb("bias_sb", [128, 32], F32)
        vb_sb = sb("vb_sb", [1, 512])
        rw_sb = sb("rw_sb", [128, 64])
        rb_sb = sb("rb_sb", [1, 8])
        b2_sb = sb("b2_sb", [1, D])
        b1_sb = sb("b1_sb", [128, 16], F32)
        c_all = sb("c_all", [128, NT * 8], F32)
        rb = sb("rb", [128, 64], F32)
        ar = Arena(nc, stack, 181 * 1024)
        kT = ar.at(0, "kT", [128, 8 * 4096], BF16)
        ow_sb = ar.at(0, "ow_sb", [128, 8 * D], BF16)
        w1g = ar.at(0, "w1g", [128, 8 * 2048], BF16)
        w2g = ar.at(32, "w2g", [128, 16 * 1024], BF16)
        v_all = ar.at(64, "v_all", [128, NT * H * 33], BF16)
        h1T = ar.at(64, "h1T", [128, 16 * 512], BF16)
        mlac = ar.at(80, "mlac", [128, 8 * 512], BF16)
        lnxstage = ar.at(88, "lnxstage", [128, 4096], BF16)
        lnT4 = ar.at(97, "lnT4", [128, 4096], BF16)
        aoT = ar.at(105, "aoT", [128, 8 * 4096], BF16)
        wq_sb = ar.at(105, "wq_sb", [128, 8 * D], BF16)
        wk_sb = ar.at(121, "wk_sb", [128, 8 * D], BF16)
        wv_sb = ar.at(137, "wv_sb", [128, 8 * 512], BF16)
        cm.sq_scr = ar.at(145, "sq_scr", [128, D], F32)
        qstage = ar.at(169, "qstage", [128, 1024], BF16)
        pt = ar.at(171, "pt", [128, 1024], BF16)
        rstg = ar.at(173, "rstg", [2, 1024], F32)
        bcs = ar.at(177, "bcs", [32, 1024], F32)

        sc_ps = ps("sc_ps", [128, 1024])
        oT_ps = ps("oT_ps", [33, 1024])
        y_ps = ps("y_ps", [128, 1024])
        rt_ps = ps("rt_ps", [128, 8])

        s_dma = p.sem("dma")
        s_wgt = p.sem("wgt")
        s_qkvm = p.sem("qkvm")
        s_qkve = p.sem("qkve")
        s_scio = p.sem("scio")
        s_scm = p.sem("scm")
        s_exp = p.sem("exp")
        s_om = p.sem("om")
        s_nrm = p.sem("nrm")
        s_bc = p.sem("bc")
        s_qst = p.sem("qst")
        s_mlm = p.sem("mlm")
        s_mle = p.sem("mle")
        s_rtm = p.sem("rtm")
        s_rte = p.sem("rte")
        s_wg2 = p.sem("wg2")
        s_hm = p.sem("hm")
        s_he = p.sem("he")
        s_ym = p.sem("ym")
        s_ya = p.sem("ya")
        s_xc = p.sem("xc")
        s_xo = p.sem("xo")
        s_lnst = p.sem("lnst")
        s_mcl = p.sem("mcl")

        cm.load_consts(s_dma)
        for _ge in ("tensor", "vector", "scalar"):
            p.add(_ge, None, wait=[w(s_dma)])
        for wdr, wsb_ in ((qwp, wq_sb), (kwp, wk_sb)):
            wsrc = wdr.ap().rearrange("(j p) c -> p j c", p=128)
            p.add("gpsimd", lambda e, a=wsb_, b=wsrc: e.dma_start(
                a[:, :].rearrange("p (j c) -> p j c", j=8), b), inc=[(s_wgt, 16)])
        p.add("gpsimd", lambda e: e.dma_start(
            wv_sb[:, :].rearrange("p (j c) -> p j c", j=8),
            vw.ap().rearrange("(j p) c -> p j c", p=128)), inc=[(s_wgt, 16)])
        p.add("gpsimd", lambda e: e.dma_start(bias_sb[:, 0:8], qbp.ap()[:, :]), inc=[(s_wgt, 16)])
        p.add("gpsimd", lambda e: e.dma_start(bias_sb[:, 8:16], kbp.ap()[:, :]), inc=[(s_wgt, 16)])
        p.add("gpsimd", lambda e: e.dma_start(bias_sb[:, 16:24], obd.ap()[:, :]), inc=[(s_wgt, 16)])
        p.add("gpsimd", lambda e: e.dma_start(vb_sb[:, :], vbd.ap()[:, :]), inc=[(s_wgt, 16)])
        p.add("gpsimd", lambda e: e.dma_start(
            rw_sb[:, :].rearrange("p (j c) -> p j c", j=8),
            rwd.ap().rearrange("(j p) c -> p j c", p=128)), inc=[(s_wgt, 16)])
        p.add("gpsimd", lambda e: e.dma_start(rb_sb[:, :], rbd.ap()[:, :]), inc=[(s_wgt, 16)])
        p.add("vector", lambda e: e.tensor_scalar(bias_sb[:, 0:8], bias_sb[:, 0:8], SC, None, op0=ALU.mult),
              wait=[w(s_wgt)], selfsync=True)
        p.add("vector", lambda e: e.memset(v_all[:, :], 1.0))

        xr = x_in.ap().rearrange("(i p) d -> p i d", p=128)
        xwr = x_out.ap().rearrange("(i p) d -> p i d", p=128)

        for c4 in range(8):
            tiles = [c4 * 4 + k for k in range(4)]
            p.add("sync",
                  lambda e, c4=c4: e.dma_start(
                      cm.x_sb[:, :].rearrange("p (k d) -> p k d", k=4),
                      xr[:, c4 * 4:(c4 + 1) * 4, :]),
                  wait=[w(cm.s_lnbf), w(s_lnst)], inc=[(s_xc, 16)])
            xdep = [w(s_xc)]
            for k, i in enumerate(tiles):
                slot = i % 2
                cm.ln_tile(i, (lambda s=slot: cm.lnbf[:, s * D:(s + 1) * D]), deps=xdep,
                           src_col=k * D, out_wait=[w(s_lnst)])
                p.add("sync", lambda e, i=i, s=slot: e.dma_start(
                    lnx_scr.ap()[i // 4][:, (i % 4) * D:((i % 4) + 1) * D],
                    cm.lnbf[:, s * D:(s + 1) * D]),
                    wait=[w(cm.s_lnbf)], inc=[(s_lnst, 16)])
                dst = lambda k=k: lnT4[:, :].rearrange("p (j t) -> p j t", j=8)[:, :, k * 128:(k + 1) * 128]
                cm.transpose_tile((lambda s=slot: cm.lnbf[:, s * D:(s + 1) * D]),
                                  dst, wait_src=[w(cm.s_lnbf)],
                                  evac_wait=[w(s_qkvm)])
            lnT_ready = [w(cm.s_lnT)]
            for which, wsb_ in (("q", wq_sb), ("k", wk_sb)):
                for ct in range(8):
                    bank = lambda ct=ct: sc_ps[:, (ct % 2) * 512:((ct % 2) + 1) * 512]
                    for j in range(8):
                        p.add("tensor",
                              lambda e, j=j, ct=ct, wsb_=wsb_, bank=bank: e.matmul(
                                  bank(), wsb_[:, j * D + ct * 128: j * D + ct * 128 + 128],
                                  lnT4[:, j * 512:(j + 1) * 512],
                                  start=(j == 0), stop=(j == 7)),
                              wait=(lnT_ready + [w(s_wgt), (s_qkve, max(0, s_qkvm.n - 1))]) if j == 0 else (),
                              inc=[(s_qkvm, 1)] if j == 7 else ())
                    if which == "q":
                        p.add("vector",
                              lambda e, ct=ct, bank=bank: e.tensor_scalar(
                                  pt[:, 0:512], bank(), SC, bias_sb[:, ct:ct + 1],
                                  op0=ALU.mult, op1=ALU.add),
                              wait=[w(s_qkvm), w(s_scio), w(s_exp)], inc=[(s_qkve, 1)])
                        p.add("sync", lambda e, ct=ct, c4=c4: e.dma_start(
                            qT_scr.ap()[:, ct * 4096 + c4 * 512: ct * 4096 + (c4 + 1) * 512],
                            pt[:, 0:512]),
                            wait=[w(s_qkve)], inc=[(s_scio, 16)])
                    else:
                        p.add("vector",
                              lambda e, ct=ct, bank=bank, c4=c4: e.tensor_scalar(
                                  kT[:, ct * 4096 + c4 * 512: ct * 4096 + (c4 + 1) * 512],
                                  bank(), bias_sb[:, 8 + ct:9 + ct], None, op0=ALU.add),
                              wait=[w(s_qkvm)], inc=[(s_qkve, 1)])
            for k in range(4):
                bank = lambda k=k: sc_ps[:, (k % 2) * 512:((k % 2) + 1) * 512]
                for j in range(8):
                    p.add("tensor",
                          lambda e, j=j, k=k, bank=bank: e.matmul(
                              bank(), lnT4[:, j * 512 + k * 128: j * 512 + k * 128 + 128],
                              wv_sb[:, j * 512:(j + 1) * 512],
                              start=(j == 0), stop=False),
                          wait=([(s_qkve, max(0, s_qkvm.n - 1))]) if j == 0 else ())
                p.add("tensor",
                      lambda e, bank=bank: e.matmul(
                          bank(), cm.ones1[0:1, 0:128], vb_sb[:, :],
                          start=False, stop=True),
                      inc=[(s_qkvm, 1)])
                def evv(e, i=c4 * 4 + k, bank=bank):
                    dstv = v_all[:, :].rearrange("p (i h c) -> p i h c", i=NT, h=H)[:, i, :, 0:32]
                    return e.tensor_copy(dstv, bank().rearrange("p (h c) -> p h c", h=16))
                p.add("vector", lambda e, f=evv: f(e),
                      wait=[w(s_qkvm), w(s_om)], inc=[(s_qkve, 1)])

        p.add("vector", lambda e: e.memset(aoT[32:64, :], 0.0),
              wait=[w(s_qkvm), w(s_qkve)], inc=[(s_nrm, 1)])
        p.add("vector", lambda e: e.memset(aoT[96:128, :], 0.0), inc=[(s_nrm, 1)])
        for h in range(H):
            g, pb = h // 2, (h % 2) * 64
            for tqc in range(4):
                p.add("sync", lambda e, g=g, pb=pb, tqc=tqc: e.dma_start(
                    qstage[pb:pb + 64, :],
                    qT_scr.ap()[pb:pb + 64, g * 4096 + tqc * 1024: g * 4096 + (tqc + 1) * 1024]),
                    wait=[w(s_scm), w(s_scio)], inc=[(s_qst, 16)])
                for j in range(NT):
                    for half in range(2):
                        p.add("tensor",
                              lambda e, j=j, half=half, g=g, pb=pb: e.matmul(
                                  sc_ps[:, half * 512:(half + 1) * 512],
                                  kT[pb:pb + 64, g * 4096 + j * 128: g * 4096 + (j + 1) * 128],
                                  qstage[pb:pb + 64, half * 512:(half + 1) * 512],
                                  start=True, stop=True),
                              wait=([w(s_exp), w(s_qst), w(s_qkve)]) if (half == 0) else (),
                              inc=[(s_scm, 1)] if half == 1 else ())
                    p.add("scalar",
                          lambda e: e.activation(pt[:, :], sc_ps[:, :], AF.Exp, bias=cm.cb(-12.0)),
                          wait=[w(s_scm), w(s_om), w(s_scio)], inc=[(s_exp, 1)])
                    for half in range(2):
                        p.add("tensor",
                              lambda e, j=j, h=h, half=half, first=(j == 0), last=(j == NT - 1): e.matmul(
                                  oT_ps[:, half * 512:(half + 1) * 512],
                                  v_all[:, j * H * 33 + h * 33: j * H * 33 + (h + 1) * 33],
                                  pt[:, half * 512:(half + 1) * 512], start=first, stop=last),
                              wait=([w(s_exp)] + ([w(s_nrm)] if j == 0 else [])) if half == 0 else (),
                              inc=[(s_om, 1)] if half == 1 else ())
                p.add("vector", lambda e: e.reciprocal(rstg[0:1, :], oT_ps[32:33, :]),
                      wait=[w(s_om), w(s_bc)], selfsync=True)
                p.add("vector", lambda e: e.tensor_copy(pt[0:1, :], rstg[0:1, :]),
                      selfsync=True)
                for half in range(2):
                    p.add("tensor", lambda e, half=half: e.matmul(
                        y_ps[0:32, half * 512:(half + 1) * 512], cm.ones1[0:1, 0:32],
                        pt[0:1, half * 512:(half + 1) * 512], start=True, stop=True),
                        wait=[p.selfwait("vector"), w(s_bc)] if half == 0 else (),
                        inc=[(s_mlm, 1)] if half == 1 else ())
                p.add("vector", lambda e: e.tensor_copy(bcs[:, :], y_ps[0:32, :]),
                      wait=[w(s_mlm)], selfsync=True)
                def evo(e, pb=pb, g=g, tqc=tqc):
                    return e.tensor_mul(
                        aoT[pb:pb + 32, g * 4096 + tqc * 1024: g * 4096 + (tqc + 1) * 1024],
                        bcs[0:32, :], oT_ps[0:32, :])
                p.add("vector", lambda e, f=evo: f(e), inc=[(s_nrm, 1)])
                p.add("vector", lambda e: e.engine_nop(), inc=[(s_bc, 1)])

        p.add("gpsimd", lambda e: e.dma_start(
            ow_sb[:, :].rearrange("p (j c) -> p j c", j=8),
            owp.ap().rearrange("(j p) c -> p j c", p=128)),
            wait=[w(s_qkvm), w(s_scm)], inc=[(s_wgt, 16)])
        nml = 0
        for tqc8 in range(8):
            for dt in range(8):
                bank = lambda dt=dt: sc_ps[:, (dt % 2) * 512:((dt % 2) + 1) * 512]
                for j in range(8):
                    p.add("tensor",
                          lambda e, j=j, dt=dt, tqc8=tqc8, bank=bank: e.matmul(
                              bank(), ow_sb[:, j * D + dt * 128: j * D + dt * 128 + 128],
                              aoT[:, j * 4096 + tqc8 * 512: j * 4096 + (tqc8 + 1) * 512],
                              start=(j == 0), stop=(j == 7)),
                          wait=([w(s_nrm), w(s_wgt), w(s_exp), (s_mle, max(0, nml - 1))]) if j == 0 else (),
                          inc=[(s_mlm, 1)] if j == 7 else ())
                nml += 1
                p.add("vector",
                      lambda e, dt=dt, bank=bank: e.tensor_scalar(
                          pt[:, 0:512], bank(), bias_sb[:, 16 + dt:17 + dt], None, op0=ALU.add),
                      wait=[w(s_mlm), w(s_scio), w(s_exp)], inc=[(s_mle, 1)])
                p.add("sync", lambda e, dt=dt, tqc8=tqc8: e.dma_start(
                    mla_scr.ap()[:, dt * 4096 + tqc8 * 512: dt * 4096 + (tqc8 + 1) * 512],
                    pt[:, 0:512]),
                    wait=[w(s_mle)], inc=[(s_scio, 16)])

        for e_id in range(8):
            war = [w(s_hm), w(s_ym), w(s_scm), w(s_mlm)]
            p.add("gpsimd", lambda e, e_id=e_id: e.dma_start(
                w1g[:, :].rearrange("p (j f) -> p j f", j=8),
                w1.ap()[e_id].rearrange("(j p) f -> p j f", p=128)),
                wait=war, inc=[(s_wg2, 16)])
            p.add("gpsimd", lambda e, e_id=e_id: e.dma_start(
                w2g[:, :].rearrange("p (ft d) -> p ft d", ft=16),
                w2.ap()[e_id].rearrange("(ft p) d -> p ft d", p=128)),
                wait=war, inc=[(s_wg2, 16)])
            p.add("gpsimd", lambda e, e_id=e_id: e.dma_start(b1_sb[:, :], b1d.ap()[e_id]),
                  wait=war, inc=[(s_wg2, 16)])
            p.add("gpsimd", lambda e, e_id=e_id: e.dma_start(b2_sb[:, :], b2d.ap()[e_id]),
                  wait=war, inc=[(s_wg2, 16)])
            for tqc8 in range(8):
                p.add("sync", lambda e, tqc8=tqc8: e.dma_start(
                    mlac[:, :].rearrange("p (dt t) -> p dt t", dt=8),
                    mla_scr.ap().rearrange("p (dt t) -> p dt t", dt=8)[:, :, tqc8 * 512:(tqc8 + 1) * 512]),
                    wait=[w(s_hm), w(s_rtm), w(s_scio), w(s_om)], inc=[(s_mcl, 16)])
                xsrc = xr if e_id == 0 else xwr
                p.add("sync",
                      lambda e, tqc8=tqc8, xsrc=xsrc: e.dma_start(
                          cm.x_sb[:, :].rearrange("p (k d) -> p k d", k=4),
                          xsrc[:, tqc8 * 4:(tqc8 + 1) * 4, :]),
                      wait=[w(s_ya), w(s_xo)], inc=[(s_xc, 16)])
                if e_id == 0:
                    p.add("sync", lambda e, tqc8=tqc8: e.dma_start(lnxstage[:, :], lnx_scr.ap()[tqc8]),
                          wait=[w(s_ya), w(s_om)], inc=[(s_xc, 16)])
                    p.add("vector", lambda e: e.tensor_add(cm.x_sb[:, :], cm.x_sb[:, :], lnxstage[:, :]),
                          wait=[w(s_xc)], selfsync=True)
                    for tt in range(4):
                        for j in range(8):
                            p.add("tensor",
                                  lambda e, j=j, tt=tt: e.matmul(
                                      rt_ps[:, :], mlac[:, j * 512 + tt * 128: j * 512 + tt * 128 + 128],
                                      rw_sb[:, j * 8:(j + 1) * 8], start=(j == 0), stop=False),
                                  wait=([w(s_mcl), w(s_wgt), w(s_rte)]) if j == 0 else ())
                        p.add("tensor",
                              lambda e: e.matmul(rt_ps[:, :], cm.ones1[0:1, 0:128], rb_sb[:, :],
                                                 start=False, stop=True),
                              inc=[(s_rtm, 1)])
                        p.add("vector", lambda e: e.reduce_max(rb[:, 0:1], rt_ps[:, :], axis=AX.X),
                              wait=[w(s_rtm)], selfsync=True)
                        p.add("vector", lambda e: e.tensor_scalar(rb[:, 1:2], rb[:, 0:1], -1.0, None, op0=ALU.mult),
                              selfsync=True)
                        p.add("scalar",
                              lambda e: e.activation(rb[:, 16:24], rt_ps[:, :], AF.Exp,
                                                     bias=rb[:, 1:2], accum_out=rb[:, 2:3]),
                              wait=[p.selfwait("vector")], inc=[(s_rte, 1)])
                        p.add("vector", lambda e: e.reciprocal(rb[:, 3:4], rb[:, 2:3]),
                              wait=[w(s_rte)], selfsync=True)
                        p.add("vector", lambda e: e.tensor_scalar(rb[:, 24:32], rb[:, 16:24], rb[:, 3:4], None, op0=ALU.mult),
                              selfsync=True)
                        p.add("vector", lambda e: e.reduce_max(rb[:, 4:5], rb[:, 24:32], axis=AX.X), selfsync=True)
                        p.add("vector", lambda e: e.tensor_scalar(rb[:, 32:40], rb[:, 24:32], rb[:, 4:5], None, op0=ALU.is_ge), selfsync=True)
                        p.add("vector", lambda e: e.scalar_tensor_tensor(rb[:, 40:48], rb[:, 32:40], -10.0, rb[:, 24:32], op0=ALU.mult, op1=ALU.add), selfsync=True)
                        p.add("vector", lambda e: e.reduce_max(rb[:, 5:6], rb[:, 40:48], axis=AX.X), selfsync=True)
                        p.add("vector", lambda e: e.tensor_scalar(rb[:, 32:40], rb[:, 24:32], rb[:, 5:6], None, op0=ALU.is_ge), selfsync=True)
                        p.add("vector", lambda e: e.tensor_add(rb[:, 6:7], rb[:, 4:5], rb[:, 5:6]), selfsync=True)
                        p.add("vector", lambda e: e.tensor_scalar(rb[:, 7:8], rb[:, 6:7], 1e-8, None, op0=ALU.add), selfsync=True)
                        p.add("vector", lambda e: e.reciprocal(rb[:, 8:9], rb[:, 7:8]), selfsync=True)
                        p.add("vector", lambda e: e.tensor_mul(rb[:, 40:48], rb[:, 24:32], rb[:, 32:40]), selfsync=True)
                        p.add("vector", lambda e, tt=tt, tqc8=tqc8: e.tensor_scalar(
                            c_all[:, (tqc8 * 4 + tt) * 8:(tqc8 * 4 + tt + 1) * 8],
                            rb[:, 40:48], rb[:, 8:9], 0.3, op0=ALU.mult, op1=ALU.mult),
                            selfsync=True)
                for ft in range(16):
                    bank = lambda ft=ft: sc_ps[:, (ft % 2) * 512:((ft % 2) + 1) * 512]
                    for j in range(8):
                        p.add("tensor",
                              lambda e, j=j, ft=ft, bank=bank: e.matmul(
                                  bank(), w1g[:, j * 2048 + ft * 128: j * 2048 + ft * 128 + 128],
                                  mlac[:, j * 512:(j + 1) * 512],
                                  start=(j == 0), stop=(j == 7)),
                              wait=([w(s_mcl), w(s_wg2), w(s_rtm), w(s_mle),
                                     (s_he, max(0, s_hm.n - 1))]) if j == 0 else (),
                              inc=[(s_hm, 1)] if j == 7 else ())
                    p.add("scalar",
                          lambda e, ft=ft, bank=bank: e.activation(
                              h1T[:, ft * 512:(ft + 1) * 512], bank(), AF.Gelu,
                              bias=b1_sb[:, ft:ft + 1]),
                          wait=[w(s_hm), w(s_ym), w(s_om)], inc=[(s_he, 1)])
                for tt in range(4):
                    for ft in range(16):
                        for dh in range(2):
                            p.add("tensor",
                                  lambda e, tt=tt, ft=ft, dh=dh: e.matmul(
                                      y_ps[:, dh * 512:(dh + 1) * 512],
                                      h1T[:, ft * 512 + tt * 128: ft * 512 + tt * 128 + 128],
                                      w2g[:, ft * 1024 + dh * 512: ft * 1024 + (dh + 1) * 512],
                                      start=(ft == 0), stop=False),
                                  wait=([w(s_he), w(s_ya), w(s_bc), w(s_mlm)]) if (ft == 0 and dh == 0) else ())
                    for dh in range(2):
                        p.add("tensor",
                              lambda e, dh=dh: e.matmul(
                                  y_ps[:, dh * 512:(dh + 1) * 512],
                                  cm.ones1[0:1, 0:128], b2_sb[:, dh * 512:(dh + 1) * 512],
                                  start=False, stop=True),
                              inc=[(s_ym, 1)] if dh == 1 else ())
                    cap = lambda tt=tt, e_id=e_id, tqc8=tqc8: c_all[:, (tqc8 * 4 + tt) * 8 + e_id: (tqc8 * 4 + tt) * 8 + e_id + 1]
                    p.add("vector", lambda e, tt=tt, cap=cap: e.scalar_tensor_tensor(
                        cm.x_sb[:, tt * D:(tt + 1) * D], y_ps[:, :], cap(),
                        cm.x_sb[:, tt * D:(tt + 1) * D], op0=ALU.mult, op1=ALU.add),
                        wait=[w(s_ym), w(s_xc)], inc=[(s_ya, 1)])
                p.add("sync",
                      lambda e, tqc8=tqc8: e.dma_start(xwr[:, tqc8 * 4:(tqc8 + 1) * 4, :],
                                                       cm.x_sb[:, :].rearrange("p (k d) -> p k d", k=4)),
                      wait=[w(s_ya)], inc=[(s_xo, 16)])

        p.add("sync", lambda e: e.wait_ge(s_xo.h, s_xo.n))
        cm.finalize_consts()
        p.emit()
    return nc


def l1_ffn_inmap(inputs, b, x):
    qw = np.asarray(inputs["L1_ffn_q_w"], np.float32)
    kw = np.asarray(inputs["L1_ffn_k_w"], np.float32)
    qb = np.asarray(inputs["L1_ffn_q_b"], np.float32)
    kb = np.asarray(inputs["L1_ffn_k_b"], np.float32)
    ow = np.asarray(inputs["L1_ffn_o_w"], np.float32)
    obv = np.asarray(inputs["L1_ffn_o_b"], np.float32)
    qwp = np.zeros((1024, 1024), np.float32)
    kwp = np.zeros((1024, 1024), np.float32)
    qbp = np.zeros((1024,), np.float32)
    kbp = np.zeros((1024,), np.float32)
    owp = np.zeros((1024, 1024), np.float32)
    for h in range(16):
        dstc = (h // 2) * 128 + (h % 2) * 64
        qwp[:, dstc:dstc + 32] = qw[:, h * 64:h * 64 + 32]
        kwp[:, dstc:dstc + 32] = kw[:, h * 32:h * 32 + 32]
        qbp[dstc:dstc + 32] = qb[h * 64:h * 64 + 32]
        kbp[dstc:dstc + 32] = kb[h * 32:h * 32 + 32]
        owp[dstc:dstc + 32, :] = ow[h * 32:h * 32 + 32, :]
    return {
        "x_in": np.ascontiguousarray(x[b]),
        "qwp": qwp, "kwp": kwp,
        "vw": hC(inputs["L1_ffn_v_w"]),
        "qbp": np.ascontiguousarray(qbp.reshape(8, 128).T),
        "kbp": np.ascontiguousarray(kbp.reshape(8, 128).T),
        "vbd": np.asarray(inputs["L1_ffn_v_b"], np.float32).reshape(1, 512),
        "owp": owp,
        "obd": np.ascontiguousarray(obv.reshape(8, 128).T),
        "rwd": hC(inputs["L1_ffn_router_w"]),
        "rbd": np.asarray(inputs["L1_ffn_router_b"], np.float32).reshape(1, 8),
        "w1": hC(inputs["L1_ffn_w1"]),
        "b1d": np.ascontiguousarray(np.asarray(inputs["L1_ffn_b1"], np.float32).reshape(8, 16, 128).transpose(0, 2, 1)),
        "w2": hC(inputs["L1_ffn_w2"]),
        "b2d": np.asarray(inputs["L1_ffn_b2"], np.float32).reshape(8, 1, 1024),
    }





def _alibi_slopes(n):
    def pow2(n):
        start = 2 ** (-2 ** (-(math.log2(n) - 3)))
        return [start * start ** i for i in range(n)]
    if math.log2(n).is_integer():
        return pow2(n)
    c = 2 ** math.floor(math.log2(n))
    return pow2(c) + _alibi_slopes(2 * c)[0::2][:n - c]


def build_L2_attn(upto=99, debug=False):
    nc = bass.Bass("TRN2", target_bir_lowering=False)
    x_in = nc.dram_tensor("x_in", [T, D], F32, kind="ExternalInput")
    x_out = nc.dram_tensor("x_out", [T, D], F32, kind="ExternalOutput")
    wqkv = nc.dram_tensor("wqkv", [D, 3 * D], F32, kind="ExternalInput")
    bqkv2 = nc.dram_tensor("bqkv2", [128, 16], F32, kind="ExternalInput")
    bv_row = nc.dram_tensor("bv_row", [1, D], F32, kind="ExternalInput")
    wproj = nc.dram_tensor("wproj", [D, D], F32, kind="ExternalInput")
    bp_row = nc.dram_tensor("bp_row", [1, D], F32, kind="ExternalInput")
    qT_scr = nc.dram_tensor("qT_scr", [128, 8 * 4096], BF16, kind="Internal")
    aoT_scr = nc.dram_tensor("aoT_scr", [128, 8 * 4096], BF16, kind="Internal")
    lnT_scr = nc.dram_tensor("lnT_scr", [8, 128, 4096], BF16, kind="Internal")

    H = 16
    slopes = _alibi_slopes(H)
    CB = 30.0
    ab = np.zeros((128, H * 32), np.float32)
    for h in range(H):
        for dlt in range(32):
            ab[:, h * 32 + dlt] = slopes[h] * np.arange(128) - slopes[h] * 128.0 * dlt - CB
    dmax = [min(31, int(math.ceil(150.0 / (128.0 * s)))) for s in slopes]
    dmc = np.zeros((128, H * 128), np.float32)
    tkv = np.arange(128)[:, None]
    tqv = np.arange(128)[None, :]
    for h in range(H):
        dmc[:, h * 128:(h + 1) * 128] = np.where(tkv > tqv, -8.0 * slopes[h] * (tkv - tqv), 0.0)
    qxc = np.zeros((128, 8 * 512), np.float32)
    for h in range(H):
        qxc[(h % 2) * 64, (h // 2) * 512:(h // 2 + 1) * 512] = -8.0 * slopes[h] * (np.arange(512) % 128)

    if debug:
        dbg_kT = nc.dram_tensor("dbg_kT", [128, 8 * 4096], F32, kind="ExternalOutput")
        dbg_v = nc.dram_tensor("dbg_v", [128, 4 * H * 65], F32, kind="ExternalOutput")
        dbg_pt = nc.dram_tensor("dbg_pt", [128, 512], F32, kind="ExternalOutput")
        dbg_ao = nc.dram_tensor("dbg_ao", [128, 512], F32, kind="ExternalOutput")
    with contextlib.ExitStack() as stack:
        p = Prog(nc, stack)
        cm = Common(p)
        cm.cb(-30.0)
        sb = lambda name, shape, dt=BF16: stack.enter_context(nc.sbuf_tensor(name, shape, dt))
        ps = lambda name, shape, dt=F32: stack.enter_context(nc.psum_tensor(name, shape, dt))

        kT = sb("kT", [128, 8 * 4096])
        v_all = sb("v_all", [128, NT * H * 65])
        ar2 = Arena(nc, stack, 16 * 1024, name="arena2")
        lnT4 = ar2.at(0, "lnT4", [128, 4096], BF16)
        qstage = sb("qstage", [128, 512])
        pt = sb("pt", [128, 512])
        ostage = sb("ostage", [128, 512])
        rbf = sb("rbf", [128, 512])
        wsb = sb("wsb", [128, 8 * 1024])
        aoc = sb("aoc", [128, 8 * 128])
        x1 = sb("x1", [128, 1024], F32)
        rstg = sb("rstg", [128, 512], F32)
        bcs = sb("bcs", [128, 512], F32)
        dmscr = sb("dmscr", [128, 128], F32)
        bias_sb = sb("bias_sb", [128, 16], F32)
        alcst = sb("alcst", [128, H * 32], F32)
        dmcs = ar2.at(0, "dmcs", [128, H * 128], F32)
        qxs = ar2.at(8, "qxs", [128, 8 * 512], BF16)
        ones_a = sb("ones_a", [128, 128])
        bvr = sb("bvr", [1, D])
        bpr = sb("bpr", [1, D])

        qk_ps = [ps("qk_ps0", [128, 512]), ps("qk_ps1", [128, 512])]
        sc_ps = ps("sc_ps", [128, 512])
        oT_ps = ps("oT_ps", [128, 512])
        rs_ps = ps("rs_ps", [128, 512])
        pr_ps = ps("pr_ps", [128, 1024])

        abd = nc.inline_tensor(ab, name="alibic")
        dmd = nc.inline_tensor(dmc, name="dmcc")
        qxd = nc.inline_tensor(qxc.astype(bfdt), name="qxcc")

        s_dma = p.sem("dma")
        s_wgt = p.sem("wgt")
        s_qkvm = p.sem("qkvm")
        s_qkve = p.sem("qkve")
        s_scio = p.sem("scio")
        s_ltio = p.sem("ltio")
        s_qst = p.sem("qst")
        s_scm = p.sem("scm")
        s_msk = p.sem("msk")
        s_exp = p.sem("exp")
        s_om = p.sem("om")
        s_nrm = p.sem("nrm")
        s_bcm = p.sem("bcm")
        s_aocl = p.sem("aocl")
        s_prm = p.sem("prm")
        s_res = p.sem("res")
        s_xc = p.sem("xc")
        s_xo = p.sem("xo")

        cm.load_consts(s_dma)
        p.add("sync", lambda e: e.dma_start(alcst[:, :], abd.ap()[:, :]), inc=[(s_dma, 16)])
        p.add("gpsimd", lambda e: e.dma_start(bias_sb[:, :], bqkv2.ap()[:, :]), inc=[(s_wgt, 16)])
        p.add("gpsimd", lambda e: e.dma_start(bvr[:, :], bv_row.ap()[:, :]), inc=[(s_wgt, 16)])
        p.add("gpsimd", lambda e: e.dma_start(bpr[:, :], bp_row.ap()[:, :]), inc=[(s_wgt, 16)])
        p.add("vector", lambda e: e.memset(v_all[:, :], 1.0))
        p.add("vector", lambda e: e.memset(ones_a[:, :], 1.0))

        xr = x_in.ap().rearrange("(i p) d -> p i d", p=128)
        xwr = x_out.ap().rearrange("(i p) d -> p i d", p=128)

        for c4 in range(8):
            tiles = [c4 * 4 + k for k in range(4)]
            p.add("sync",
                  lambda e, c4=c4: e.dma_start(
                      cm.x_sb[:, :].rearrange("p (k d) -> p k d", k=4),
                      xr[:, c4 * 4:(c4 + 1) * 4, :]),
                  wait=[w(cm.s_lnbf)], inc=[(s_xc, 16)])
            xdep = [w(s_xc)]
            for k, i in enumerate(tiles):
                slot = i % 2
                cm.ln_tile(i, (lambda s=slot: cm.lnbf[:, s * D:(s + 1) * D]), deps=xdep,
                           src_col=k * D)
                dst = lambda k=k: lnT4[:, :].rearrange("p (j t) -> p j t", j=8)[:, :, k * 128:(k + 1) * 128]
                cm.transpose_tile((lambda s=slot: cm.lnbf[:, s * D:(s + 1) * D]),
                                  dst, wait_src=[w(cm.s_lnbf)],
                                  evac_wait=[w(s_ltio)])
            p.add("sync", lambda e, c4=c4: e.dma_start(lnT_scr.ap()[c4], lnT4[:, :]),
                  wait=[w(cm.s_lnT)], inc=[(s_ltio, 16)])

        for which, cols0 in (("q", 0), ("k", D), ("v", 2 * D)):
            p.add("gpsimd", lambda e, cols0=cols0: e.dma_start(
                wsb[:, :].rearrange("p (j c) -> p j c", j=8),
                wqkv.ap().rearrange("(j p) c -> p j c", p=128)[:, :, cols0:cols0 + D]),
                wait=[w(s_qkvm)], inc=[(s_wgt, 16)])
            wdep = [w(s_wgt)]
            for c4 in range(8):
                p.add("sync", lambda e, c4=c4: e.dma_start(lnT4[:, :], lnT_scr.ap()[c4]),
                      wait=[w(s_qkvm), w(s_ltio), w(cm.s_lnT)], inc=[(s_ltio, 16)])
                ldep = [w(s_ltio)]
                if which in ("q", "k"):
                    for ct in range(8):
                        bank = qk_ps[ct % 2]
                        for j in range(8):
                            p.add("tensor",
                                  lambda e, j=j, ct=ct, bank=bank: e.matmul(
                                      bank[:, :], wsb[:, j * D + ct * 128: j * D + ct * 128 + 128],
                                      lnT4[:, j * 512:(j + 1) * 512],
                                      start=(j == 0), stop=(j == 7)),
                                  wait=(wdep + ldep + [(s_qkve, max(0, s_qkvm.n - 1))]) if j == 0 else (),
                                  inc=[(s_qkvm, 1)] if j == 7 else ())
                        if which == "q":
                            p.add("vector",
                                  lambda e, ct=ct, bank=bank: e.tensor_scalar(
                                      pt[:, 0:512], bank[:, :], bias_sb[:, ct:ct + 1], None, op0=ALU.add),
                                  wait=[w(s_qkvm), w(s_scio)], inc=[(s_qkve, 1)])
                            p.add("sync", lambda e, ct=ct, c4=c4: e.dma_start(
                                qT_scr.ap()[:, ct * 4096 + c4 * 512: ct * 4096 + (c4 + 1) * 512],
                                pt[:, 0:512]),
                                wait=[w(s_qkve)], inc=[(s_scio, 16)])
                        else:
                            p.add("vector",
                                  lambda e, ct=ct, bank=bank, c4=c4: e.tensor_scalar(
                                      kT[:, ct * 4096 + c4 * 512: ct * 4096 + (c4 + 1) * 512],
                                      bank[:, :], bias_sb[:, 8 + ct:9 + ct], None, op0=ALU.add),
                                  wait=[w(s_qkvm)], inc=[(s_qkve, 1)])
                else:
                    for k in range(4):
                        for half in range(2):
                            bank = qk_ps[half]
                            for j in range(8):
                                p.add("tensor",
                                      lambda e, j=j, k=k, half=half, bank=bank: e.matmul(
                                          bank[:, :],
                                          lnT4[:, j * 512 + k * 128: j * 512 + k * 128 + 128],
                                          wsb[:, j * D + half * 512: j * D + (half + 1) * 512],
                                          start=(j == 0), stop=False),
                                      wait=(wdep + ldep + [(s_qkve, max(0, s_qkvm.n - 1))]) if j == 0 else ())
                            p.add("tensor",
                                  lambda e, half=half, bank=bank: e.matmul(
                                      bank[:, :], cm.ones1[0:1, 0:128],
                                      bvr[:, half * 512:(half + 1) * 512],
                                      start=False, stop=True),
                                  inc=[(s_qkvm, 1)])
                            def evv(e, i=c4 * 4 + k, half=half, bank=bank):
                                dstv = v_all[:, :].rearrange("p (i h c) -> p i h c", i=NT, h=H)[
                                    :, i, half * 8:(half + 1) * 8, 0:64]
                                return e.tensor_copy(dstv, bank[:, :].rearrange("p (h c) -> p h c", h=8))
                            p.add("vector", lambda e, f=evv: f(e),
                                  wait=[w(s_qkvm), w(s_om)], inc=[(s_qkve, 1)])

        if upto not in (24, 26):
            p.add("sync", lambda e: e.dma_start(dmcs[:, :], dmd.ap()[:, :]),
                  wait=[w(s_qkvm), w(s_ltio)], inc=[(s_dma, 16)])
            p.add("sync", lambda e: e.dma_start(qxs[:, :], qxd.ap()[:, :]),
                  wait=[w(s_qkvm), w(s_ltio)], inc=[(s_dma, 16)])
            p.add("tensor", None, wait=[w(s_dma)])
            p.add("vector", None, wait=[w(s_dma)])
            p.add("scalar", None, wait=[w(s_dma)])
        s_dbg = p.sem("dbg")
        if debug:
            p.add("gpsimd", lambda e: e.dma_start(dbg_kT.ap()[:, :], kT[:, :]),
                  wait=[w(s_qkve)], inc=[(s_dbg, 16)])
            p.add("gpsimd", lambda e: e.dma_start(dbg_v.ap()[:, :], v_all[:, 0:4 * H * 65]),
                  wait=[w(s_qkve)], inc=[(s_dbg, 16)])
        for h in range(H if upto >= 2 else 0):
            g, pb = h // 2, (h % 2) * 64
            for i4 in range(8):
                jlo = max(0, i4 * 4 - dmax[h])
                p.add("sync", lambda e, g=g, pb=pb, i4=i4: e.dma_start(
                    qstage[pb:pb + 64, :],
                    qT_scr.ap()[pb:pb + 64, g * 4096 + i4 * 512: g * 4096 + (i4 + 1) * 512]),
                    wait=[w(s_scm), w(s_scio)], inc=[(s_qst, 16)])
                first = True
                for jt in range(jlo, NT):
                    fully_past = (jt < i4 * 4)
                    fully_future = (jt > i4 * 4 + 3)
                    sp1 = fully_future or (upto == 22)
                    p.add("tensor",
                          lambda e, jt=jt, g=g, pb=pb, sp=sp1: e.matmul(
                              sc_ps[:, :],
                              kT[pb:pb + 64, g * 4096 + jt * 128: g * 4096 + (jt + 1) * 128],
                              qstage[pb:pb + 64, :],
                              start=True, stop=sp),
                          wait=[w(s_exp), w(s_msk), w(s_qst), w(s_qkve)],
                          inc=[(s_scm, 1)] if fully_future else ())
                    if upto == 22:
                        if not fully_future:
                            p.add("tensor", lambda e: e.nop(), inc=[(s_scm, 1)])
                    elif fully_past:
                        p.add("tensor",
                              lambda e, g=g, pb=pb: e.matmul(
                                  sc_ps[:, :], ones_a[pb:pb + 1, 0:128],
                                  qxs[pb:pb + 1, g * 512:(g + 1) * 512],
                                  start=False, stop=True),
                              inc=[(s_scm, 1)])
                    elif not fully_future:
                        subs = [s2 for s2 in range(4) if i4 * 4 + s2 >= jt]
                        for n2, s2 in enumerate(subs):
                            p.add("tensor",
                                  lambda e, g=g, pb=pb, s2=s2, i4=i4, last=(n2 == len(subs) - 1): e.matmul(
                                      sc_ps[:, s2 * 128:(s2 + 1) * 128], ones_a[pb:pb + 1, 0:128],
                                      qxs[pb:pb + 1, g * 512 + s2 * 128: g * 512 + (s2 + 1) * 128],
                                      start=False, stop=last),
                                  inc=[(s_scm, 1)] if n2 == len(subs) - 1 else ())
                    if fully_future and upto != 23:
                        p.add("scalar",
                              lambda e: e.activation(pt[:, :], sc_ps[:, :], AF.Exp,
                                                     scale=0.125, bias=cm.cb(-30.0)),
                              wait=[w(s_scm), w(s_om)], inc=[(s_exp, 1)])
                    else:
                        subops = []
                        for sub in range(4):
                            isub = i4 * 4 + sub
                            dlt = isub - jt
                            if dlt > dmax[h]:
                                p.add("vector",
                                      lambda e, sub=sub: e.memset(pt[:, sub * 128:(sub + 1) * 128], 0.0),
                                      wait=[w(s_om), w(s_scm)], inc=[(s_msk, 1)])
                                subops.append(("v", sub))
                                continue
                            diag = (dlt == 0)
                            if diag:
                                p.add("vector",
                                      lambda e, sub=sub, h=h: e.tensor_add(
                                          dmscr[:, :],
                                          sc_ps[:, sub * 128:(sub + 1) * 128],
                                          dmcs[:, h * 128:(h + 1) * 128]),
                                      wait=[w(s_scm)], inc=[(s_msk, 1)])
                            bias_ap = (lambda h=h, dlt=dlt: alcst[:, h * 32 + dlt:h * 32 + dlt + 1]) if dlt >= 0 else (lambda: cm.cb(-30.0))
                            is_last_scalar = all((i4 * 4 + s2 - jt > dmax[h]) for s2 in range(sub + 1, 4))
                            src_ap = (lambda: dmscr[:, :]) if diag else (
                                lambda sub=sub: sc_ps[:, sub * 128:(sub + 1) * 128])
                            p.add("scalar",
                                  lambda e, sub=sub, b=bias_ap, sa=src_ap: e.activation(
                                      pt[:, sub * 128:(sub + 1) * 128],
                                      sa(), AF.Exp,
                                      scale=0.125, bias=b()),
                                  wait=[w(s_scm), w(s_om), w(s_msk)],
                                  inc=[(s_exp, 1)] if is_last_scalar else ())
                    last = (jt == NT - 1)
                    p.add("tensor",
                          lambda e, jt=jt, h=h, pb=pb, first=first, last=last: e.matmul(
                              oT_ps[pb:pb + 64, :],
                              v_all[:, jt * H * 65 + h * 65: jt * H * 65 + h * 65 + 64],
                              pt[:, :], start=first, stop=last),
                          wait=[w(s_exp), w(s_msk)] + ([w(s_nrm)] if first else []),
                          inc=[(s_om, 1)])
                    p.add("tensor",
                          lambda e, jt=jt, h=h, pb=pb, first=first, last=last: e.matmul(
                              rs_ps[pb:pb + 1, :],
                              v_all[:, jt * H * 65 + h * 65 + 64: jt * H * 65 + h * 65 + 65],
                              pt[:, :], start=first, stop=last),
                          inc=[(s_om, 1)])
                    first = False
                    if debug and h == 2 and i4 == 1 and jt == 4:
                        p.add("gpsimd", lambda e: e.dma_start(dbg_pt.ap()[:, :], pt[:, :]),
                              wait=[w(s_exp)], inc=[(s_dbg, 16)])
                p.add("vector", lambda e, pb=pb: e.reciprocal(rstg[pb:pb + 1, :], rs_ps[pb:pb + 1, :]),
                      wait=[w(s_om), w(s_bcm)], selfsync=True)
                p.add("vector", lambda e, pb=pb: e.tensor_copy(rbf[pb:pb + 1, :], rstg[pb:pb + 1, :]),
                      selfsync=True)
                p.add("tensor", lambda e, pb=pb: e.matmul(
                    pr_ps[pb:pb + 64, 0:512], ones_a[pb:pb + 1, 0:64], rbf[pb:pb + 1, :],
                    start=True, stop=True),
                    wait=[p.selfwait("vector"), w(s_prm), w(s_res)], inc=[(s_bcm, 1)])
                p.add("vector", lambda e, pb=pb: e.tensor_copy(bcs[pb:pb + 64, :], pr_ps[pb:pb + 64, 0:512]),
                      wait=[w(s_bcm)], selfsync=True)
                p.add("vector", lambda e, pb=pb: e.tensor_mul(ostage[pb:pb + 64, :], bcs[pb:pb + 64, :],
                                                              oT_ps[pb:pb + 64, :]),
                      inc=[(s_nrm, 1)])
                p.add("sync", lambda e, g=g, pb=pb, i4=i4: e.dma_start(
                    aoT_scr.ap()[pb:pb + 64, g * 4096 + i4 * 512: g * 4096 + (i4 + 1) * 512],
                    ostage[pb:pb + 64, :]),
                    wait=[w(s_nrm)], inc=[(s_scio, 16)])
                if debug and h == 2 and i4 == 1:
                    p.add("gpsimd", lambda e, pb=pb: e.dma_start(dbg_ao.ap()[0:64, :], ostage[pb:pb + 64, :]),
                          wait=[w(s_nrm)], inc=[(s_dbg, 16)])

        if upto < 3:
            p.add("sync", lambda e: e.wait_ge(s_scio.h, s_scio.n))
            cm.finalize_consts()
            p.emit()
            return nc
        p.add("gpsimd", lambda e: e.dma_start(
            wsb[:, :].rearrange("p (j c) -> p j c", j=8),
            wproj.ap().rearrange("(j p) c -> p j c", p=128)),
            wait=[w(s_qkvm), w(s_scm)], inc=[(s_wgt, 16)])
        for i in range(NT if upto >= 4 else 1):
            p.add("sync", lambda e, i=i: e.dma_start(
                aoc[:, :].rearrange("p (g t) -> p g t", g=8),
                aoT_scr.ap().rearrange("p (g t) -> p g t", g=8)[:, :, i * 128:(i + 1) * 128]),
                wait=[w(s_prm), w(s_scio)], inc=[(s_aocl, 16)])
            p.add("sync", lambda e, i=i: e.dma_start(x1[:, :], xr[:, i, :]),
                  wait=[w(s_res), w(s_xo)], inc=[(s_aocl, 16)])
            if upto == 31:
                p.add("sync", lambda e, i=i: e.dma_start(xwr[:, i, :], x1[:, :]),
                      wait=[w(s_aocl)], inc=[(s_xo, 16)])
                break
            for g in range(8 if upto != 32 else 1):
                for dh in range(2):
                    p.add("tensor",
                          lambda e, g=g, dh=dh: e.matmul(
                              pr_ps[:, dh * 512:(dh + 1) * 512],
                              aoc[:, g * 128:(g + 1) * 128],
                              wsb[:, g * 1024 + dh * 512: g * 1024 + (dh + 1) * 512],
                              start=(g == 0), stop=False),
                          wait=([w(s_aocl), w(s_wgt), w(s_res), w(s_bcm), w(s_nrm)]) if (g == 0 and dh == 0) else ())
            for dh in range(2):
                p.add("tensor",
                      lambda e, dh=dh: e.matmul(
                          pr_ps[:, dh * 512:(dh + 1) * 512],
                          cm.ones1[0:1, 0:128], bpr[:, dh * 512:(dh + 1) * 512],
                          start=False, stop=True),
                      inc=[(s_prm, 1)] if dh == 1 else ())
            p.add("vector", lambda e: e.tensor_add(x1[:, :], x1[:, :], pr_ps[:, :]),
                  wait=[w(s_prm), w(s_aocl)], inc=[(s_res, 1)])
            p.add("sync", lambda e, i=i: e.dma_start(xwr[:, i, :], x1[:, :]),
                  wait=[w(s_res)], inc=[(s_xo, 16)])

        p.add("sync", lambda e: e.wait_ge(s_xo.h, s_xo.n))
        if s_dbg.n:
            p.add("sync", lambda e: e.wait_ge(s_dbg.h, s_dbg.n))
        cm.finalize_consts()
        p.emit()
    return nc


def l2_attn_inmap(inputs, b, x):
    bq = np.asarray(inputs["L2_attn_b_qkv"], np.float32)
    bqk = np.concatenate([bq[0:1024].reshape(8, 128).T, bq[1024:2048].reshape(8, 128).T], axis=1)
    return {
        "x_in": np.ascontiguousarray(x[b]),
        "wqkv": hC(inputs["L2_attn_W_qkv"]),
        "bqkv2": np.ascontiguousarray(bqk),
        "bv_row": bq[2048:3072].reshape(1, 1024).copy(),
        "wproj": hC(inputs["L2_attn_W_proj"]),
        "bp_row": np.asarray(inputs["L2_attn_b_proj"], np.float32).reshape(1, 1024).copy(),
    }





def build_L3_attn(debug=False):
    nc = bass.Bass("TRN2", target_bir_lowering=False)
    x_in = nc.dram_tensor("x_in", [T, D], F32, kind="ExternalInput")
    x_out = nc.dram_tensor("x_out", [T, D], F32, kind="ExternalOutput")
    wqkv = nc.dram_tensor("wqkv", [D, 3 * D], F32, kind="ExternalInput")
    bq_row = nc.dram_tensor("bq_row", [1, 3 * D], F32, kind="ExternalInput")
    wln = nc.dram_tensor("wln", [D, D], F32, kind="ExternalInput")
    bl_row = nc.dram_tensor("bl_row", [1, D], F32, kind="ExternalInput")
    skind = "ExternalOutput" if debug else "Internal"
    v_scr = nc.dram_tensor("v_scr", [8, 128, 4096], BF16, kind=skind)
    q2_scr = nc.dram_tensor("q2_scr", [8, 128, 4096], BF16, kind=skind)
    if debug:
        dbg_kv = nc.dram_tensor("dbg_kv", [128, 8 * D], F32, kind="ExternalOutput")
        dbg_k2 = nc.dram_tensor("dbg_k2", [128, 4 * D], F32, kind="ExternalOutput")
        dbg_rcn = nc.dram_tensor("dbg_rcn", [128, 8], F32, kind="ExternalOutput")
        dbg_cq = nc.dram_tensor("dbg_cq", [1, D], F32, kind="ExternalOutput")

    with contextlib.ExitStack() as stack:
        p = Prog(nc, stack)
        cm = Common(p)
        sb = lambda name, shape, dt=BF16: stack.enter_context(nc.sbuf_tensor(name, shape, dt))
        ps = lambda name, shape, dt=F32: stack.enter_context(nc.psum_tensor(name, shape, dt))

        ar = Arena(nc, stack, 48 * 1024)
        wqkv_sb = ar.at(0, "wqkv_sb", [128, 8 * 3 * D], BF16)
        kv_bf = ar.at(0, "kv_bf", [128, 8 * D], BF16)
        wln_sb = ar.at(16, "wln_sb", [128, 8 * D], BF16)
        k2 = sb("k2", [128, NT * D])
        lnT4 = sb("lnT4", [128, 4096])
        qkvbuf = sb("qkvbuf", [128, 3 * D], F32)
        chunkb = sb("chunkb", [128, 2 * 4096])
        vstage = sb("vstage", [128, D])
        z_bf = sb("z_bf", [128, D])
        zT = sb("zT", [128, D])
        q2T = sb("q2T", [128, D])
        sqs = sb("sqs", [128, D])
        bqr = sb("bqr", [1, 3 * D])
        blr = sb("blr", [1, D])
        colsq = sb("colsq", [1, D], F32)
        s_csa = None
        rcn = sb("rcn", [128, 8], F32)
        rcnb = sb("rcnb", [1, D])
        onesc = sb("onesc", [128, 1])
        st2 = sb("st2", [128, 16], F32)
        x1 = sb("x1", [128, D], F32)

        kv_ps = ps("kv_ps", [128, 2048])
        cs_ps = [ps("cs_ps0", [128, 512]), ps("cs_ps1", [128, 512])]
        qb = [lambda: kv_ps[:, 0:512], lambda: kv_ps[:, 512:1024]]

        s_dma = p.sem("dma")
        s_wgt = p.sem("wgt")
        s_qm = p.sem("qm")
        s_qe = p.sem("qe")
        s_ch = p.sem("ch")
        s_io = p.sem("io")
        s_cs = p.sem("cs")
        s_csa2 = p.sem("csa2")
        s_vld = p.sem("vld")
        s_kvm = p.sem("kvm")
        s_kve = p.sem("kve")
        s_rt = p.sem("rt")
        s_t2m = p.sem("t2m")
        s_zb = p.sem("zb")
        s_lnm = p.sem("lnm")
        s_res = p.sem("res")
        s_xc = p.sem("xc")
        s_xo = p.sem("xo")
        s_chld = p.sem("chld")

        cm.load_consts(s_dma)
        for _ge in ("tensor", "vector", "scalar"):
            p.add(_ge, None, wait=[w(s_dma)])
        p.add("gpsimd", lambda e: e.dma_start(
            wqkv_sb[:, :].rearrange("p (j c) -> p j c", j=8),
            wqkv.ap().rearrange("(j p) c -> p j c", p=128)), inc=[(s_wgt, 16)])
        p.add("gpsimd", lambda e: e.dma_start(bqr[:, :], bq_row.ap()[:, :]), inc=[(s_wgt, 16)])
        p.add("gpsimd", lambda e: e.dma_start(blr[:, :], bl_row.ap()[:, :]), inc=[(s_wgt, 16)])
        p.add("vector", lambda e: e.memset(onesc[:, :], 1.0))
        p.add("vector", lambda e: e.memset(colsq[0:1, :], 0.0))

        xr = x_in.ap().rearrange("(i p) d -> p i d", p=128)
        xwr = x_out.ap().rearrange("(i p) d -> p i d", p=128)

        for c4 in range(8):
            tiles = [c4 * 4 + k for k in range(4)]
            p.add("sync",
                  lambda e, c4=c4: e.dma_start(
                      cm.x_sb[:, :].rearrange("p (k d) -> p k d", k=4),
                      xr[:, c4 * 4:(c4 + 1) * 4, :]),
                  wait=[w(cm.s_lnbf)], inc=[(s_xc, 16)])
            xdep = [w(s_xc)]
            for k, i in enumerate(tiles):
                slot = i % 2
                cm.ln_tile(i, (lambda s=slot: cm.lnbf[:, s * D:(s + 1) * D]), deps=xdep,
                           src_col=k * D)
                dst = lambda k=k: lnT4[:, :].rearrange("p (j t) -> p j t", j=8)[:, :, k * 128:(k + 1) * 128]
                cm.transpose_tile((lambda s=slot: cm.lnbf[:, s * D:(s + 1) * D]),
                                  dst, wait_src=[w(cm.s_lnbf)],
                                  evac_wait=[w(s_qm)])
            for k, i in enumerate(tiles):
                for c6 in range(6):
                    bank = qb[c6 % 2]
                    for j in range(8):
                        p.add("tensor",
                              lambda e, j=j, k=k, c6=c6, bank=bank: e.matmul(
                                  bank(), lnT4[:, j * 512 + k * 128: j * 512 + k * 128 + 128],
                                  wqkv_sb[:, j * 3 * D + c6 * 512: j * 3 * D + (c6 + 1) * 512],
                                  start=(j == 0), stop=False),
                              wait=([w(cm.s_lnT), w(s_wgt), (s_qe, max(0, s_qm.n - 1))]) if j == 0 else ())
                    p.add("tensor",
                          lambda e, c6=c6, bank=bank: e.matmul(
                              bank(), cm.ones1[0:1, 0:128], bqr[:, c6 * 512:(c6 + 1) * 512],
                              start=False, stop=True),
                          inc=[(s_qm, 1)])
                    p.add("vector",
                          lambda e, c6=c6, bank=bank: e.tensor_copy(qkvbuf[:, c6 * 512:(c6 + 1) * 512], bank()),
                          wait=[w(s_qm), w(s_ch)], inc=[(s_qe, 1)])
                p.add("vector", lambda e: e.reduce_sum(st2[:, 0:1], qkvbuf[:, 0:D], axis=AX.X),
                      wait=[w(s_qe)], selfsync=True)
                p.add("vector", lambda e: e.tensor_scalar(st2[:, 1:2], st2[:, 0:1], 1.0 / D, None, op0=ALU.mult), selfsync=True)
                p.add("vector", lambda e: e.tensor_scalar(qkvbuf[:, 0:D], qkvbuf[:, 0:D], st2[:, 1:2], None, op0=ALU.subtract), selfsync=True)
                p.add("scalar", lambda e: e.activation(qkvbuf[:, 0:D], qkvbuf[:, 0:D], AF.Square,
                                                       accum_out=st2[:, 2:3]),
                      wait=[p.selfwait("vector")], inc=[(s_ch, 1)])
                p.add("vector", lambda e: e.tensor_scalar(st2[:, 3:4], st2[:, 2:3], 1e-7, None, op0=ALU.add),
                      wait=[w(s_ch)], selfsync=True)
                p.add("vector", lambda e: e.reciprocal(st2[:, 4:5], st2[:, 3:4]), selfsync=True)
                p.add("vector", lambda e: e.tensor_scalar(qkvbuf[:, 0:D], qkvbuf[:, 0:D], st2[:, 4:5], None, op0=ALU.mult), selfsync=True)
                p.add("scalar", lambda e: e.activation(cm.sq_scr[:, :], qkvbuf[:, 0:D], AF.Square,
                                                       accum_out=st2[:, 5:6]),
                      wait=[p.selfwait("vector")], inc=[(s_ch, 1)])
                p.add("scalar", lambda e: e.activation(st2[:, 6:7], st2[:, 5:6], AF.Ln), inc=[(s_ch, 1)])
                p.add("scalar", lambda e: e.activation(st2[:, 7:8], st2[:, 6:7], AF.Exp, scale=0.5), inc=[(s_ch, 1)])
                p.add("vector", lambda e: e.tensor_scalar_max(st2[:, 8:9], st2[:, 7:8], 1e-12),
                      wait=[w(s_ch)], selfsync=True)
                p.add("vector", lambda e: e.reciprocal(st2[:, 9:10], st2[:, 8:9]), selfsync=True)
                p.add("vector", lambda e: e.tensor_scalar(st2[:, 10:11], st2[:, 9:10], 1.0 / 64.0, None, op0=ALU.mult), selfsync=True)
                p.add("vector", lambda e: e.tensor_scalar(z_bf[:, :], qkvbuf[:, 0:D], st2[:, 10:11], None, op0=ALU.mult),
                      inc=[(s_ch, 1)])
                p.add("sync", lambda e, i=i: e.dma_start(q2_scr.ap()[i // 4][:, (i % 4) * D:((i % 4) + 1) * D], z_bf[:, :]),
                      wait=[w(s_ch)], inc=[(s_io, 16)])
                p.add("vector", lambda e: e.reduce_sum(st2[:, 0:1], qkvbuf[:, D:2 * D], axis=AX.X),
                      wait=[w(s_io)], selfsync=True)
                p.add("vector", lambda e: e.tensor_scalar(st2[:, 1:2], st2[:, 0:1], 1.0 / D, None, op0=ALU.mult), selfsync=True)
                p.add("vector", lambda e: e.tensor_scalar(qkvbuf[:, D:2 * D], qkvbuf[:, D:2 * D], st2[:, 1:2], None, op0=ALU.subtract), selfsync=True)
                p.add("scalar", lambda e: e.activation(qkvbuf[:, D:2 * D], qkvbuf[:, D:2 * D], AF.Square,
                                                       accum_out=st2[:, 2:3]),
                      wait=[p.selfwait("vector")], inc=[(s_ch, 1)])
                p.add("vector", lambda e: e.tensor_scalar(st2[:, 3:4], st2[:, 2:3], 1e-7, None, op0=ALU.add),
                      wait=[w(s_ch)], selfsync=True)
                p.add("vector", lambda e: e.reciprocal(st2[:, 4:5], st2[:, 3:4]), selfsync=True)
                p.add("vector", lambda e, i=i: e.tensor_scalar(k2[:, i * D:(i + 1) * D], qkvbuf[:, D:2 * D], st2[:, 4:5], None, op0=ALU.mult),
                      inc=[(s_ch, 1)])
                p.add("scalar", lambda e, i=i: e.activation(sqs[:, :], k2[:, i * D:(i + 1) * D], AF.Square),
                      wait=[w(s_ch), w(s_cs)], inc=[(s_ch, 1)])
                for half in range(2):
                    p.add("tensor",
                          lambda e, half=half: e.matmul(
                              cs_ps[half][0:1, :], onesc[:, :],
                              sqs[:, half * 512:(half + 1) * 512],
                              start=True, stop=True),
                          wait=[w(s_ch), w(s_csa2)] if half == 0 else (),
                          inc=[(s_cs, 1)] if half == 1 else ())
                for half in range(2):
                    p.add("vector",
                          lambda e, half=half: e.tensor_add(
                              colsq[0:1, half * 512:(half + 1) * 512],
                              colsq[0:1, half * 512:(half + 1) * 512],
                              cs_ps[half][0:1, :]),
                          wait=[w(s_cs)] if half == 0 else (),
                          inc=[(s_csa2, 1)] if half == 1 else ())
                p.add("vector", lambda e: e.tensor_copy(vstage[:, :], qkvbuf[:, 2 * D:3 * D]),
                      wait=[w(s_io)], inc=[(s_ch, 1)])
                p.add("sync", lambda e, i=i: e.dma_start(v_scr.ap()[i // 4][:, (i % 4) * D:((i % 4) + 1) * D], vstage[:, :]),
                      wait=[w(s_ch)], inc=[(s_io, 16)])

        p.add("scalar", lambda e: e.activation(colsq[0:1, :], colsq[0:1, :], AF.Ln),
              wait=[w(s_csa2)], inc=[(s_ch, 1)])
        p.add("scalar", lambda e: e.activation(colsq[0:1, :], colsq[0:1, :], AF.Exp, scale=0.5),
              inc=[(s_ch, 1)])
        p.add("vector", lambda e: e.tensor_scalar_max(colsq[0:1, :], colsq[0:1, :], 1e-12),
              wait=[w(s_ch)], selfsync=True)
        p.add("vector", lambda e: e.reciprocal(colsq[0:1, :], colsq[0:1, :]), selfsync=True)
        p.add("vector", lambda e: e.tensor_copy(rcnb[0:1, :], colsq[0:1, :]),
              inc=[(s_ch, 1)])
        for ct in range(8):
            p.add("tensor", lambda e, ct=ct: e.matmul(
                cs_ps[0][:, 0:1], rcnb[0:1, ct * 128:(ct + 1) * 128], cm.ones1[0:1, 0:1],
                start=True, stop=True),
                wait=[w(s_ch), w(s_csa2)] + ([w(s_cs)] if ct == 0 else []) + ([w(s_rt)] if ct > 0 else []),
                inc=[(s_rt, 1)])
            p.add("vector", lambda e, ct=ct: e.tensor_copy(rcn[:, ct:ct + 1], cs_ps[0][:, 0:1]),
                  wait=[w(s_rt)], inc=[(s_rt, 1)])

        for chalf in range(4):
            for i in range(NT):
                p.add("sync", lambda e, i=i: e.dma_start(vstage[:, :], v_scr.ap()[i // 4][:, (i % 4) * D:((i % 4) + 1) * D]),
                      wait=[w(s_kvm), w(s_io), w(s_ch)], inc=[(s_vld, 16)])
                for cc in range(2):
                    ct = chalf * 2 + cc
                    for dh in range(2):
                        p.add("tensor",
                              lambda e, i=i, ct=ct, cc=cc, dh=dh, first=(i == 0), last=(i == NT - 1): e.matmul(
                                  kv_ps[:, cc * 1024 + dh * 512: cc * 1024 + (dh + 1) * 512],
                                  k2[:, i * D + ct * 128: i * D + ct * 128 + 128],
                                  vstage[:, dh * 512:(dh + 1) * 512],
                                  start=first, stop=last),
                              wait=([w(s_vld), w(s_qe), w(s_rt), w(s_kve)]) if (cc == 0 and dh == 0) else (),
                              inc=[(s_kvm, 1)] if (cc == 1 and dh == 1) else ())
            for cc in range(2):
                ct = chalf * 2 + cc
                p.add("vector", lambda e, ct=ct, cc=cc: e.tensor_scalar(
                    kv_bf[:, ct * D:(ct + 1) * D], kv_ps[:, cc * 1024:(cc + 1) * 1024],
                    rcn[:, ct:ct + 1], None, op0=ALU.mult),
                    wait=[w(s_kvm), w(s_wgt)], inc=[(s_kve, 1)])

        if debug:
            s_dbg3 = p.sem("dbg3")
            p.add("gpsimd", lambda e: e.dma_start(dbg_rcn.ap()[:, :], rcn[:, :]),
                  wait=[w(s_rt)], inc=[(s_dbg3, 16)])
            p.add("gpsimd", lambda e: e.dma_start(dbg_cq.ap()[:, :], colsq[0:1, :]),
                  wait=[w(s_rt)], inc=[(s_dbg3, 16)])
            p.add("gpsimd", lambda e: e.dma_start(dbg_kv.ap()[:, :], kv_bf[:, :]),
                  wait=[w(s_kve)], inc=[(s_dbg3, 16)])
            p.add("gpsimd", lambda e: e.dma_start(dbg_k2.ap()[:, :], k2[:, 0:4 * D]),
                  wait=[w(s_kve)], inc=[(s_dbg3, 16)])
            p.add("sync", lambda e: e.wait_ge(s_dbg3.h, s_dbg3.n))
        p.add("gpsimd", lambda e: e.dma_start(
            wln_sb[:, :].rearrange("p (j c) -> p j c", j=8),
            wln.ap().rearrange("(j p) c -> p j c", p=128)),
            wait=[w(s_qm)], inc=[(s_wgt, 16)])

        for c4 in range(8):
            p.add("sync", lambda e, c4=c4: e.dma_start(chunkb[:, 0:4096], v_scr.ap()[c4]),
                  wait=[w(s_zb), w(s_kvm)], inc=[(s_chld, 16)])
            p.add("sync", lambda e, c4=c4: e.dma_start(chunkb[:, 4096:8192], q2_scr.ap()[c4]),
                  wait=[w(s_t2m)], inc=[(s_chld, 16)])
            for k in range(4):
                for j in range(8):
                    p.add("tensor",
                          lambda e, j=j, k=k: e.transpose(
                              cm.tr_ps[:, j * 128:(j + 1) * 128],
                              chunkb[:, 4096 + k * D + j * 128: 4096 + k * D + (j + 1) * 128],
                              cm.ident[:, :]),
                          wait=([w(s_chld), w(cm.s_lnT)]) if j == 0 else (),
                          inc=[(cm.s_tr, 1)] if j == 7 else ())
                p.add("vector", lambda e: e.tensor_copy(q2T[:, :], cm.tr_ps[:, :]),
                      wait=[w(cm.s_tr), w(s_t2m)], inc=[(cm.s_lnT, 1)])
                for j in range(8):
                    for dh in range(2):
                        p.add("tensor",
                              lambda e, j=j, dh=dh: e.matmul(
                                  kv_ps[:, dh * 512:(dh + 1) * 512],
                                  q2T[:, j * 128:(j + 1) * 128],
                                  kv_bf[:, j * D + dh * 512: j * D + (dh + 1) * 512],
                                  start=(j == 0), stop=(j == 7)),
                              wait=([w(cm.s_lnT), w(s_kve), w(s_zb)]) if (j == 0 and dh == 0) else (),
                              inc=[(s_t2m, 1)] if (j == 7 and dh == 1) else ())
                p.add("vector", lambda e, k=k: e.tensor_add(z_bf[:, :], chunkb[:, k * D:(k + 1) * D], kv_ps[:, 0:1024]),
                      wait=[w(s_t2m)], inc=[(s_zb, 1)])
                for j in range(8):
                    p.add("tensor",
                          lambda e, j=j: e.transpose(
                              cm.tr_ps[:, j * 128:(j + 1) * 128],
                              z_bf[:, j * 128:(j + 1) * 128],
                              cm.ident[:, :]),
                          wait=([w(s_zb), w(cm.s_lnT)]) if j == 0 else (),
                          inc=[(cm.s_tr, 1)] if j == 7 else ())
                p.add("vector", lambda e: e.tensor_copy(zT[:, :], cm.tr_ps[:, :]),
                      wait=[w(cm.s_tr), w(s_lnm)], inc=[(cm.s_lnT, 1)])
                for j in range(8):
                    for dh in range(2):
                        p.add("tensor",
                              lambda e, j=j, dh=dh: e.matmul(
                                  kv_ps[:, 1024 + dh * 512: 1024 + (dh + 1) * 512],
                                  zT[:, j * 128:(j + 1) * 128],
                                  wln_sb[:, j * D + dh * 512: j * D + (dh + 1) * 512],
                                  start=(j == 0), stop=False),
                              wait=([w(cm.s_lnT), w(s_wgt), w(s_res)]) if (j == 0 and dh == 0) else ())
                for dh in range(2):
                    p.add("tensor",
                          lambda e, dh=dh: e.matmul(
                              kv_ps[:, 1024 + dh * 512: 1024 + (dh + 1) * 512],
                              cm.ones1[0:1, 0:128], blr[:, dh * 512:(dh + 1) * 512],
                              start=False, stop=True),
                          inc=[(s_lnm, 1)] if dh == 1 else ())
                i = c4 * 4 + k
                p.add("sync", lambda e, i=i: e.dma_start(x1[:, :], xr[:, i, :]),
                      wait=[w(s_res), w(s_xo)], inc=[(s_xc, 16)])
                p.add("vector", lambda e: e.tensor_add(x1[:, :], x1[:, :], kv_ps[:, 1024:2048]),
                      wait=[w(s_lnm), w(s_xc)], inc=[(s_res, 1)])
                p.add("sync", lambda e, i=i: e.dma_start(xwr[:, i, :], x1[:, :]),
                      wait=[w(s_res)], inc=[(s_xo, 16)])

        p.add("sync", lambda e: e.wait_ge(s_xo.h, s_xo.n))
        cm.finalize_consts()
        p.emit()
    return nc


def l3_attn_inmap(inputs, b, x):
    return {
        "x_in": np.ascontiguousarray(x[b]),
        "wqkv": hC(inputs["L3_attn_Wqkv"]),
        "bq_row": np.asarray(inputs["L3_attn_bqkv"], np.float32).reshape(1, 3 * 1024).copy(),
        "wln": hC(inputs["L3_attn_Wln"]),
        "bl_row": np.asarray(inputs["L3_attn_bln"], np.float32).reshape(1, 1024).copy(),
    }





def build_L2_ffn(debug=False):
    nc = bass.Bass("TRN2", target_bir_lowering=False)
    x_in = nc.dram_tensor("x_in", [T, D], F32, kind="ExternalInput")
    x_out = nc.dram_tensor("x_out", [T, D], F32, kind="ExternalOutput")
    wg = nc.dram_tensor("wg", [2816, D], F32, kind="ExternalInput")
    wu = nc.dram_tensor("wu", [2816, D], F32, kind="ExternalInput")
    wd = nc.dram_tensor("wd", [D, 2816], F32, kind="ExternalInput")
    I2816 = 2816

    with contextlib.ExitStack() as stack:
        p = Prog(nc, stack)
        cm = Common(p, x_cols=D, lnbf_cols=D)
        sb = lambda name, shape, dt=BF16: stack.enter_context(nc.sbuf_tensor(name, shape, dt))
        ps = lambda name, shape, dt=F32: stack.enter_context(nc.psum_tensor(name, shape, dt))

        wgT = sb("wgT", [128, 8 * I2816])
        wuT = sb("wuT", [128, 8 * I2816])
        wdT = sb("wdT", [128, 22 * D])
        rmax = sb("rmax", [128, 176], F32)
        sbuf_s = sb("sbuf_s", [128, 176], F32)
        s16 = sb("s16", [128, 176])
        xn = sb("xn", [128, I2816], F32)
        ci = sb("ci", [128, I2816], I32)
        c1f = sb("c1f", [128, I2816])
        c2f = sb("c2f", [128, I2816])
        gA = sb("gA", [128, I2816])
        gB = sb("gB", [128, I2816])
        ar = Arena(nc, stack, 11 * 1024 + 256, name="arena_l2f")
        wstage = ar.at(0, "wstage", [128, I2816], F32)
        lnf = ar.at(0, "lnf", [128, D], F32)
        xqT = ar.at(4, "xqT", [128, D], BF16)
        h32 = ar.at(6, "h32", [128, 512], F32)
        sg = ar.at(8, "sg", [128, 512], F32)
        hqT = ar.at(10, "hqT", [128, 512], BF16)
        qout = sb("qout", [128, I2816])

        gate_ps = ps("gate_ps", [128, 512])
        up_ps = ps("up_ps", [128, 512])
        out_ps = ps("out_ps", [128, 1024])

        if debug:
            dbg_xq = nc.dram_tensor("dbg_xq", [128, D], F32, kind="ExternalOutput")
            dbg_hq = nc.dram_tensor("dbg_hq", [128, 512], F32, kind="ExternalOutput")
            dbg_h = nc.dram_tensor("dbg_h", [128, 512], F32, kind="ExternalOutput")
            s_dbg = p.sem("dbgq")
        s_dma = p.sem("dma")
        s_wld = p.sem("wld")
        s_qv = p.sem("qv")
        s_qs = p.sem("qs")
        s_tr2 = p.sem("tr2")
        s_te = p.sem("te")
        s_gm = p.sem("gm")
        s_sil = p.sem("sil")
        s_dm = p.sem("dm")
        s_res = p.sem("res")
        s_xc = p.sem("xc")
        s_xo = p.sem("xo")

        cm.load_consts(s_dma)
        for _ge in ("tensor", "vector", "scalar"):
            p.add(_ge, None, wait=[w(s_dma)])
        xr = x_in.ap().rearrange("(i p) d -> p i d", p=128)
        xwr = x_out.ap().rearrange("(i p) d -> p i d", p=128)

        def emit_quant(src_fn, dst_fn, W, waits):
            nb = W // 16
            v3 = lambda a, n: a.rearrange("p (g s) -> p g s", g=n)
            p.add("vector", lambda e: e.reduce_max(
                v3(rmax[:, 0:nb], nb), v3(src_fn(), nb), axis=AX.X,
                apply_absolute_value=True),
                wait=list(waits) + [w(s_qs), w(s_tr2)], selfsync=True)
            p.add("vector", lambda e: e.tensor_scalar(
                sbuf_s[:, 0:nb], rmax[:, 0:nb], 1.0 / 6.0, 1e-12, op0=ALU.mult, op1=ALU.max),
                selfsync=True)
            p.add("vector", lambda e: e.reciprocal(rmax[:, 0:nb], sbuf_s[:, 0:nb]),
                  selfsync=True)
            rs0 = lambda: bass.AP(rmax.ap().tensor, 0, [[176, 128], [1, nb], [0, 16]])
            p.add("vector", lambda e: e.tensor_tensor(
                v3(xn[:, 0:W], nb), v3(src_fn(), nb).rearrange("p g s -> p g s"),
                rs0(), op=ALU.mult),
                inc=[(s_qv, 1)])
            p.add("scalar", lambda e: e.activation(ci[:, 0:W], xn[:, 0:W], AF.Copy, scale=2.0),
                  wait=[w(s_qv)])
            p.add("scalar", lambda e: e.activation(c1f[:, 0:W], ci[:, 0:W], AF.Copy))
            p.add("scalar", lambda e: e.activation(ci[:, 0:W], xn[:, 0:W], AF.Copy))
            p.add("scalar", lambda e: e.activation(c2f[:, 0:W], ci[:, 0:W], AF.Copy),
                  inc=[(s_qs, 1)])
            p.add("vector", lambda e: e.tensor_scalar(gA[:, 0:W], c1f[:, 0:W], -4.0, 4.0, op0=ALU.max, op1=ALU.min),
                  wait=[w(s_qs)])
            p.add("vector", lambda e: e.tensor_scalar(gB[:, 0:W], c2f[:, 0:W], 2.0, 4.0, op0=ALU.max, op1=ALU.min))
            p.add("vector", lambda e: e.scalar_tensor_tensor(gA[:, 0:W], gA[:, 0:W], 0.5, gB[:, 0:W], op0=ALU.mult, op1=ALU.add))
            p.add("vector", lambda e: e.tensor_scalar(gB[:, 0:W], c2f[:, 0:W], -4.0, -2.0, op0=ALU.max, op1=ALU.min))
            p.add("vector", lambda e: e.tensor_add(gA[:, 0:W], gA[:, 0:W], gB[:, 0:W]))
            p.add("vector", lambda e: e.tensor_single_scalar(gB[:, 0:W], xn[:, 0:W], 5.0, op=ALU.is_gt))
            p.add("vector", lambda e: e.scalar_tensor_tensor(gA[:, 0:W], gB[:, 0:W], 2.0, gA[:, 0:W], op0=ALU.mult, op1=ALU.add))
            p.add("vector", lambda e: e.tensor_single_scalar(gB[:, 0:W], xn[:, 0:W], -5.0, op=ALU.is_lt))
            p.add("vector", lambda e: e.scalar_tensor_tensor(gA[:, 0:W], gB[:, 0:W], -2.0, gA[:, 0:W], op0=ALU.mult, op1=ALU.add))
            p.add("vector", lambda e: e.tensor_copy(s16[:, 0:nb], sbuf_s[:, 0:nb]), selfsync=True)
            s0 = lambda: bass.AP(s16.ap().tensor, 0, [[176, 128], [1, nb], [0, 16]])
            p.add("vector", lambda e: e.tensor_tensor(
                v3(dst_fn(), nb), v3(gA[:, 0:W], nb), s0(), op=ALU.mult),
                inc=[(s_qv, 1)])

        for wi, (wdr, ntile, wT, wTcols, tpr) in enumerate((
                (wg, 22, wgT, I2816, 8), (wu, 22, wuT, I2816, 8), (wd, 8, wdT, D, 22))):
            Wq = 1024 if tpr == 8 else I2816
            for r in range(ntile):
                p.add("sync", lambda e, wdr=wdr, r=r, Wq=Wq: e.dma_start(
                    wstage[:, 0:Wq], wdr.ap()[r * 128:(r + 1) * 128, :]),
                    wait=[w(s_qv)], inc=[(s_wld, 16)])
                emit_quant(lambda Wq=Wq: wstage[:, 0:Wq], lambda Wq=Wq: qout[:, 0:Wq], Wq,
                           [w(s_wld)])
                nrounds = (tpr + 7) // 8
                for rd in range(nrounds):
                    j0 = rd * 8
                    jn = min(8, tpr - j0)
                    for j in range(jn):
                        p.add("tensor",
                              lambda e, j=j, j0=j0: e.transpose(
                                  cm.tr_ps[:, j * 128:(j + 1) * 128],
                                  qout[:, (j0 + j) * 128:(j0 + j + 1) * 128],
                                  cm.ident[:, :]),
                              wait=([w(s_qv), w(s_te)]) if j == 0 else (),
                              inc=[(s_tr2, 1)] if j == jn - 1 else ())
                    def evt(e, r=r, j0=j0, jn=jn, wT=wT, wTcols=wTcols):
                        dstv = bass.AP(wT.ap().tensor, j0 * wTcols + r * 128,
                                       [[wT.shape[1], 128], [wTcols, jn], [1, 128]])
                        return e.tensor_copy(dstv, cm.tr_ps[:, 0:jn * 128].rearrange("p (j t) -> p j t", j=jn))
                    p.add("vector", lambda e, f=evt: f(e),
                          wait=[w(s_tr2)], inc=[(s_te, 1)])

        CHK = [(0, 512), (512, 512), (1024, 512), (1536, 512), (2048, 512), (2560, 256)]
        for i in range(NT):
            p.add("sync", lambda e, i=i: e.dma_start(cm.x_sb[:, :], xr[:, i, :]),
                  wait=[w(s_res), w(cm.s_lnbf), w(s_xo)], inc=[(s_xc, 16)])
            cm.ln_tile(i, (lambda: lnf[:, :]), deps=[w(s_xc)], src_col=0,
                       out_wait=[w(s_qv)])
            emit_quant(lambda: lnf[:, :], lambda: qout[:, 0:D], D, [w(cm.s_lnbf)])
            if debug and i == 3:
                p.add("gpsimd", lambda e: e.dma_start(dbg_xq.ap()[:, :], qout[:, 0:D]),
                      wait=[w(s_qv)], inc=[(s_dbg, 16)])
            for j in range(8):
                p.add("tensor",
                      lambda e, j=j: e.transpose(
                          cm.tr_ps[:, j * 128:(j + 1) * 128],
                          qout[:, j * 128:(j + 1) * 128], cm.ident[:, :]),
                      wait=([w(s_qv), w(s_te)]) if j == 0 else (),
                      inc=[(s_tr2, 1)] if j == 7 else ())
            p.add("vector", lambda e: e.tensor_copy(xqT[:, :], cm.tr_ps[:, 0:1024]),
                  wait=[w(s_tr2), w(s_gm)], inc=[(s_te, 1)])
            for ci_, (k0, kw) in enumerate(CHK):
                for which, ps_, wT in ((0, gate_ps, wgT), (1, up_ps, wuT)):
                    for j in range(8):
                        p.add("tensor",
                              lambda e, j=j, k0=k0, kw=kw, ps_=ps_, wT=wT: e.matmul(
                                  ps_[:, 0:kw], xqT[:, j * 128:(j + 1) * 128],
                                  wT[:, j * I2816 + k0: j * I2816 + k0 + kw],
                                  start=(j == 0), stop=(j == 7)),
                              wait=([w(s_te), w(s_wld), w(s_sil)]) if j == 0 else (),
                              inc=[(s_gm, 1)] if j == 7 else ())
                p.add("scalar", lambda e, kw=kw: e.activation(sg[:, 0:kw], gate_ps[:, 0:kw], AF.Silu),
                      wait=[w(s_gm)], inc=[(s_qs, 1)])
                p.add("vector", lambda e, kw=kw: e.tensor_mul(h32[:, 0:kw], sg[:, 0:kw], up_ps[:, 0:kw]),
                      wait=[w(s_qs)], inc=[(s_sil, 1)])
                if debug and i == 3 and ci_ == 0:
                    p.add("gpsimd", lambda e: e.dma_start(dbg_h.ap()[:, :], h32[:, 0:512]),
                          wait=[w(s_sil)], inc=[(s_dbg, 16)])
                emit_quant(lambda kw=kw: h32[:, 0:kw], lambda kw=kw: qout[:, 0:kw], kw,
                           [w(s_sil)])
                if debug and i == 3 and ci_ == 0:
                    p.add("gpsimd", lambda e: e.dma_start(dbg_hq.ap()[:, :], qout[:, 0:512]),
                          wait=[w(s_qv)], inc=[(s_dbg, 16)])
                nkt = kw // 128
                for j in range(nkt):
                    p.add("tensor",
                          lambda e, j=j: e.transpose(
                              cm.tr_ps[:, j * 128:(j + 1) * 128],
                              qout[:, j * 128:(j + 1) * 128], cm.ident[:, :]),
                          wait=([w(s_qv), w(s_te)]) if j == 0 else (),
                          inc=[(s_tr2, 1)] if j == nkt - 1 else ())
                p.add("vector", lambda e, nkt=nkt: e.tensor_copy(hqT[:, 0:nkt * 128], cm.tr_ps[:, 0:nkt * 128]),
                      wait=[w(s_tr2), w(s_dm)], inc=[(s_te, 1)])
                for j in range(nkt):
                    kt = k0 // 128 + j
                    for dh in range(2):
                        p.add("tensor",
                              lambda e, j=j, kt=kt, dh=dh, st=(ci_ == 0 and j == 0),
                                     sp=(ci_ == len(CHK) - 1 and j == nkt - 1): e.matmul(
                                  out_ps[:, dh * 512:(dh + 1) * 512],
                                  hqT[:, j * 128:(j + 1) * 128],
                                  wdT[:, kt * D + dh * 512: kt * D + (dh + 1) * 512],
                                  start=st, stop=sp),
                              wait=([w(s_te), w(s_res)]) if (j == 0 and dh == 0) else (),
                              inc=[(s_dm, 1)] if (j == nkt - 1 and dh == 1) else ())
            p.add("vector", lambda e: e.tensor_add(cm.x_sb[:, :], cm.x_sb[:, :], out_ps[:, :]),
                  wait=[w(s_dm), w(s_xc)], inc=[(s_res, 1)])
            p.add("sync", lambda e, i=i: e.dma_start(xwr[:, i, :], cm.x_sb[:, :]),
                  wait=[w(s_res)], inc=[(s_xo, 16)])

        p.add("sync", lambda e: e.wait_ge(s_xo.h, s_xo.n))
        if debug:
            p.add("sync", lambda e: e.wait_ge(s_dbg.h, s_dbg.n))
        cm.finalize_consts()
        p.emit()
    return nc


def l2_ffn_inmap(inputs, b, x):
    return {
        "x_in": np.ascontiguousarray(x[b]),
        "wg": hC(inputs["L2_ffn_w_gate"]),
        "wu": hC(inputs["L2_ffn_w_up"]),
        "wd": hC(inputs["L2_ffn_w_down"]),
    }





def build_L0_attn(debug=False):
    nc = bass.Bass("TRN2", target_bir_lowering=False)
    x_in = nc.dram_tensor("x_in", [T, D], F32, kind="ExternalInput")
    x_out = nc.dram_tensor("x_out", [T, D], F32, kind="ExternalOutput")
    wqkvT = nc.dram_tensor("wqkvT", [D, 3 * D], F32, kind="ExternalInput")
    wgbT = nc.dram_tensor("wgbT", [D, 64], F32, kind="ExternalInput")
    bgb = nc.dram_tensor("bgb", [1, 64], F32, kind="ExternalInput")
    woutT = nc.dram_tensor("woutT", [D, D], F32, kind="ExternalInput")
    dk = "ExternalOutput" if debug else "Internal"
    qT_scr = nc.dram_tensor("qT_scr", [128, 8 * 4096], BF16, kind=dk)
    kT_scr = nc.dram_tensor("kT_scr", [128, 8 * 4096], BF16, kind=dk)
    kn_scr = nc.dram_tensor("kn_scr", [NT, 128, D], BF16, kind=dk)
    vn_scr = nc.dram_tensor("vn_scr", [NT, 128, D], BF16, kind=dk)
    oT_scr = nc.dram_tensor("oT_scr", [8, 128, 4096], BF16, kind=dk)
    if debug:
        gdbg = nc.dram_tensor("gdbg", [128, 4 * NT * 16], F32, kind="ExternalOutput")

    H = 16
    SQK = 6

    r_ = np.arange(128)
    UT = (r_[:, None] <= r_[None, :]).astype(bfdt)
    LTs = (r_[:, None] > r_[None, :]).astype(bfdt)
    Bext = np.zeros((128, 130), bfdt)
    Bext[:, 0] = 1.0
    Bext[:, 1:129] = (r_[:, None] > r_[None, :]).astype(bfdt)
    maskc = np.zeros((128, 130), np.float32)
    maskc[:, 1:129] = np.where(r_[None, :] <= r_[:, None], 0.0, -1e30)
    onemI = (1.0 - np.eye(128)).astype(bfdt)

    with contextlib.ExitStack() as stack:
        p = Prog(nc, stack)
        cm = Common(p)
        cm.cb(1.0)
        sb = lambda name, shape, dt=BF16: stack.enter_context(nc.sbuf_tensor(name, shape, dt))
        ps = lambda name, shape, dt=F32: stack.enter_context(nc.psum_tensor(name, shape, dt))

        ar = Arena(nc, stack, 48 * 1024)
        wq_sb = ar.at(0, "wq_sb", [128, 8 * 3 * D], BF16)
        woutT_sb = ar.at(0, "woutT_sb", [128, 8 * D], BF16)
        lnT4 = sb("lnT4", [128, 4096])
        qkvbuf = sb("qkvbuf", [128, 3 * D], F32)
        wgb_sb = sb("wgb_sb", [128, 8 * 64])
        bgb_sb = sb("bgb_sb", [1, 64])
        lgs = sb("lgs", [128, 64], F32)
        ab_sc = sb("ab_sc", [128, 80], F32)
        la_all = sb("la_all", [128, NT * H], F32)
        bt_all = sb("bt_all", [128, NT * H], F32)
        nbt_all = sb("nbt_all", [128, NT * H], F32)
        lah = sb("lah", [128, NT * H], F32)
        lal = sb("lal", [128, NT * H], F32)
        bscr = sb("bscr", [128, 16])
        nrm = sb("nrm", [128, 64], F32)
        qn_bf = sb("qn_bf", [128, D])
        kn_buf = sb("kn_buf", [128, D])
        vn_buf = sb("vn_buf", [128, D])
        qTst = sb("qTst", [128, 8 * 128])
        qTh = sb("qTh", [128, 4096])
        kTh = sb("kTh", [128, 4096])
        kv_sb = sb("kv_sb", [128, 256])
        utc = sb("utc", [128, 128])
        lts = sb("lts", [128, 128])
        bxc = sb("bxc", [128, 130])
        mkc = sb("mkc", [128, 130], F32)
        om1 = sb("om1", [128, 128])
        onescol = sb("onescol", [128, 1])
        A_sb = sb("A_sb", [128, 256])
        A2_sb = sb("A2_sb", [128, 256])
        Mm_sb = sb("Mm_sb", [128, 130], F32)
        Dg = sb("Dg", [128, 130])
        Dl = sb("Dl", [128, 128])
        DT = sb("DT", [128, 128])
        At_sb = sb("At_sb", [128, 128])
        Pn = [sb(f"Pn{k}", [128, 128]) for k in range(SQK + 1)]
        Pt = [sb(f"Pt{k}", [128, 128]) for k in range(SQK)]
        Rt_sb = sb("Rt_sb", [128, 128])
        bV_sb = sb("bV_sb", [128, 64])
        W2in = sb("W2in", [128, 64])
        Wv_sb = sb("Wv_sb", [128, 64])
        U1_sb = sb("U1_sb", [128, 64])
        W2n_sb = sb("W2n_sb", [128, 64])
        U2n_sb = sb("U2n_sb", [128, 64])
        Zn_sb = sb("Zn_sb", [128, 64])
        AW2nt = sb("AW2nt", [128, 128])
        O2c = sb("O2c", [128, 64])
        Obuf = sb("Obuf", [128, 64])
        St32 = sb("St32", [128, 64], F32)
        St_bf = sb("St_bf", [128, 64])
        scol = sb("scol", [128, 8], F32)
        gf = sb("gf", [128, 4], F32)
        gneg = sb("gneg", [128, 1], F32)
        oTh = sb("oTh", [128, 4096])
        oTW = sb("oTW", [128, 8 * 128])
        x1 = sb("x1", [128, D], F32)

        ps_q = ps("ps_q", [128, 512])
        ps_q2 = ps("ps_q2", [128, 512])
        ps_m = ps("ps_m", [128, 132])
        ps_e = ps("ps_e", [128, 256])
        ps_sq = ps("ps_sq", [128, 256])
        ps_w = ps("ps_w", [128, 320])
        ps_o = ps("ps_o", [128, 128])

        utd = nc.inline_tensor(UT, name="utc0")
        ltd = nc.inline_tensor(LTs, name="lts0")
        bxd = nc.inline_tensor(Bext, name="bxc0")
        mkd = nc.inline_tensor(maskc, name="mkc0")
        omd = nc.inline_tensor(onemI, name="om10")

        s_dma = p.sem("dma")
        s_wgt = p.sem("wgt")
        s_qm = p.sem("qm")
        s_qe = p.sem("qe")
        s_ab = p.sem("ab")
        s_nv = p.sem("nv")
        s_nr = p.sem("nr")
        s_io = p.sem("io")
        s_knv = p.sem("knv")
        s_xc = p.sem("xc")
        s_hl = p.sem("hl")
        s_kv = p.sem("kv")
        s_tm = p.sem("tm")
        s_ve = p.sem("ve")
        s_se = p.sem("se")
        s_oio = p.sem("oio")
        s_wm = p.sem("wm")
        s_res = p.sem("res")
        s_xo = p.sem("xo")

        cm.load_consts(s_dma)
        for dsts, srcs in ((utc, utd), (lts, ltd), (bxc, bxd), (mkc, mkd), (om1, omd)):
            p.add("sync", lambda e, a=dsts, b=srcs: e.dma_start(a[:, :], b.ap()[:, :]), inc=[(s_dma, 16)])
        for _ge in ("tensor", "vector", "scalar"):
            p.add(_ge, None, wait=[w(s_dma)])
        p.add("gpsimd", lambda e: e.dma_start(
            wq_sb[:, :].rearrange("p (j c) -> p j c", j=8),
            wqkvT.ap().rearrange("(j p) c -> p j c", p=128)), inc=[(s_wgt, 16)])
        p.add("gpsimd", lambda e: e.dma_start(
            wgb_sb[:, :].rearrange("p (j c) -> p j c", j=8),
            wgbT.ap().rearrange("(j p) c -> p j c", p=128)), inc=[(s_wgt, 16)])
        p.add("gpsimd", lambda e: e.dma_start(bgb_sb[:, :], bgb.ap()[:, :]), inc=[(s_wgt, 16)])
        p.add("vector", lambda e: e.memset(onescol[:, :], 1.0))

        xr = x_in.ap().rearrange("(i p) d -> p i d", p=128)
        xwr = x_out.ap().rearrange("(i p) d -> p i d", p=128)

        for c4 in range(8):
            tiles = [c4 * 4 + kk for kk in range(4)]
            p.add("sync",
                  lambda e, c4=c4: e.dma_start(
                      cm.x_sb[:, :].rearrange("p (k d) -> p k d", k=4),
                      xr[:, c4 * 4:(c4 + 1) * 4, :]),
                  wait=[w(cm.s_lnbf)], inc=[(s_xc, 16)])
            xdep = [w(s_xc), w(s_nr)]
            for kk, i in enumerate(tiles):
                slot = i % 2
                cm.ln_tile(i, (lambda s=slot: cm.lnbf[:, s * D:(s + 1) * D]), deps=xdep,
                           src_col=kk * D)
                dst = lambda kk=kk: lnT4[:, :].rearrange("p (j t) -> p j t", j=8)[:, :, kk * 128:(kk + 1) * 128]
                cm.transpose_tile((lambda s=slot: cm.lnbf[:, s * D:(s + 1) * D]),
                                  dst, wait_src=[w(cm.s_lnbf)],
                                  evac_wait=[w(s_qm)])
            for kk, i in enumerate(tiles):
                for c6 in range(6):
                    bank = (ps_q if c6 % 2 == 0 else ps_q2)
                    for j in range(8):
                        p.add("tensor",
                              lambda e, j=j, kk=kk, c6=c6, bank=bank: e.matmul(
                                  bank[:, :], lnT4[:, j * 512 + kk * 128: j * 512 + kk * 128 + 128],
                                  wq_sb[:, j * 3 * D + c6 * 512: j * 3 * D + (c6 + 1) * 512],
                                  start=(j == 0), stop=(j == 7)),
                              wait=([w(cm.s_lnT), w(s_wgt), (s_qe, max(0, s_qe.n - 1))]) if j == 0 else (),
                              inc=[(s_qm, 1)] if j == 7 else ())
                    p.add("vector",
                          lambda e, c6=c6, bank=bank: e.tensor_copy(qkvbuf[:, c6 * 512:(c6 + 1) * 512], bank[:, :]),
                          wait=[w(s_qm), w(s_nv), w(s_ab)], inc=[(s_qe, 1)])
                for j in range(8):
                    p.add("tensor",
                          lambda e, j=j, kk=kk: e.matmul(
                              ps_m[:, 0:64], lnT4[:, j * 512 + kk * 128: j * 512 + kk * 128 + 128],
                              wgb_sb[:, j * 64:(j + 1) * 64],
                              start=(j == 0), stop=False),
                          wait=([(s_qe, max(0, s_qe.n - 1))]) if j == 0 else ())
                p.add("tensor",
                      lambda e: e.matmul(ps_m[:, 0:64], cm.ones1[0:1, 0:128], bgb_sb[0:1, :],
                                         start=False, stop=True),
                      inc=[(s_qm, 1)])
                p.add("vector", lambda e: e.tensor_copy(lgs[:, :], ps_m[:, 0:64]),
                      wait=[w(s_qm), w(s_ab)], inc=[(s_qe, 1)])
                p.add("scalar", lambda e: e.activation(ab_sc[:, 0:16], lgs[:, 32:48], AF.Exp, scale=-1.0),
                      wait=[w(s_qe), w(s_nv)], inc=[(s_ab, 1)])
                p.add("vector", lambda e: e.tensor_scalar(ab_sc[:, 16:32], ab_sc[:, 0:16], 1.0, None, op0=ALU.add),
                      wait=[w(s_ab)], selfsync=True)
                p.add("vector", lambda e, i=i: e.reciprocal(bt_all[:, i * H:(i + 1) * H], ab_sc[:, 16:32]),
                      selfsync=True)
                p.add("vector", lambda e, i=i: e.tensor_scalar(nbt_all[:, i * H:(i + 1) * H], bt_all[:, i * H:(i + 1) * H], -1.0, None, op0=ALU.mult),
                      inc=[(s_nv, 1)])
                p.add("scalar", lambda e: e.activation(ab_sc[:, 32:48], lgs[:, 0:16], AF.Abs),
                      wait=[w(s_nv)], inc=[(s_ab, 1)])
                p.add("scalar", lambda e: e.activation(ab_sc[:, 48:64], ab_sc[:, 32:48], AF.Exp, scale=-1.0),
                      inc=[(s_ab, 1)])
                p.add("scalar", lambda e: e.activation(ab_sc[:, 64:80], ab_sc[:, 48:64], AF.Ln, bias=cm.cb(1.0)),
                      inc=[(s_ab, 1)])
                p.add("vector", lambda e: e.tensor_scalar_min(ab_sc[:, 32:48], lgs[:, 0:16], 0.0),
                      wait=[w(s_ab)], selfsync=True)
                p.add("vector", lambda e, i=i: e.tensor_sub(la_all[:, i * H:(i + 1) * H], ab_sc[:, 32:48], ab_sc[:, 64:80]),
                      selfsync=True)
                p.add("vector", lambda e, i=i: e.tensor_copy(bscr[:, :], la_all[:, i * H:(i + 1) * H]),
                      selfsync=True)
                p.add("vector", lambda e, i=i: e.tensor_copy(lah[:, i * H:(i + 1) * H], bscr[:, :]),
                      selfsync=True)
                p.add("vector", lambda e, i=i: e.tensor_sub(lal[:, i * H:(i + 1) * H], la_all[:, i * H:(i + 1) * H], lah[:, i * H:(i + 1) * H]),
                      inc=[(s_nv, 1)])
                for which, col0 in ((0, 0), (1, D)):
                    p.add("scalar", lambda e, col0=col0: e.activation(
                        cm.sq_scr[:, :], qkvbuf[:, col0:col0 + D], AF.Square),
                        wait=[w(s_qe), w(s_nr)], inc=[(s_ab, 1)])
                    p.add("vector", lambda e: e.reduce_sum(
                        nrm[:, 0:16].rearrange("p (h o) -> p h o", h=16),
                        cm.sq_scr[:, :].rearrange("p (h d) -> p h d", h=16), axis=AX.X),
                        wait=[w(s_ab)], inc=[(s_nr, 1)])
                    p.add("scalar", lambda e: e.activation(nrm[:, 16:32], nrm[:, 0:16], AF.Ln),
                          wait=[w(s_nr)], inc=[(s_ab, 1)])
                    p.add("scalar", lambda e: e.activation(nrm[:, 32:48], nrm[:, 16:32], AF.Exp, scale=0.5),
                          inc=[(s_ab, 1)])
                    p.add("vector", lambda e: e.tensor_scalar_max(nrm[:, 32:48], nrm[:, 32:48], 1e-12),
                          wait=[w(s_ab)], selfsync=True)
                    p.add("vector", lambda e: e.reciprocal(nrm[:, 48:64], nrm[:, 32:48]),
                          selfsync=True)
                    rv = lambda: bass.AP(nrm.ap().tensor, 48, [[64, 128], [1, 16], [0, 64]])
                    dbuf = (lambda: qn_bf[:, :]) if which == 0 else (lambda: kn_buf[:, :])
                    p.add("vector", lambda e, col0=col0, rv=rv, dbuf=dbuf: e.tensor_tensor(
                        dbuf()[:, :].rearrange("p (h d) -> p h d", h=16),
                        qkvbuf[:, col0:col0 + D].rearrange("p (h d) -> p h d", h=16),
                        rv(), op=ALU.mult),
                        wait=[w(cm.s_tr), w(s_knv), w(s_qe)], inc=[(s_nv, 1)])
                    if which == 1:
                        p.add("sync", lambda e, i=i: e.dma_start(kn_scr.ap()[i][:, :], kn_buf[:, :]),
                              wait=[w(s_nv)], inc=[(s_knv, 16)])
                    for j in range(8):
                        p.add("tensor",
                              lambda e, j=j, dbuf=dbuf: e.transpose(
                                  cm.tr_ps[:, j * 128:(j + 1) * 128],
                                  dbuf()[:, j * 128:(j + 1) * 128], cm.ident[:, :]),
                              wait=([w(s_nv), w(cm.s_lnT)]) if j == 0 else (),
                              inc=[(cm.s_tr, 1)] if j == 7 else ())
                    p.add("vector", lambda e: e.tensor_copy(qTst[:, :], cm.tr_ps[:, 0:1024]),
                          wait=[w(cm.s_tr), w(s_io)], inc=[(cm.s_lnT, 1)])
                    scr = qT_scr if which == 0 else kT_scr
                    p.add("sync", lambda e, i=i, scr=scr: e.dma_start(
                        scr.ap().rearrange("p (g t) -> p g t", g=8)[:, :, i * 128:(i + 1) * 128],
                        qTst[:, :].rearrange("p (g t) -> p g t", g=8)),
                        wait=[w(cm.s_lnT)], inc=[(s_io, 16)])
                p.add("vector", lambda e: e.tensor_copy(vn_buf[:, :], qkvbuf[:, 2 * D:3 * D]),
                      wait=[w(s_qe), w(s_knv)], inc=[(s_nv, 1)])
                p.add("sync", lambda e, i=i: e.dma_start(vn_scr.ap()[i][:, :], vn_buf[:, :]),
                      wait=[w(s_nv)], inc=[(s_knv, 16)])

        if debug:
            for nmi, buf in enumerate((la_all, bt_all, lah, lal)):
                p.add("sync", lambda e, nmi=nmi, buf=buf: e.dma_start(
                    gdbg.ap()[:, nmi * NT * H:(nmi + 1) * NT * H], buf[:, :]),
                    wait=[w(s_nv)], inc=[(s_io, 16)])
        for h in range(H):
            g, pb = h // 2, (h % 2) * 64
            p.add("sync", lambda e, g=g, pb=pb: e.dma_start(
                qTh[0:64, :], qT_scr.ap()[pb:pb + 64, g * 4096:(g + 1) * 4096]),
                wait=[w(s_tm), w(s_io)], inc=[(s_hl, 16)])
            p.add("sync", lambda e, g=g, pb=pb: e.dma_start(
                kTh[0:64, :], kT_scr.ap()[pb:pb + 64, g * 4096:(g + 1) * 4096]),
                wait=[w(s_tm), w(s_io)], inc=[(s_hl, 16)])
            p.add("sync", lambda e, h=h: e.dma_start(
                kv_sb[:, 0:64], kn_scr.ap()[0][:, h * 64:(h + 1) * 64]),
                wait=[w(s_tm), w(s_knv)], inc=[(s_kv, 16)])
            p.add("sync", lambda e, h=h: e.dma_start(
                kv_sb[:, 64:128], vn_scr.ap()[0][:, h * 64:(h + 1) * 64]),
                wait=[w(s_tm), w(s_knv)], inc=[(s_kv, 16)])
            p.add("vector", lambda e: e.memset(St32[:, :], 0.0), wait=[w(s_tm), w(s_ve)], selfsync=True)
            p.add("vector", lambda e: e.memset(St_bf[:, :], 0.0), inc=[(s_ve, 1)])
            for c in range(NT):
                sl = (c % 2) * 128
                lahc = lambda c=c, h=h: lah[:, c * H + h: c * H + h + 1]
                lalc = lambda c=c, h=h: lal[:, c * H + h: c * H + h + 1]
                btc = lambda c=c, h=h: bt_all[:, c * H + h: c * H + h + 1]
                nbtc = lambda c=c, h=h: nbt_all[:, c * H + h: c * H + h + 1]
                qTc = lambda c=c: qTh[0:64, c * 128:(c + 1) * 128]
                kTc = lambda c=c: kTh[0:64, c * 128:(c + 1) * 128]
                knc = lambda sl=sl: kv_sb[:, sl:sl + 64]
                vnc = lambda sl=sl: kv_sb[:, sl + 64:sl + 128]
                p.add("vector", lambda e, lahc=lahc: e.tensor_scalar(A_sb[:, 0:128], utc[:, :], lahc(), None, op0=ALU.mult),
                      wait=[w(s_tm), w(s_se)], selfsync=True)
                p.add("vector", lambda e, lalc=lalc: e.tensor_scalar(A_sb[:, 128:256], utc[:, :], lalc(), None, op0=ALU.mult),
                      selfsync=True)
                p.add("vector", lambda e, lahc=lahc: e.tensor_scalar(A2_sb[:, 0:128], lts[:, :], lahc(), None, op0=ALU.mult),
                      selfsync=True)
                p.add("vector", lambda e, lalc=lalc: e.tensor_scalar(A2_sb[:, 128:256], lts[:, :], lalc(), None, op0=ALU.mult),
                      inc=[(s_ve, 1)])
                p.add("tensor", lambda e: e.matmul(ps_m[:, 0:130], A_sb[:, 0:128], bxc[:, :], start=True, stop=False),
                      wait=[w(s_ve), w(s_se)])
                p.add("tensor", lambda e: e.matmul(ps_m[:, 0:130], A_sb[:, 128:256], bxc[:, :], start=False, stop=True))
                p.add("tensor", lambda e: e.matmul(ps_m[:, 130:131], A2_sb[:, 0:128], onescol[:, :], start=True, stop=False))
                p.add("tensor", lambda e: e.matmul(ps_m[:, 130:131], A2_sb[:, 128:256], onescol[:, :], start=False, stop=True),
                      inc=[(s_tm, 1)])
                p.add("vector", lambda e: e.tensor_add(Mm_sb[:, :], mkc[:, :], ps_m[:, 0:130]),
                      wait=[w(s_tm), w(s_se)], selfsync=True)
                p.add("vector", lambda e: e.tensor_copy(scol[:, 0:1], ps_m[:, 130:131]), selfsync=True)
                p.add("vector", lambda e: e.tensor_copy(scol[:, 1:2], Mm_sb[:, 0:1]), selfsync=True)
                p.add("vector", lambda e: e.tensor_add(scol[:, 2:3], scol[:, 0:1], scol[:, 1:2]),
                      inc=[(s_ve, 1)])
                p.add("scalar", lambda e: e.activation(Dg[:, :], Mm_sb[:, :], AF.Exp),
                      wait=[w(s_ve)], inc=[(s_se, 1)])
                p.add("scalar", lambda e: e.activation(gf[:, 0:3], scol[:, 0:3], AF.Exp),
                      inc=[(s_se, 1)])
                p.add("vector", lambda e: e.tensor_mul(Dl[:, :], Dg[:, 1:129], om1[:, :]),
                      wait=[w(s_se)], selfsync=True)
                p.add("vector", lambda e: e.tensor_scalar(gneg[:, :], gf[:, 0:1], -1.0, None, op0=ALU.mult),
                      selfsync=True)
                p.add("vector", lambda e, btc=btc: e.tensor_mul(scol[:, 4:5], btc(), gf[:, 1:2]),
                      inc=[(s_ve, 1)])
                p.add("tensor", lambda e, kTc=kTc: e.matmul(ps_e[:, 0:128], kTc(), kTc(), start=True, stop=True),
                      wait=[w(s_ve), w(s_hl), w(s_kv)])
                p.add("tensor", lambda e, qTc=qTc, kTc=kTc: e.matmul(ps_e[:, 128:256], kTc(), qTc(), start=True, stop=True))
                p.add("tensor", lambda e: e.transpose(cm.tr_ps[:, 0:128], Dg[:, 1:129], cm.ident[:, :]),
                      inc=[(s_tm, 1)])
                p.add("vector", lambda e: e.tensor_copy(DT[:, :], cm.tr_ps[:, 0:128]),
                      wait=[w(s_tm)], selfsync=True)
                p.add("vector", lambda e: e.tensor_mul(At_sb[:, :], ps_e[:, 128:256], DT[:, :]),
                      selfsync=True)
                p.add("vector", lambda e: e.tensor_mul(Pn[0][:, :], ps_e[:, 0:128], Dl[:, :]),
                      selfsync=True)
                p.add("vector", lambda e, nbtc=nbtc: e.tensor_scalar(Pn[0][:, :], Pn[0][:, :], nbtc(), None, op0=ALU.mult),
                      inc=[(s_ve, 1)])
                p.add("tensor", lambda e: e.transpose(cm.tr_ps[:, 128:256], Pn[0][:, :], cm.ident[:, :]),
                      wait=[w(s_ve)], inc=[(s_tm, 1)])
                p.add("vector", lambda e: e.tensor_copy(Pt[0][:, :], cm.tr_ps[:, 128:256]),
                      wait=[w(s_tm)], inc=[(s_ve, 1)])
                for kq in range(1, SQK + 1):
                    bank = lambda kq=kq: ps_sq[:, (kq % 2) * 128:((kq % 2) + 1) * 128]
                    p.add("tensor", lambda e, kq=kq, bank=bank: e.matmul(
                        bank(), Pt[kq - 1][:, :], Pn[kq - 1][:, :], start=True, stop=True),
                        wait=[w(s_ve)], inc=[(s_tm, 1)])
                    p.add("vector", lambda e, kq=kq, bank=bank: e.tensor_copy(Pn[kq][:, :], bank()),
                          wait=[w(s_tm)], inc=[(s_ve, 1)])
                    if kq < SQK:
                        p.add("tensor", lambda e, kq=kq, bank=bank: e.matmul(
                            bank(), Pn[kq - 1][:, :], Pt[kq - 1][:, :], start=True, stop=True),
                            wait=[w(s_ve)], inc=[(s_tm, 1)])
                        p.add("vector", lambda e, kq=kq, bank=bank: e.tensor_copy(Pt[kq][:, :], bank()),
                              wait=[w(s_tm)], inc=[(s_ve, 1)])
                p.add("vector", lambda e: e.tensor_add(Rt_sb[:, :], cm.ident[:, :], Pt[0][:, :]),
                      wait=[w(s_ve)], inc=[(s_ve, 1)])
                for kq in range(1, SQK + 1):
                    p.add("tensor", lambda e, kq=kq: e.matmul(
                        ps_sq[:, 0:128], Pn[kq][:, :], Rt_sb[:, :], start=True, stop=True),
                        wait=[w(s_ve)], inc=[(s_tm, 1)])
                    p.add("vector", lambda e: e.tensor_add(Rt_sb[:, :], Rt_sb[:, :], ps_sq[:, 0:128]),
                          wait=[w(s_tm)], inc=[(s_ve, 1)])
                p.add("vector", lambda e, vnc=vnc, btc=btc: e.tensor_scalar(bV_sb[:, :], vnc(), btc(), None, op0=ALU.mult),
                      selfsync=True)
                p.add("vector", lambda e, knc=knc: e.tensor_scalar(W2in[:, :], knc(), scol[:, 4:5], None, op0=ALU.mult),
                      wait=[w(s_ve)], inc=[(s_ve, 1)])
                p.add("tensor", lambda e: e.matmul(ps_w[:, 0:64], Rt_sb[:, :], bV_sb[:, :], start=True, stop=True),
                      wait=[w(s_ve)])
                p.add("tensor", lambda e: e.matmul(ps_w[:, 64:128], Rt_sb[:, :], W2in[:, :], start=True, stop=True),
                      inc=[(s_tm, 1)])
                p.add("vector", lambda e: e.tensor_copy(Wv_sb[:, :], ps_w[:, 0:64]),
                      wait=[w(s_tm)], selfsync=True)
                p.add("vector", lambda e: e.tensor_scalar(U1_sb[:, :], ps_w[:, 0:64], gf[:, 0:1], None, op0=ALU.mult),
                      selfsync=True)
                p.add("vector", lambda e: e.tensor_scalar(W2n_sb[:, :], ps_w[:, 64:128], -1.0, None, op0=ALU.mult),
                      selfsync=True)
                p.add("vector", lambda e: e.tensor_scalar(U2n_sb[:, :], ps_w[:, 64:128], gneg[:, :], None, op0=ALU.mult),
                      inc=[(s_ve, 1)])
                p.add("tensor", lambda e, knc=knc: e.matmul(ps_w[0:64, 128:192], U2n_sb[:, :], knc(), start=True, stop=True),
                      wait=[w(s_ve)])
                p.add("tensor", lambda e: e.matmul(ps_w[0:64, 192:320], W2n_sb[:, :], At_sb[:, :], start=True, stop=True),
                      inc=[(s_tm, 1)])
                p.add("vector", lambda e: e.tensor_copy(Zn_sb[:, :], ps_w[:, 128:192]),
                      wait=[w(s_tm)], selfsync=True)
                p.add("vector", lambda e: e.tensor_copy(AW2nt[:, :], ps_w[:, 192:320]),
                      inc=[(s_ve, 1)])
                p.add("tensor", lambda e, qTc=qTc: e.matmul(ps_o[:, 0:64], qTc(), St_bf[0:64, :], start=True, stop=True),
                      wait=[w(s_ve)])
                p.add("tensor", lambda e: e.matmul(ps_o[:, 64:128], At_sb[:, :], Wv_sb[:, :], start=True, stop=False))
                p.add("tensor", lambda e: e.matmul(ps_o[:, 64:128], AW2nt[0:64, :], St_bf[0:64, :], start=False, stop=True),
                      inc=[(s_tm, 1)])
                p.add("vector", lambda e: e.tensor_copy(O2c[:, :], ps_o[:, 64:128]),
                      wait=[w(s_tm)], selfsync=True)
                p.add("vector", lambda e: e.scalar_tensor_tensor(Obuf[:, :], ps_o[:, 0:64], gf[:, 1:2], O2c[:, :], op0=ALU.mult, op1=ALU.add),
                      inc=[(s_ve, 1)])
                p.add("tensor", lambda e: e.transpose(cm.tr_ps[0:64, 256:384], Obuf[:, :], cm.ident[:, :]),
                      wait=[w(s_ve)], inc=[(s_tm, 1)])
                p.add("vector", lambda e, c=c: e.tensor_copy(oTh[0:64, c * 128:(c + 1) * 128], cm.tr_ps[0:64, 256:384]),
                      wait=([w(s_tm), w(s_oio)] if c == 0 else [w(s_tm)]), inc=[(s_ve, 1)])
                p.add("tensor", lambda e, knc=knc: e.matmul(ps_m[0:64, 0:64], knc(), U1_sb[:, :], start=True, stop=False),
                      wait=[w(s_ve)])
                p.add("tensor", lambda e: e.matmul(ps_m[0:64, 0:64], Zn_sb[0:64, :], St_bf[0:64, :], start=False, stop=True),
                      inc=[(s_tm, 1)])
                p.add("vector", lambda e: e.scalar_tensor_tensor(St32[0:64, :], St32[0:64, :], gf[0:64, 2:3], ps_m[0:64, 0:64], op0=ALU.mult, op1=ALU.add),
                      wait=[w(s_tm)], selfsync=True)
                p.add("vector", lambda e: e.tensor_copy(St_bf[0:64, :], St32[0:64, :]),
                      inc=[(s_ve, 1)])
                if c + 1 < NT:
                    sln = ((c + 1) % 2) * 128
                    p.add("sync", lambda e, c=c, h=h, sln=sln: e.dma_start(
                        kv_sb[:, sln:sln + 64], kn_scr.ap()[c + 1][:, h * 64:(h + 1) * 64]),
                        wait=[w(s_tm)], inc=[(s_kv, 16)])
                    p.add("sync", lambda e, c=c, h=h, sln=sln: e.dma_start(
                        kv_sb[:, sln + 64:sln + 128], vn_scr.ap()[c + 1][:, h * 64:(h + 1) * 64]),
                        inc=[(s_kv, 16)])
            p.add("sync", lambda e, g=g, pb=pb: e.dma_start(
                oT_scr.ap()[g][pb:pb + 64, :], oTh[0:64, :]),
                wait=[w(s_ve)], inc=[(s_oio, 16)])

        p.add("gpsimd", lambda e: e.dma_start(
            woutT_sb[:, :].rearrange("p (j c) -> p j c", j=8),
            woutT.ap().rearrange("(j p) c -> p j c", p=128)),
            wait=[w(s_qm)], inc=[(s_wgt, 16)])
        for i in range(NT):
            p.add("sync", lambda e, i=i: e.dma_start(
                oTW[:, :].rearrange("p (g t) -> p g t", g=8),
                bass.AP(oT_scr.ap().tensor, i * 128, [[4096, 128], [128 * 4096, 8], [1, 128]])),
                wait=[w(s_wm), w(s_oio)], inc=[(s_hl, 16)])
            p.add("sync", lambda e, i=i: e.dma_start(x1[:, :], xr[:, i, :]),
                  wait=[w(s_res), w(s_xo)], inc=[(s_hl, 16)])
            for dh in range(2):
                bank = ps_q if dh == 0 else ps_q2
                for gq in range(8):
                    p.add("tensor",
                          lambda e, gq=gq, dh=dh, bank=bank: e.matmul(
                              bank[:, :],
                              oTW[:, gq * 128:(gq + 1) * 128],
                              woutT_sb[:, gq * D + dh * 512: gq * D + (dh + 1) * 512],
                              start=(gq == 0), stop=(gq == 7)),
                          wait=([w(s_hl), w(s_wgt), w(s_res), w(s_qe), w(s_tm), w(s_ve)]) if (gq == 0 and dh == 0) else (),
                          inc=[(s_wm, 1)] if gq == 7 else ())
            p.add("vector", lambda e: e.tensor_add(x1[:, 0:512], x1[:, 0:512], ps_q[:, :]),
                  wait=[w(s_wm), w(s_hl)], selfsync=True)
            p.add("vector", lambda e: e.tensor_add(x1[:, 512:1024], x1[:, 512:1024], ps_q2[:, :]),
                  inc=[(s_res, 1)])
            p.add("sync", lambda e, i=i: e.dma_start(xwr[:, i, :], x1[:, :]),
                  wait=[w(s_res)], inc=[(s_xo, 16)])

        p.add("sync", lambda e: e.wait_ge(s_xo.h, s_xo.n))
        cm.finalize_consts()
        p.emit()
    return nc


def l0_attn_inmap(inputs, b, x):
    wg = np.asarray(inputs["L0_attn_Wg"], np.float32)
    wb = np.asarray(inputs["L0_attn_Wb"], np.float32)
    bg = np.asarray(inputs["L0_attn_bg"], np.float32)
    bb = np.asarray(inputs["L0_attn_bb"], np.float32)
    wgb = np.zeros((1024, 64), np.float32)
    wgb[:, 0:16] = wg.T
    wgb[:, 32:48] = wb.T
    bgbv = np.zeros((1, 64), np.float32)
    bgbv[0, 0:16] = bg
    bgbv[0, 32:48] = bb
    return {
        "x_in": np.ascontiguousarray(x[b]),
        "wqkvT": hT(inputs["L0_attn_Wqkv"]),
        "wgbT": wgb,
        "bgb": bgbv,
        "woutT": hT(inputs["L0_attn_Wout"]),
    }


SUBLAYERS = [
    ("L0_attn", build_L0_attn, l0_attn_inmap),
    ("L0_ffn", build_L0_ffn, l0_ffn_inmap),
    ("L1_attn", build_L1_attn, l1_attn_inmap),
    ("L1_ffn", build_L1_ffn, l1_ffn_inmap),
    ("L2_attn", build_L2_attn, l2_attn_inmap),
    ("L2_ffn", build_L2_ffn, l2_ffn_inmap),
    ("L3_attn", build_L3_attn, l3_attn_inmap),
    ("L3_ffn", build_L3_ffn, l3_ffn_inmap),
]


def kernel(**inputs):
    x = np.asarray(inputs["L0_attn_x"], np.float32)
    B = x.shape[0]
    for name, build, mkmap in SUBLAYERS:
        nc = build()
        in_maps = [mkmap(inputs, b, x) for b in range(B)]
        res = run_bass_kernel_spmd(nc, in_maps, core_ids=list(range(B)))
        x = np.stack([np.asarray(r["x_out"]) for r in res.results], 0)
    return x


# version 06 (final)
# speedup vs baseline: 1.0023x; 1.0023x over previous
import contextlib
import math

import numpy as np
import ml_dtypes

import concourse.bass as bass
import concourse.mybir as mybir
from concourse.bass_utils import run_bass_kernel_spmd

F32 = mybir.dt.float32
BF16 = mybir.dt.bfloat16
I32 = mybir.dt.int32
AF = mybir.ActivationFunctionType
ALU = mybir.AluOpType
AX = mybir.AxisListType
bfdt = ml_dtypes.bfloat16

T = 4096
D = 1024
NT = 32
NDT = 8
NEG = -1.0e30

ENGINES = ("sync", "gpsimd", "vector", "scalar", "tensor")


class Sem:
    __slots__ = ("h", "n", "name")

    def __init__(self, h, name):
        self.h = h
        self.n = 0
        self.name = name


class Prog:

    def __init__(self, nc, stack):
        self.nc = nc
        self.stack = stack
        self.streams = {e: [] for e in ENGINES}
        self.nsem = 0
        self._selfsem = {}
        self._named = {}

    def sem(self, name=None):
        if name is not None and name in self._named:
            return self._named[name]
        name = name or f"s{self.nsem}"
        self.nsem += 1
        h = self.stack.enter_context(self.nc.semaphore(name))
        s = Sem(h, name)
        self._named[name] = s
        return s

    def _self(self, eng):
        if eng not in self._selfsem:
            self._selfsem[eng] = self.sem(f"self_{eng}")
        return self._selfsem[eng]

    def selfwait(self, eng):
        s = self._self(eng)
        return (s, s.n)

    def add(self, eng, fn, wait=(), inc=(), selfsync=False):
        wl = []
        if eng == "vector" and eng in self._selfsem:
            s = self._selfsem[eng]
            if s.n > 0:
                wl.append((s, s.n))
        for s, v in wait:
            assert v <= s.n, f"wait on future count of {s.name}: {v} > {s.n}"
            if v > 0:
                wl.append((s, v))
        il = []
        for s, k in inc:
            s.n += k
            il.append((s, k))
        if selfsync:
            s = self._self(eng)
            s.n += 1
            il.append((s, 1))
        assert len(il) <= 1, "at most one sync update per instruction"
        self.streams[eng].append((fn, wl, il))

    def emit(self):
        with self.nc.Block() as block:
            for e in ENGINES:
                ops = self.streams[e]
                if not ops:
                    continue

                def body(engine, ops=ops):
                    lastw = {}
                    for fn, wl, il in ops:
                        for s, v in wl:
                            if lastw.get(id(s), 0) >= v:
                                continue
                            engine.wait_ge(s.h, v)
                            lastw[id(s)] = v
                        ins = fn(engine) if fn is not None else None
                        for s, k in il:
                            assert ins is not None, "inc requires an instruction"
                            ins = ins.then_inc(s.h, k)

                getattr(block, e)(body)


def w(sem):
    return (sem, sem.n)


def relw(sem_b, sem_a, back=1):
    assert sem_a.n == sem_b.n, \
        f"rel pairing broken: {sem_a.name}={sem_a.n} {sem_b.name}={sem_b.n}"
    return (sem_b, max(0, sem_b.n - back))


class Arena:

    def __init__(self, nc, stack, size_bytes, name="arena", pfx="", at_addr=None):
        self.nc = nc
        self.pfx = pfx
        if at_addr is not None:
            self.base = at_addr
        else:
            base_t = stack.enter_context(nc.sbuf_tensor(pfx + name, [128, size_bytes], mybir.dt.uint8))
            self.base = nc.lookup_mloc(base_t).addr
        self.size = size_bytes

    def at(self, off_kb, name, shape, dtype):
        off = int(off_kb * 1024)
        nbytes = int(np.prod(shape[1:])) * (2 if dtype == BF16 else 4)
        assert off + nbytes <= self.size, f"{name} overflows arena"
        return self.nc.alloc_sbuf_tensor_at(self.pfx + name, shape, dtype, offset=self.base + off)



def hT(a):
    return np.ascontiguousarray(np.asarray(a).T)


def hC(a):
    return np.ascontiguousarray(np.asarray(a))



class Common:

    def __init__(self, p: Prog, x_cols=4 * D, lnbf_cols=2 * D):
        self.p = p
        nc = p.nc
        st = p.stack
        self.x_sb = st.enter_context(nc.sbuf_tensor("x_sb", [128, x_cols], F32))
        self.lnbf = st.enter_context(nc.sbuf_tensor("lnbf", [128, lnbf_cols], BF16))
        self.sq_scr = None
        self.stat = st.enter_context(nc.sbuf_tensor("stat", [128, 16], F32))
        self.tr_ps = st.enter_context(nc.psum_tensor("tr_ps", [128, 1024], BF16))
        self.ident = st.enter_context(nc.sbuf_tensor("ident", [128, 128], BF16))
        self.ones1 = st.enter_context(nc.sbuf_tensor("ones1", [1, 128], BF16))
        self.s_ssq = p.sem("ssq")
        self.s_var = p.sem("var")
        self.s_rstd = p.sem("rstd")
        self.s_lnbf = p.sem("lnbf")
        self.s_tr = p.sem("tr")
        self.s_lnT = p.sem("lnT")
        self.ident_dram = nc.inline_tensor(np.eye(128, dtype=bfdt), name="identc")
        self.ones1_dram = nc.inline_tensor(np.ones((1, 128), dtype=bfdt), name="ones1c")
        self.cbuf = st.enter_context(nc.sbuf_tensor("cbuf", [128, 16], F32))
        self.cvals = {}
        self.s_cst = p.sem("cst")
        self.cb(1e-5)

    def cb(self, val):
        if val not in self.cvals:
            assert len(self.cvals) < 16
            self.cvals[val] = len(self.cvals)
        c = self.cvals[val]
        return self.cbuf[:, c:c + 1]

    def load_consts(self, s_dma):
        p = self.p
        p.add("sync", lambda e: e.dma_start(self.ident[:, :], self.ident_dram.ap()[:, :]),
              inc=[(s_dma, 16)])
        p.add("sync", lambda e: e.dma_start(self.ones1[:, :], self.ones1_dram.ap()[:, :]),
              inc=[(s_dma, 16)])

    def finalize_consts(self):
        p = self.p
        vals = np.zeros((1, 16), np.float32)
        for v, c in self.cvals.items():
            vals[0, c] = v
        cd = p.nc.inline_tensor(np.tile(vals, (128, 1)), name="cbank")
        op = (lambda e: e.dma_start(self.cbuf[:, :], cd.ap()[:, :]), [], [(self.s_cst, 16)])
        self.s_cst.n += 16
        p.streams["gpsimd"].insert(0, op)
        for e in ENGINES:
            if e == "gpsimd" or not p.streams[e]:
                continue
            p.streams[e].insert(0, (None, [(self.s_cst, 16)], []))
        if len(p.streams["gpsimd"]) > 1:
            p.streams["gpsimd"].insert(1, (None, [(self.s_cst, 16)], []))

    def ln_tile(self, i, out_ap_fn, deps=(), slot=None, src_col=None, out_wait=()):
        p = self.p
        assert self.sq_scr is not None, "sublayer must set cm.sq_scr"
        c0 = (i * D) if src_col is None else src_col
        xt = lambda: self.x_sb[:, c0:c0 + D]
        st = self.stat
        p.add("scalar",
              lambda e, sq=self.sq_scr: e.activation(sq[:, :], xt(), AF.Square,
                                                     accum_out=st[:, 1:2]),
              wait=list(deps) + [w(self.s_var)],
              inc=[(self.s_ssq, 1)])
        p.add("vector",
              lambda e: e.reduce_sum(st[:, 0:1], xt(), axis=AX.X),
              wait=list(deps), selfsync=True)
        p.add("vector", lambda e: e.tensor_scalar(st[:, 2:3], st[:, 0:1], 1.0 / D, None, op0=ALU.mult),
              selfsync=True)
        p.add("vector", lambda e: e.tensor_mul(st[:, 5:6], st[:, 2:3], st[:, 2:3]),
              selfsync=True)
        p.add("vector",
              lambda e: e.tensor_scalar(st[:, 6:7], st[:, 1:2], 1.0 / D, None, op0=ALU.mult),
              wait=[w(self.s_ssq)], selfsync=True)
        p.add("vector", lambda e: e.tensor_sub(st[:, 3:4], st[:, 6:7], st[:, 5:6]),
              wait=[w(self.s_rstd)],
              inc=[(self.s_var, 1)])
        p.add("scalar", lambda e: e.activation(st[:, 7:8], st[:, 3:4], AF.Ln, bias=self.cb(1e-5)),
              wait=[w(self.s_var)])
        p.add("scalar", lambda e: e.activation(st[:, 4:5], st[:, 7:8], AF.Exp, scale=-0.5),
              inc=[(self.s_rstd, 1)])
        p.add("vector",
              lambda e: e.tensor_scalar(out_ap_fn(), xt(), st[:, 2:3], st[:, 4:5],
                                        op0=ALU.subtract, op1=ALU.mult),
              wait=[w(self.s_rstd), (self.s_tr, max(0, self.s_tr.n - 1))] + list(out_wait),
              inc=[(self.s_lnbf, 1)])

    def transpose_tile(self, src_ap_fn, dst_ap_fn, wait_src, evac_wait=()):
        p = self.p

        def tr(e, j):
            return e.transpose(self.tr_ps[:, j * 128:(j + 1) * 128],
                               src_ap_fn()[:, j * 128:(j + 1) * 128],
                               self.ident[:, :])

        for j in range(8):
            p.add("tensor", lambda e, j=j: tr(e, j),
                  wait=(list(wait_src) + [w(self.s_lnT)]) if j == 0 else (),
                  inc=[(self.s_tr, 1)] if j == 7 else ())
        p.add("vector", lambda e: e.tensor_copy(dst_ap_fn(), self.tr_ps[:, :]),
              wait=[w(self.s_tr)] + list(evac_wait), inc=[(self.s_lnT, 1)])



def emit_L1_attn(fz, pfx, x_in, x_out, debug=False):
    nc, p, cm = fz.nc, fz.p, fz.cm
    wq = nc.dram_tensor(pfx + "wqT", [D, D], F32, kind="ExternalInput")
    wk = nc.dram_tensor(pfx + "wkT", [D, D], F32, kind="ExternalInput")
    wv = nc.dram_tensor(pfx + "wvT", [D, D], F32, kind="ExternalInput")
    wo = nc.dram_tensor(pfx + "woT", [D, D], F32, kind="ExternalInput")

    H = 16
    DH = 64
    VW = DH + 1

    mask_diag = np.where(np.arange(128)[:, None] <= np.arange(128)[None, :], 0.0, NEG).astype(np.float32)
    mask_prev = np.where(np.arange(128)[:, None] >= np.arange(128)[None, :], 0.0, NEG).astype(np.float32)

    if debug:
        dbg_ln = nc.dram_tensor("dbg_ln", [128, D], F32, kind="ExternalOutput")
        dbg_qT = nc.dram_tensor("dbg_qT", [128, 8 * 512], F32, kind="ExternalOutput")
        dbg_kT = nc.dram_tensor("dbg_kT", [128, 8 * 512], F32, kind="ExternalOutput")
        dbg_v = nc.dram_tensor("dbg_v", [128, 4 * 16 * 65], F32, kind="ExternalOutput")
        dbg_at = nc.dram_tensor("dbg_at", [128, D], F32, kind="ExternalOutput")
    with contextlib.ExitStack() as stack:
        sb = lambda name, shape, dt=BF16: stack.enter_context(nc.sbuf_tensor(pfx + name, shape, dt))
        ps = lambda name, shape, dt=F32: stack.enter_context(nc.psum_tensor(pfx + name, shape, dt))
        cm.sq_scr = sb("sq_scr2", [128, D], F32)
        dbg32 = sb("dbg32", [128, 8 * 512], F32) if debug else None

        wq_sb = sb("wq_sb", [128, 8 * D])
        wk_sb = sb("wk_sb", [128, 8 * D])
        wv_sb = sb("wv_sb", [128, 8 * D])
        wo_sb = sb("wo_sb", [128, 8 * D])
        lnT4 = sb("lnT4", [128, 4096])
        qT4 = sb("qT4", [128, 8 * 512])
        kT4 = sb("kT4", [128, 8 * 512])
        kTprev = sb("kTprev", [128, 8 * 128])
        v4 = sb("v4", [128, 4 * H * VW])
        vprev = sb("vprev", [128, H * VW])
        attn = sb("attn", [128, D])
        attnT_sb = sb("attnT_sb", [128, D])
        attnT_ps = ps("attnT_ps", [128, 1024], BF16)
        mdiag = sb("mdiag", [128, 128], F32)
        mprev = sb("mprev", [128, 128], F32)
        pt2 = sb("pt2", [128, 2 * 128])

        qkv_ps = [ps("qkv_ps0", [128, 512]), ps("qkv_ps1", [128, 512])]
        sc_ps = ps("sc_ps", [128, 128])
        o_ps = ps("o_ps", [128, 128])
        pr_ps = ps("pr_ps", [128, 1024])

        md = nc.inline_tensor(mask_diag, name=pfx + "mdiagc")
        mp = nc.inline_tensor(mask_prev, name=pfx + "mprevc")

        s_dma = p.sem("dma")
        s_wgt = p.sem("wgt")
        s_qkvm = p.sem("qkvm")
        s_qkve = p.sem("qkve")
        s_prev = p.sem("prev")
        s_scm = p.sem("scm")
        s_msk = p.sem("msk")
        s_exp = p.sem("exp")
        s_om = p.sem("om")
        s_oev = p.sem("oev")
        s_atr = p.sem("atr")
        s_ate = p.sem("ate")
        s_prm = p.sem("prm")
        s_res = p.sem("res")
        s_out = p.sem("out")

        cm.load_consts(s_dma)
        p.add("sync", lambda e: e.dma_start(mdiag[:, :], md.ap()[:, :]), inc=[(s_dma, 16)])
        p.add("sync", lambda e: e.dma_start(mprev[:, :], mp.ap()[:, :]), inc=[(s_dma, 16)])
        for _ge in ("tensor", "vector", "scalar"):
            p.add(_ge, None, wait=[w(s_dma)])
        xr = x_in.ap().rearrange("(i p) d -> p i d", p=128)
        xwr = x_out.ap().rearrange("(i p) d -> p i d", p=128)
        s_xc = p.sem("xc")
        s_xo = p.sem("xo")
        s_dbg = p.sem("dbg")
        for wdr, wsb in ((wq, wq_sb), (wk, wk_sb), (wv, wv_sb), (wo, wo_sb)):
            wsrc = wdr.ap().rearrange("(j p) c -> p j c", p=128)
            p.add("gpsimd", lambda e, a=wsb, b=wsrc: e.dma_start(
                      a[:, :].rearrange("p (j c) -> p j c", j=8), b),
                  inc=[(s_wgt, 16)])
        for slot in range(4):
            for h in range(H):
                col = slot * H * VW + h * VW + DH
                p.add("vector", lambda e, c=col: e.memset(v4[:, c:c + 1], 1.0))
        for h in range(H):
            col = h * VW + DH
            p.add("vector", lambda e, c=col: e.memset(vprev[:, c:c + 1], 1.0))

        wdep = [(s_wgt, s_wgt.n)]

        for c4 in range(8):
            tiles = [c4 * 4 + k for k in range(4)]
            p.add("sync",
                  lambda e, c4=c4: e.dma_start(
                      cm.x_sb[:, :].rearrange("p (k d) -> p k d", k=4),
                      xr[:, c4 * 4:(c4 + 1) * 4, :]),
                  wait=[w(s_res), w(s_xo)], inc=[(s_xc, 16)])
            xdep = [w(s_xc)]
            for k, i in enumerate(tiles):
                slot = i % 2
                cm.ln_tile(i, (lambda s=slot: cm.lnbf[:, s * D:(s + 1) * D]), deps=xdep,
                           src_col=k * D)
                if debug and i == 4:
                    p.add("gpsimd", lambda e, s=slot: e.dma_start(dbg_ln.ap()[:, :], cm.lnbf[:, s * D:(s + 1) * D]),
                          wait=[w(cm.s_lnbf)], inc=[(s_dbg, 16)])
                dst = lambda k=k: lnT4[:, :].rearrange("p (j t) -> p j t", j=8)[:, :, k * 128:(k + 1) * 128]
                cm.transpose_tile((lambda s=slot: cm.lnbf[:, s * D:(s + 1) * D]),
                                  dst, wait_src=[w(cm.s_lnbf)],
                                  evac_wait=[w(s_qkvm)])
            lnT_ready = [w(cm.s_lnT)]

            for which, wsb, dstbuf, scale in (("q", wq_sb, qT4, 0.125), ("k", wk_sb, kT4, None)):
                for ct in range(8):
                    bank = qkv_ps[ct % 2]
                    def mm(e, j, ct=ct, wsb=wsb, bank=bank):
                        return e.matmul(bank[:, :],
                                        wsb[:, j * D + ct * 128: j * D + ct * 128 + 128],
                                        lnT4[:, j * 512:(j + 1) * 512],
                                        start=(j == 0), stop=(j == 7))
                    for j in range(8):
                        p.add("tensor", lambda e, j=j, f=mm: f(e, j),
                              wait=(lnT_ready + wdep + [relw(s_qkve, s_qkvm)]) if j == 0 else (),
                              inc=[(s_qkvm, 1)] if j == 7 else ())
                    if scale is not None:
                        p.add("scalar",
                              lambda e, ct=ct, bank=bank, dstbuf=dstbuf, sc2=scale: e.activation(
                                  dstbuf[:, ct * 512:(ct + 1) * 512], bank[:, :], AF.Copy, scale=sc2),
                              wait=[w(s_qkvm), w(s_scm)], inc=[(s_qkve, 1)])
                    else:
                        p.add("vector",
                              lambda e, ct=ct, bank=bank, dstbuf=dstbuf: e.tensor_copy(
                                  dstbuf[:, ct * 512:(ct + 1) * 512], bank[:, :]),
                              wait=[w(s_qkvm), w(s_scm)], inc=[(s_qkve, 1)])
            for k in range(4):
                for half in range(2):
                    bank = qkv_ps[half]
                    def mmv(e, j, k=k, half=half, bank=bank):
                        return e.matmul(bank[:, :],
                                        lnT4[:, j * 512 + k * 128: j * 512 + k * 128 + 128],
                                        wv_sb[:, j * D + half * 512: j * D + (half + 1) * 512],
                                        start=(j == 0), stop=(j == 7))
                    for j in range(8):
                        p.add("tensor", lambda e, j=j, f=mmv: f(e, j),
                              wait=([relw(s_qkve, s_qkvm)]) if j == 0 else (),
                              inc=[(s_qkvm, 1)] if j == 7 else ())
                    def ev(e, k=k, half=half, bank=bank):
                        dst = v4[:, :].rearrange("p (s h c) -> p s h c", s=4, h=H)[
                            :, k, half * 8:(half + 1) * 8, 0:DH]
                        return e.tensor_copy(dst, bank[:, :].rearrange("p (h c) -> p h c", h=8))
                    p.add("vector", lambda e, f=ev: f(e),
                          wait=[w(s_qkvm), w(s_om)], inc=[(s_qkve, 1)])

            if debug and c4 == 1:
                p.add("gpsimd", lambda e: e.dma_start(dbg_qT.ap()[:, :], qT4[:, :]), wait=[w(s_qkve)], inc=[(s_dbg, 16)])
                p.add("gpsimd", lambda e: e.dma_start(dbg_kT.ap()[:, :], kT4[:, :]), wait=[w(s_qkve)], inc=[(s_dbg, 16)])
                p.add("gpsimd", lambda e: e.dma_start(dbg_v.ap()[:, :], v4[:, :]), wait=[w(s_qkve)], inc=[(s_dbg, 16)])
            for k, i in enumerate(tiles):
                for h in range(H):
                    ct, prow = h // 2, (h % 2) * 64
                    jts = ([-1] if i > 0 else []) + [0]
                    nj = len(jts)
                    for idx, jt in enumerate(jts):
                        if jt == 0:
                            kt_ap = lambda ct=ct, prow=prow, k=k: kT4[prow:prow + 64, ct * 512 + k * 128: ct * 512 + k * 128 + 128]
                        elif k > 0:
                            kt_ap = lambda ct=ct, prow=prow, k=k: kT4[prow:prow + 64, ct * 512 + (k - 1) * 128: ct * 512 + (k - 1) * 128 + 128]
                        else:
                            kt_ap = lambda ct=ct, prow=prow: kTprev[prow:prow + 64, ct * 128:(ct + 1) * 128]
                        qt_ap = lambda ct=ct, prow=prow, k=k: qT4[prow:prow + 64, ct * 512 + k * 128: ct * 512 + k * 128 + 128]
                        p.add("tensor",
                              lambda e, ka=kt_ap, qa=qt_ap: e.matmul(sc_ps[:, :], ka(), qa(),
                                                                     start=True, stop=True),
                              wait=[w(s_exp), w(s_qkve), w(s_prev)],
                              inc=[(s_scm, 1)])
                        msk = mprev if jt == -1 else mdiag
                        p.add("vector",
                              lambda e, m=msk: e.tensor_add(sc_ps[:, :], sc_ps[:, :], m[:, :]),
                              wait=[w(s_scm)], inc=[(s_msk, 1)])
                        p.add("scalar",
                              lambda e, idx=idx: e.activation(pt2[:, idx * 128:(idx + 1) * 128],
                                                              sc_ps[:, :], AF.Exp, bias=cm.cb(-20.0)),
                              wait=[w(s_msk), w(s_om)], inc=[(s_exp, 1)])
                        if jt == 0:
                            v_ap = lambda k=k, h=h: v4[:, k * H * VW + h * VW: k * H * VW + (h + 1) * VW]
                        elif k > 0:
                            v_ap = lambda k=k, h=h: v4[:, (k - 1) * H * VW + h * VW: (k - 1) * H * VW + (h + 1) * VW]
                        else:
                            v_ap = lambda h=h: vprev[:, h * VW:(h + 1) * VW]
                        p.add("tensor",
                              lambda e, idx=idx, va=v_ap, st=(idx == 0), sp=(idx == nj - 1):
                                  e.matmul(o_ps[:, 0:VW], pt2[:, idx * 128:(idx + 1) * 128],
                                           va(), start=st, stop=sp),
                              wait=[w(s_exp)] + ([w(s_oev)] if idx == 0 else []),
                              inc=[(s_om, 1)])
                    p.add("vector", lambda e: e.reciprocal(cm.stat[:, 8:9], o_ps[:, DH:DH + 1]),
                          wait=[w(s_om), w(s_atr)], selfsync=True)
                    p.add("vector",
                          lambda e, h=h: e.tensor_scalar(attn[:, h * DH:(h + 1) * DH],
                                                         o_ps[:, 0:DH], cm.stat[:, 8:9], None,
                                                         op0=ALU.mult),
                          inc=[(s_oev, 1)])
                if debug and i == 4:
                    p.add("gpsimd", lambda e: e.dma_start(dbg_at.ap()[:, :], attn[:, :]), wait=[w(s_oev)], inc=[(s_dbg, 16)])
                for j in range(8):
                    p.add("tensor",
                          lambda e, j=j: e.transpose(attnT_ps[:, j * 128:(j + 1) * 128],
                                                     attn[:, j * 128:(j + 1) * 128], cm.ident[:, :]),
                          wait=([w(s_oev), w(s_ate)]) if j == 0 else (),
                          inc=[(s_atr, 1)] if j == 7 else ())
                p.add("vector", lambda e: e.tensor_copy(attnT_sb[:, :], attnT_ps[:, :]),
                      wait=[w(s_atr), w(s_prm)], inc=[(s_ate, 1)])
                def mmp(e, j, half):
                    return e.matmul(pr_ps[:, half * 512:(half + 1) * 512],
                                    attnT_sb[:, j * 128:(j + 1) * 128],
                                    wo_sb[:, j * D + half * 512: j * D + (half + 1) * 512],
                                    start=(j == 0), stop=(j == 7))
                for half in range(2):
                    for j in range(8):
                        p.add("tensor", lambda e, j=j, h2=half: mmp(e, j, h2),
                              wait=([w(s_ate), w(s_res)] + wdep) if (j == 0 and half == 0) else (),
                              inc=[(s_prm, 1)] if j == 7 else ())
                p.add("vector",
                      lambda e, k=k: e.tensor_add(cm.x_sb[:, k * D:(k + 1) * D],
                                                  cm.x_sb[:, k * D:(k + 1) * D], pr_ps[:, :]),
                      wait=[w(s_prm)], inc=[(s_res, 1)])
                p.add("sync",
                      lambda e, i=i, k=k: e.dma_start(xwr[:, i, :],
                                                      cm.x_sb[:, k * D:(k + 1) * D]),
                      wait=[w(s_res)], inc=[(s_xo, 16)])

            def savek(e):
                src = kT4[:, :].rearrange("p (ct t) -> p ct t", ct=8)[:, :, 3 * 128:4 * 128]
                return e.tensor_copy(kTprev[:, :].rearrange("p (ct t) -> p ct t", ct=8), src)
            p.add("vector", lambda e: savek(e), wait=[w(s_qkve), w(s_scm)], inc=[(s_prev, 1)])
            p.add("vector",
                  lambda e: e.tensor_copy(vprev[:, :], v4[:, 3 * H * VW: 4 * H * VW]),
                  wait=[w(s_qkve), w(s_om)], inc=[(s_prev, 1)])

        p.add("sync", lambda e, v=s_xo.n: e.wait_ge(s_xo.h, v))
        if s_dbg.n:
            p.add("sync", lambda e, v=s_dbg.n: e.wait_ge(s_dbg.h, v))


def l1_attn_inmap(inputs, b, x):
    return {
        "x_in": np.ascontiguousarray(x[b]),
        "wqT": hT(inputs["L1_attn_Wq"]),
        "wkT": hT(inputs["L1_attn_Wk"]),
        "wvT": hT(inputs["L1_attn_Wv"]),
        "woT": hT(inputs["L1_attn_Wo"]),
    }





def emit_L0_ffn(fz, pfx, x_in, x_out):
    nc, p, cm = fz.nc, fz.p, fz.cm
    gwT = nc.dram_tensor(pfx + "gwT", [D, 16], F32, kind="ExternalInput")
    w1 = nc.dram_tensor(pfx + "w1", [16, D, 512], F32, kind="ExternalInput")
    w3 = nc.dram_tensor(pfx + "w3", [16, D, 512], F32, kind="ExternalInput")
    w2 = nc.dram_tensor(pfx + "w2", [16, 512, D], F32, kind="ExternalInput")
    ws1 = nc.dram_tensor(pfx + "ws1", [D, D], F32, kind="ExternalInput")
    ws3 = nc.dram_tensor(pfx + "ws3", [D, D], F32, kind="ExternalInput")
    ws2 = nc.dram_tensor(pfx + "ws2", [D, D], F32, kind="ExternalInput")
    lnT_scr = nc.dram_tensor(pfx + "lnT_scr", [8, 128, 4096], BF16, kind="Internal")

    GROUPS = [[0, 1, 2, 3, 4], [5, 6, 7, 8, 9], [10, 11, 12, 13, 14], [15, "s0", "s1"]]
    NSLOT = 5

    with contextlib.ExitStack() as stack:
        sb = lambda name, shape, dt=BF16: stack.enter_context(nc.sbuf_tensor(pfx + name, shape, dt))
        ps = lambda name, shape, dt=F32: stack.enter_context(nc.psum_tensor(pfx + name, shape, dt))
        cm.sq_scr = sb("sq_scr2", [128, D], F32)

        w1g = sb("w1g", [128, NSLOT * 4096])
        w3g = sb("w3g", [128, NSLOT * 4096])
        w2g = sb("w2g", [128, NSLOT * 4096])
        lnT4 = sb("lnT4", [128, 4096])
        gsb = sb("gsb", [128, 8 * 16])
        s1T = sb("s1T", [128, 2048])
        h3T = sb("h3T", [128, 2048])
        aT = sb("aT", [128, 2048])
        y_sb = sb("y_sb", [128, 4096], F32)
        c_all = sb("c_all", [128, NT * 16], F32)
        rb = sb("rb", [128, 128], F32)
        gsb32 = sb("gsb32", [128, 8 * 16], F32)
        glo = sb("glo", [128, 8 * 16])
        lnf_sc = sb("lnf_sc", [128, D], F32)
        lo_bf = sb("lo_bf", [128, D])
        loT4 = sb("loT4", [128, 4096])

        rt_ps = ps("rt_ps", [128, 16])
        hps = [ps("hps0", [128, 512]), ps("hps1", [128, 512])]
        y_ps = ps("y_ps", [128, 1024])

        s_dma = p.sem("dma")
        s_wg = p.sem("wg")
        s_ltio = p.sem("ltio")
        s_rtm = p.sem("rtm")
        s_rte = p.sem("rte")
        s_c = p.sem("c")
        s_hm = p.sem("hm")
        s_he = p.sem("he")
        s_aT = p.sem("aT")
        s_ym = p.sem("ym")
        s_ya = p.sem("ya")
        s_xc = p.sem("xc")
        s_xo = p.sem("xo")
        s_xadd = p.sem("xadd")
        s_glo = p.sem("glo")
        s_lo = p.sem("lo")
        s_lo2 = p.sem("lo2")

        cm.load_consts(s_dma)
        for _ge in ("tensor", "vector", "scalar"):
            p.add(_ge, None, wait=[w(s_dma)])
        gsrc = gwT.ap().rearrange("(j p) e -> p j e", p=128)
        p.add("gpsimd", lambda e: e.dma_start(gsb[:, :].rearrange("p (j e) -> p j e", j=8), gsrc),
              inc=[(s_wg, 16)])
        p.add("sync", lambda e: e.dma_start(gsb32[:, :].rearrange("p (j e) -> p j e", j=8), gsrc),
              inc=[(s_glo, 16)])
        p.add("vector", lambda e: e.tensor_sub(glo[:, :], gsb32[:, :], gsb[:, :]),
              wait=[w(s_wg), w(s_glo)], inc=[(s_glo, 1)])

        xr = x_in.ap().rearrange("(i p) d -> p i d", p=128)
        xwr = x_out.ap().rearrange("(i p) d -> p i d", p=128)

        def wslot_srcs(e):
            if e == "s0":
                return (ws1.ap()[:, 0:512], ws3.ap()[:, 0:512], ws2.ap()[0:512, :])
            if e == "s1":
                return (ws1.ap()[:, 512:1024], ws3.ap()[:, 512:1024], ws2.ap()[512:1024, :])
            return (w1.ap()[e], w3.ap()[e], w2.ap()[e])

        for g, grp in enumerate(GROUPS):
            for slot, e in enumerate(grp):
                a1, a3, a2 = wslot_srcs(e)
                war = [w(s_hm), w(s_ym)]
                p.add("gpsimd", lambda e_, a=a1, s=slot: e_.dma_start(
                    w1g[:, s * 4096:(s + 1) * 4096].rearrange("p (j f) -> p j f", j=8),
                    a.rearrange("(j p) f -> p j f", p=128)), wait=war, inc=[(s_wg, 16)])
                p.add("gpsimd", lambda e_, a=a3, s=slot: e_.dma_start(
                    w3g[:, s * 4096:(s + 1) * 4096].rearrange("p (j f) -> p j f", j=8),
                    a.rearrange("(j p) f -> p j f", p=128)), wait=war, inc=[(s_wg, 16)])
                p.add("gpsimd", lambda e_, a=a2, s=slot: e_.dma_start(
                    w2g[:, s * 4096:(s + 1) * 4096].rearrange("p (ft d) -> p ft d", ft=4),
                    a.rearrange("(ft p) d -> p ft d", p=128)), wait=war, inc=[(s_wg, 16)])
            wdep = [w(s_wg)]

            for c4 in range(8):
                tiles = [c4 * 4 + k for k in range(4)]
                xsrc = xr if g == 0 else xwr
                p.add("sync",
                      lambda e, c4=c4, xsrc=xsrc: e.dma_start(
                          cm.x_sb[:, :].rearrange("p (k d) -> p k d", k=4),
                          xsrc[:, c4 * 4:(c4 + 1) * 4, :]),
                      wait=[w(s_xadd), w(s_xo), w(s_lo)], inc=[(s_xc, 16)])
                xdep = [w(s_xc)]

                if g == 0:
                    for k, i in enumerate(tiles):
                        slot = i % 2
                        cm.ln_tile(i, (lambda s=slot: cm.lnbf[:, s * D:(s + 1) * D]),
                                   deps=xdep, src_col=k * D)
                        dst = lambda k=k: lnT4[:, :].rearrange("p (j t) -> p j t", j=8)[:, :, k * 128:(k + 1) * 128]
                        cm.transpose_tile((lambda s=slot: cm.lnbf[:, s * D:(s + 1) * D]),
                                          dst, wait_src=[w(cm.s_lnbf)],
                                          evac_wait=[w(s_hm), w(s_rtm), w(s_ltio)])
                        st = cm.stat
                        p.add("vector", lambda e, k=k: e.tensor_scalar(
                            lnf_sc[:, :], cm.x_sb[:, k * D:(k + 1) * D], st[:, 2:3], st[:, 4:5],
                            op0=ALU.subtract, op1=ALU.mult),
                            wait=[w(cm.s_rstd)], inc=[(s_lo, 1)])
                        p.add("vector", lambda e, s=slot: e.tensor_sub(
                            lo_bf[:, :], lnf_sc[:, :], cm.lnbf[:, s * D:(s + 1) * D]),
                            wait=[w(s_lo)], inc=[(s_lo2, 1)])
                        dstlo = lambda k=k: loT4[:, :].rearrange("p (j t) -> p j t", j=8)[:, :, k * 128:(k + 1) * 128]
                        cm.transpose_tile((lambda: lo_bf[:, :]), dstlo,
                                          wait_src=[w(s_lo2)],
                                          evac_wait=[w(s_rtm)])
                    p.add("sync", lambda e, c4=c4: e.dma_start(lnT_scr.ap()[c4], lnT4[:, :]),
                          wait=[w(cm.s_lnT)], inc=[(s_ltio, 16)])
                    for k, i in enumerate(tiles):
                        for pi, (lsrc, rsrc) in enumerate(((lnT4, gsb), (lnT4, glo), (loT4, gsb))):
                            for j in range(8):
                                p.add("tensor",
                                      lambda e, j=j, k=k, pi=pi, lsrc=lsrc, rsrc=rsrc: e.matmul(
                                          rt_ps[:, :], lsrc[:, j * 512 + k * 128: j * 512 + k * 128 + 128],
                                          rsrc[:, j * 16:(j + 1) * 16],
                                          start=(pi == 0 and j == 0), stop=(pi == 2 and j == 7)),
                                      wait=([w(cm.s_lnT), w(s_wg), w(s_rte), w(s_glo)]) if (pi == 0 and j == 0) else (),
                                      inc=[(s_rtm, 1)] if (pi == 2 and j == 7) else ())
                        p.add("vector", lambda e: e.reduce_max(rb[:, 0:1], rt_ps[:, :], axis=AX.X),
                              wait=[w(s_rtm)], selfsync=True)
                        p.add("vector", lambda e: e.tensor_scalar(rb[:, 1:2], rb[:, 0:1], -1.0, None, op0=ALU.mult),
                              selfsync=True)
                        p.add("scalar",
                              lambda e: e.activation(rb[:, 16:32], rt_ps[:, :], AF.Exp,
                                                     bias=rb[:, 1:2], accum_out=rb[:, 2:3]),
                              wait=[p.selfwait("vector")], inc=[(s_rte, 1)])
                        p.add("vector", lambda e: e.reciprocal(rb[:, 3:4], rb[:, 2:3]),
                              wait=[w(s_rte)], selfsync=True)
                        p.add("vector", lambda e: e.tensor_scalar(rb[:, 32:48], rb[:, 16:32], rb[:, 3:4], None, op0=ALU.mult),
                              selfsync=True)
                        p.add("vector", lambda e: e.reduce_max(
                            rb[:, 48:52].rearrange("p (g o) -> p g o", g=4),
                            rb[:, 32:48].rearrange("p (g f) -> p g f", g=4), axis=AX.X), selfsync=True)
                        p.add("vector", lambda e: e.reduce_max(rb[:, 52:53], rb[:, 48:52], axis=AX.X), selfsync=True)
                        p.add("vector", lambda e: e.tensor_scalar(rb[:, 54:58], rb[:, 48:52], rb[:, 52:53], None, op0=ALU.is_ge), selfsync=True)
                        p.add("vector", lambda e: e.scalar_tensor_tensor(rb[:, 58:62], rb[:, 54:58], -10.0, rb[:, 48:52], op0=ALU.mult, op1=ALU.add), selfsync=True)
                        p.add("vector", lambda e: e.reduce_max(rb[:, 53:54], rb[:, 58:62], axis=AX.X), selfsync=True)
                        p.add("vector", lambda e: e.tensor_scalar(rb[:, 54:58], rb[:, 48:52], rb[:, 53:54], None, op0=ALU.is_ge), selfsync=True)
                        for jj in range(4):
                            p.add("vector", lambda e, jj=jj: e.tensor_copy(
                                rb[:, 60:76].rearrange("p (g f) -> p g f", g=4)[:, :, jj:jj + 1],
                                rb[:, 54:58].rearrange("p (g o) -> p g o", g=4)),
                                selfsync=(jj == 3))
                        p.add("vector", lambda e: e.tensor_mul(rb[:, 76:92], rb[:, 32:48], rb[:, 60:76]), selfsync=True)
                        p.add("vector", lambda e: e.tensor_copy(rb[:, 92:108], rb[:, 76:92]), selfsync=True)
                        for r in range(3):
                            p.add("vector", lambda e: e.reduce_max(rb[:, 108:109], rb[:, 92:108], axis=AX.X), selfsync=True)
                            p.add("vector", lambda e: e.tensor_scalar(rb[:, 109:125], rb[:, 92:108], rb[:, 108:109], None, op0=ALU.is_ge), selfsync=True)
                            p.add("vector", lambda e: e.scalar_tensor_tensor(rb[:, 92:108], rb[:, 109:125], -10.0, rb[:, 92:108], op0=ALU.mult, op1=ALU.add), selfsync=True)
                        p.add("vector", lambda e: e.reduce_max(rb[:, 108:109], rb[:, 92:108], axis=AX.X), selfsync=True)
                        p.add("vector", lambda e: e.tensor_scalar(rb[:, 109:125], rb[:, 76:92], rb[:, 108:109], None, op0=ALU.is_ge), selfsync=True)
                        p.add("vector", lambda e, i=i: e.tensor_mul(c_all[:, i * 16:(i + 1) * 16], rb[:, 32:48], rb[:, 109:125]),
                              selfsync=True)
                else:
                    p.add("sync", lambda e, c4=c4: e.dma_start(lnT4[:, :], lnT_scr.ap()[c4]),
                          wait=[w(s_hm)], inc=[(s_ltio, 16)])
                ltdep = [w(s_ltio)] if g > 0 else [w(cm.s_lnT)]

                for slot, e_id in enumerate(grp):
                    shared = isinstance(e_id, str)
                    for which, wg_, dst_, act in ((0, w1g, s1T, True), (1, w3g, h3T, False)):
                        for ft in range(4):
                            bank = hps[ft % 2]
                            for j in range(8):
                                p.add("tensor",
                                      lambda e, j=j, ft=ft, slot=slot, wg_=wg_, bank=bank: e.matmul(
                                          bank[:, :],
                                          wg_[:, slot * 4096 + j * 512 + ft * 128: slot * 4096 + j * 512 + ft * 128 + 128],
                                          lnT4[:, j * 512:(j + 1) * 512],
                                          start=(j == 0), stop=(j == 7)),
                                      wait=(ltdep + wdep + [relw(s_he, s_hm)]) if j == 0 else (),
                                      inc=[(s_hm, 1)] if j == 7 else ())
                            if act:
                                p.add("scalar",
                                      lambda e, ft=ft, bank=bank, dst_=dst_: e.activation(
                                          dst_[:, ft * 512:(ft + 1) * 512], bank[:, :], AF.Silu),
                                      wait=[w(s_hm), w(s_aT)], inc=[(s_he, 1)])
                            else:
                                p.add("vector",
                                      lambda e, ft=ft, bank=bank, dst_=dst_: e.tensor_copy(
                                          dst_[:, ft * 512:(ft + 1) * 512], bank[:, :]),
                                      wait=[w(s_hm), w(s_aT)], inc=[(s_he, 1)])
                    p.add("vector", lambda e: e.tensor_mul(aT[:, :], s1T[:, :], h3T[:, :]),
                          wait=[w(s_he), w(s_ym)], inc=[(s_aT, 1)])
                    for tt in range(4):
                        for ft in range(4):
                            for dh in range(2):
                                p.add("tensor",
                                      lambda e, tt=tt, ft=ft, dh=dh, slot=slot: e.matmul(
                                          y_ps[:, dh * 512:(dh + 1) * 512],
                                          aT[:, ft * 512 + tt * 128: ft * 512 + tt * 128 + 128],
                                          w2g[:, slot * 4096 + ft * 1024 + dh * 512: slot * 4096 + ft * 1024 + (dh + 1) * 512],
                                          start=(ft == 0), stop=(ft == 3)),
                                      wait=([w(s_aT), w(s_ya)]) if (ft == 0 and dh == 0) else (),
                                      inc=[(s_ym, 1)] if (ft == 3 and dh == 1) else ())
                        first = (slot == 0)
                        if shared:
                            if first:
                                p.add("vector", lambda e, tt=tt: e.tensor_copy(
                                    y_sb[:, tt * D:(tt + 1) * D], y_ps[:, :]),
                                    wait=[w(s_ym), w(s_xadd)], inc=[(s_ya, 1)])
                            else:
                                p.add("vector", lambda e, tt=tt: e.tensor_add(
                                    y_sb[:, tt * D:(tt + 1) * D], y_sb[:, tt * D:(tt + 1) * D], y_ps[:, :]),
                                    wait=[w(s_ym)], inc=[(s_ya, 1)])
                        else:
                            cap = lambda tt=tt, e_id=e_id, c4=c4: c_all[:, (c4 * 4 + tt) * 16 + e_id: (c4 * 4 + tt) * 16 + e_id + 1]
                            if first:
                                p.add("vector", lambda e, tt=tt, cap=cap: e.tensor_scalar(
                                    y_sb[:, tt * D:(tt + 1) * D], y_ps[:, :], cap(), None, op0=ALU.mult),
                                    wait=[w(s_ym), w(s_xadd)], inc=[(s_ya, 1)])
                            else:
                                p.add("vector", lambda e, tt=tt, cap=cap: e.scalar_tensor_tensor(
                                    y_sb[:, tt * D:(tt + 1) * D], y_ps[:, :], cap(),
                                    y_sb[:, tt * D:(tt + 1) * D], op0=ALU.mult, op1=ALU.add),
                                    wait=[w(s_ym)], inc=[(s_ya, 1)])

                p.add("vector", lambda e: e.tensor_add(cm.x_sb[:, :], cm.x_sb[:, :], y_sb[:, :]),
                      wait=[w(s_ya)] + xdep, inc=[(s_xadd, 1)])
                p.add("sync",
                      lambda e, c4=c4: e.dma_start(xwr[:, c4 * 4:(c4 + 1) * 4, :],
                                                   cm.x_sb[:, :].rearrange("p (k d) -> p k d", k=4)),
                      wait=[w(s_xadd)], inc=[(s_xo, 16)])

        p.add("sync", lambda e, v=s_xo.n: e.wait_ge(s_xo.h, v))


def l0_ffn_inmap(inputs, b, x):
    return {
        "x_in": np.ascontiguousarray(x[b]),
        "gwT": hT(inputs["L0_ffn_gate_w"]),
        "w1": hC(inputs["L0_ffn_w1"]),
        "w3": hC(inputs["L0_ffn_w3"]),
        "w2": hC(inputs["L0_ffn_w2"]),
        "ws1": hC(inputs["L0_ffn_ws1"]),
        "ws3": hC(inputs["L0_ffn_ws3"]),
        "ws2": hC(inputs["L0_ffn_ws2"]),
    }





def emit_L3_ffn(fz, pfx, x_in, x_out):
    nc, p, cm = fz.nc, fz.p, fz.cm
    gwT = nc.dram_tensor(pfx + "gwT", [D, 8], F32, kind="ExternalInput")
    w1 = nc.dram_tensor(pfx + "w1", [8, D, D], F32, kind="ExternalInput")
    w2 = nc.dram_tensor(pfx + "w2", [8, D, D], F32, kind="ExternalInput")
    w3 = nc.dram_tensor(pfx + "w3", [8, D, D], F32, kind="ExternalInput")
    w1s = nc.dram_tensor(pfx + "w1s", [D, D], F32, kind="ExternalInput")
    w2s = nc.dram_tensor(pfx + "w2s", [D, D], F32, kind="ExternalInput")
    w3s = nc.dram_tensor(pfx + "w3s", [D, D], F32, kind="ExternalInput")
    lnx_scr = nc.dram_tensor(pfx + "lnx_scr", [8, 128, 4096], BF16, kind="Internal")

    GROUPS = [[0, 1], [2, 3], [4, 5], [6, 7], ["sh"]]

    with contextlib.ExitStack() as stack:
        sb = lambda name, shape, dt=BF16: stack.enter_context(nc.sbuf_tensor(pfx + name, shape, dt))
        ps = lambda name, shape, dt=F32: stack.enter_context(nc.psum_tensor(pfx + name, shape, dt))
        cm.sq_scr = sb("sq_scr2", [128, D], F32)

        w1g = sb("w1g", [128, 2 * 8192])
        w3g = sb("w3g", [128, 2 * 8192])
        w2g = sb("w2g", [128, 2 * 8192])
        lnx4 = sb("lnx4", [128, 4096])
        xeT = sb("xeT", [128, 8 * 512])
        s1T = sb("s1T", [128, 8 * 512])
        h3T = sb("h3T", [128, 8 * 512])
        aT = sb("aT", [128, 8 * 512])
        diag4 = sb("diag4", [128, 4 * 128])
        c_all = sb("c_all", [128, NT * 8], F32)
        rb = sb("rb", [128, 64], F32)

        rt_ps = ps("rt_ps", [128, 8])
        xe_ps = ps("xe_ps", [128, 512])
        hps = [ps("hps0", [128, 512]), ps("hps1", [128, 512])]
        y_ps = ps("y_ps", [128, 1024])

        s_dma = p.sem("dma")
        s_wg = p.sem("wg")
        s_lxio = p.sem("lxio")
        s_rtm = p.sem("rtm")
        s_rte = p.sem("rte")
        s_dg = p.sem("dg")
        s_xem = p.sem("xem")
        s_xee = p.sem("xee")
        s_hm = p.sem("hm")
        s_he = p.sem("he")
        s_aT = p.sem("aT")
        s_ym = p.sem("ym")
        s_ya = p.sem("ya")
        s_xc = p.sem("xc")
        s_xo = p.sem("xo")

        cm.load_consts(s_dma)
        for _ge in ("tensor", "vector", "scalar"):
            p.add(_ge, None, wait=[w(s_dma)])
        gsrc = gwT.ap().rearrange("(j p) e -> p j e", p=128)
        gsb = sb("gsb", [128, 8 * 8])
        p.add("gpsimd", lambda e: e.dma_start(gsb[:, :].rearrange("p (j e) -> p j e", j=8), gsrc),
              inc=[(s_wg, 16)])

        xr = x_in.ap().rearrange("(i p) d -> p i d", p=128)
        xwr = x_out.ap().rearrange("(i p) d -> p i d", p=128)

        def wsrcs(e):
            if e == "sh":
                return (w1s.ap(), w3s.ap(), w2s.ap())
            return (w1.ap()[e], w3.ap()[e], w2.ap()[e])

        for g, grp in enumerate(GROUPS):
            for slot, e_id in enumerate(grp):
                a1, a3, a2 = wsrcs(e_id)
                war = [w(s_hm), w(s_ym)]
                for a, wg_ in ((a1, w1g), (a3, w3g), (a2, w2g)):
                    p.add("gpsimd", lambda e_, a=a, wg_=wg_, s=slot: e_.dma_start(
                        wg_[:, s * 8192:(s + 1) * 8192].rearrange("p (j f) -> p j f", j=8),
                        a.rearrange("(j p) f -> p j f", p=128)), wait=war, inc=[(s_wg, 16)])
            wdep = [w(s_wg)]

            for c4 in range(8):
                tiles = [c4 * 4 + k for k in range(4)]
                xsrc = xr if g == 0 else xwr
                p.add("sync",
                      lambda e, c4=c4, xsrc=xsrc: e.dma_start(
                          cm.x_sb[:, :].rearrange("p (k d) -> p k d", k=4),
                          xsrc[:, c4 * 4:(c4 + 1) * 4, :]),
                      wait=[w(s_ya), w(s_xo), w(s_lxio)], inc=[(s_xc, 16)])
                xdep = [w(s_xc)]

                if g == 0:
                    for k, i in enumerate(tiles):
                        cm.ln_tile(i, (lambda k=k: lnx4[:, k * D:(k + 1) * D]), deps=xdep,
                                   src_col=k * D)
                    p.add("sync", lambda e, c4=c4: e.dma_start(lnx_scr.ap()[c4], lnx4[:, :]),
                          wait=[w(cm.s_lnbf)], inc=[(s_lxio, 16)])
                    for k, i in enumerate(tiles):
                        for j in range(8):
                            p.add("tensor",
                                  lambda e, j=j, k=k: e.transpose(
                                      cm.tr_ps[:, j * 128:(j + 1) * 128],
                                      lnx4[:, k * D + j * 128: k * D + (j + 1) * 128],
                                      cm.ident[:, :]),
                                  wait=([w(cm.s_lnbf), w(cm.s_lnT), w(s_lxio)]) if j == 0 else (),
                                  inc=[(cm.s_tr, 1)] if j == 7 else ())
                        p.add("vector", lambda e: e.tensor_copy(cm.lnbf[:, 0:D], cm.tr_ps[:, :]),
                              wait=[w(cm.s_tr), w(s_rtm)], inc=[(cm.s_lnT, 1)])
                        for j in range(8):
                            p.add("tensor",
                                  lambda e, j=j: e.matmul(
                                      rt_ps[:, :], cm.lnbf[:, j * 128:(j + 1) * 128],
                                      gsb[:, j * 8:(j + 1) * 8], start=(j == 0), stop=(j == 7)),
                                  wait=([w(cm.s_lnT), w(s_wg), w(s_rte)]) if j == 0 else (),
                                  inc=[(s_rtm, 1)] if j == 7 else ())
                        p.add("vector", lambda e: e.reduce_max(rb[:, 0:1], rt_ps[:, :], axis=AX.X),
                              wait=[w(s_rtm)], selfsync=True)
                        p.add("vector", lambda e: e.tensor_scalar(rb[:, 1:2], rb[:, 0:1], -1.0, None, op0=ALU.mult),
                              selfsync=True)
                        p.add("scalar",
                              lambda e: e.activation(rb[:, 16:24], rt_ps[:, :], AF.Exp,
                                                     bias=rb[:, 1:2], accum_out=rb[:, 2:3]),
                              wait=[p.selfwait("vector")], inc=[(s_rte, 1)])
                        p.add("vector", lambda e: e.reciprocal(rb[:, 3:4], rb[:, 2:3]),
                              wait=[w(s_rte)], selfsync=True)
                        p.add("vector", lambda e: e.tensor_scalar(rb[:, 24:32], rb[:, 16:24], rb[:, 3:4], None, op0=ALU.mult),
                              selfsync=True)
                        p.add("vector", lambda e: e.reduce_max(rb[:, 4:5], rb[:, 24:32], axis=AX.X), selfsync=True)
                        p.add("vector", lambda e: e.tensor_scalar(rb[:, 32:40], rb[:, 24:32], rb[:, 4:5], None, op0=ALU.is_ge), selfsync=True)
                        p.add("vector", lambda e: e.scalar_tensor_tensor(rb[:, 40:48], rb[:, 32:40], -10.0, rb[:, 24:32], op0=ALU.mult, op1=ALU.add), selfsync=True)
                        p.add("vector", lambda e: e.reduce_max(rb[:, 5:6], rb[:, 40:48], axis=AX.X), selfsync=True)
                        p.add("vector", lambda e: e.tensor_scalar(rb[:, 32:40], rb[:, 24:32], rb[:, 5:6], None, op0=ALU.is_ge), selfsync=True)
                        p.add("vector", lambda e, i=i: e.tensor_mul(c_all[:, i * 8:(i + 1) * 8], rb[:, 24:32], rb[:, 32:40]),
                              selfsync=True)
                else:
                    p.add("sync", lambda e, c4=c4: e.dma_start(lnx4[:, :], lnx_scr.ap()[c4]),
                          wait=[w(s_xem), w(s_rtm)], inc=[(s_lxio, 16)])
                lxdep = [w(s_lxio)] if g > 0 else [w(cm.s_lnbf)]

                for slot, e_id in enumerate(grp):
                    shared = (e_id == "sh")
                    if not shared:
                        for tt in range(4):
                            cap = lambda tt=tt, e_id=e_id, c4=c4: c_all[:, (c4 * 4 + tt) * 8 + e_id: (c4 * 4 + tt) * 8 + e_id + 1]
                            p.add("vector",
                                  lambda e, tt=tt, cap=cap: e.tensor_scalar(
                                      diag4[:, tt * 128:(tt + 1) * 128], cm.ident[:, :], cap(), None, op0=ALU.mult),
                                  wait=[w(s_xem)] if tt == 0 else (),
                                  inc=[(s_dg, 1)] if tt == 3 else ())
                    for dt in range(8):
                        for tt in range(4):
                            rhs_ap = (lambda: cm.ident[:, :]) if shared else (lambda tt=tt: diag4[:, tt * 128:(tt + 1) * 128])
                            p.add("tensor",
                                  lambda e, dt=dt, tt=tt, r=rhs_ap: e.matmul(
                                      xe_ps[:, tt * 128:(tt + 1) * 128],
                                      lnx4[:, tt * D + dt * 128: tt * D + (dt + 1) * 128],
                                      r(), start=True, stop=True),
                                  wait=(lxdep + [w(s_dg), w(s_xee)]) if (dt == 0 and tt == 0)
                                       else ([w(s_xee)] if tt == 0 else ()),
                                  inc=[(s_xem, 1)] if tt == 3 else ())
                        p.add("vector", lambda e, dt=dt: e.tensor_copy(xeT[:, dt * 512:(dt + 1) * 512], xe_ps[:, :]),
                              wait=[w(s_xem), w(s_hm)], inc=[(s_xee, 1)])
                    for which, wg_, dst_, act in ((0, w1g, s1T, True), (1, w3g, h3T, False)):
                        for ft in range(8):
                            bank = hps[ft % 2]
                            for j in range(8):
                                p.add("tensor",
                                      lambda e, j=j, ft=ft, slot=slot, wg_=wg_, bank=bank: e.matmul(
                                          bank[:, :],
                                          wg_[:, slot * 8192 + j * 1024 + ft * 128: slot * 8192 + j * 1024 + ft * 128 + 128],
                                          xeT[:, j * 512:(j + 1) * 512],
                                          start=(j == 0), stop=(j == 7)),
                                      wait=(wdep + [w(s_xee), relw(s_he, s_hm)]) if j == 0 else (),
                                      inc=[(s_hm, 1)] if j == 7 else ())
                            if act:
                                p.add("scalar",
                                      lambda e, ft=ft, bank=bank, dst_=dst_: e.activation(
                                          dst_[:, ft * 512:(ft + 1) * 512], bank[:, :], AF.Silu),
                                      wait=[w(s_hm), w(s_aT)], inc=[(s_he, 1)])
                            else:
                                p.add("vector",
                                      lambda e, ft=ft, bank=bank, dst_=dst_: e.tensor_copy(
                                          dst_[:, ft * 512:(ft + 1) * 512], bank[:, :]),
                                      wait=[w(s_hm), w(s_aT)], inc=[(s_he, 1)])
                    p.add("vector", lambda e: e.tensor_mul(aT[:, :], s1T[:, :], h3T[:, :]),
                          wait=[w(s_he), w(s_ym)], inc=[(s_aT, 1)])
                    for tt in range(4):
                        for ft in range(8):
                            for dh in range(2):
                                p.add("tensor",
                                      lambda e, tt=tt, ft=ft, dh=dh, slot=slot: e.matmul(
                                          y_ps[:, dh * 512:(dh + 1) * 512],
                                          aT[:, ft * 512 + tt * 128: ft * 512 + tt * 128 + 128],
                                          w2g[:, slot * 8192 + ft * 1024 + dh * 512: slot * 8192 + ft * 1024 + (dh + 1) * 512],
                                          start=(ft == 0), stop=(ft == 7)),
                                      wait=([w(s_aT), w(s_ya)]) if (ft == 0 and dh == 0) else (),
                                      inc=[(s_ym, 1)] if (ft == 7 and dh == 1) else ())
                        p.add("vector", lambda e, tt=tt: e.tensor_add(
                            cm.x_sb[:, tt * D:(tt + 1) * D], cm.x_sb[:, tt * D:(tt + 1) * D], y_ps[:, :]),
                            wait=[w(s_ym)] + xdep, inc=[(s_ya, 1)])

                p.add("sync",
                      lambda e, c4=c4: e.dma_start(xwr[:, c4 * 4:(c4 + 1) * 4, :],
                                                   cm.x_sb[:, :].rearrange("p (k d) -> p k d", k=4)),
                      wait=[w(s_ya)], inc=[(s_xo, 16)])

        p.add("sync", lambda e, v=s_xo.n: e.wait_ge(s_xo.h, v))


def l3_ffn_inmap(inputs, b, x):
    return {
        "x_in": np.ascontiguousarray(x[b]),
        "gwT": hT(inputs["L3_ffn_gate_w"]),
        "w1": hC(inputs["L3_ffn_w1"]),
        "w2": hC(inputs["L3_ffn_w2"]),
        "w3": hC(inputs["L3_ffn_w3"]),
        "w1s": hC(inputs["L3_ffn_w1s"][0]),
        "w2s": hC(inputs["L3_ffn_w2s"][0]),
        "w3s": hC(inputs["L3_ffn_w3s"][0]),
    }





def emit_L1_ffn(fz, pfx, x_in, x_out):
    nc, p, cm = fz.nc, fz.p, fz.cm
    qwp = nc.dram_tensor(pfx + "qwp", [D, D], F32, kind="ExternalInput")
    kwp = nc.dram_tensor(pfx + "kwp", [D, D], F32, kind="ExternalInput")
    vw = nc.dram_tensor(pfx + "vw", [D, 512], F32, kind="ExternalInput")
    qbp = nc.dram_tensor(pfx + "qbp", [128, 8], F32, kind="ExternalInput")
    kbp = nc.dram_tensor(pfx + "kbp", [128, 8], F32, kind="ExternalInput")
    vbd = nc.dram_tensor(pfx + "vbd", [1, 512], F32, kind="ExternalInput")
    owp = nc.dram_tensor(pfx + "owp", [D, D], F32, kind="ExternalInput")
    obd = nc.dram_tensor(pfx + "obd", [128, 8], F32, kind="ExternalInput")
    rwd = nc.dram_tensor(pfx + "rwd", [D, 8], F32, kind="ExternalInput")
    rbd = nc.dram_tensor(pfx + "rbd", [1, 8], F32, kind="ExternalInput")
    w1 = nc.dram_tensor(pfx + "w1", [8, D, 2048], F32, kind="ExternalInput")
    b1d = nc.dram_tensor(pfx + "b1d", [8, 128, 16], F32, kind="ExternalInput")
    w2 = nc.dram_tensor(pfx + "w2", [8, 2048, D], F32, kind="ExternalInput")
    b2d = nc.dram_tensor(pfx + "b2d", [8, 1, D], F32, kind="ExternalInput")
    qT_scr = nc.dram_tensor(pfx + "qT_scr", [128, 8 * 4096], BF16, kind="Internal")
    lnx_scr = nc.dram_tensor(pfx + "lnx_scr", [8, 128, 4096], BF16, kind="Internal")
    mla_scr = nc.dram_tensor(pfx + "mla_scr", [128, 8 * 4096], BF16, kind="Internal")

    H = 16
    SC = 1.0 / math.sqrt(32.0)

    with contextlib.ExitStack() as stack:
        sb = lambda name, shape, dt=BF16: stack.enter_context(nc.sbuf_tensor(pfx + name, shape, dt))
        ps = lambda name, shape, dt=F32: stack.enter_context(nc.psum_tensor(pfx + name, shape, dt))

        bias_sb = sb("bias_sb", [128, 32], F32)
        vb_sb = sb("vb_sb", [1, 512])
        rw_sb = sb("rw_sb", [128, 64])
        rb_sb = sb("rb_sb", [1, 8])
        b2_sb = sb("b2_sb", [1, D])
        b1_sb = sb("b1_sb", [128, 16], F32)
        c_all = sb("c_all", [128, NT * 8], F32)
        rb = sb("rb", [128, 64], F32)
        ar = Arena(nc, stack, 181 * 1024, pfx=pfx)
        kT = ar.at(0, "kT", [128, 8 * 4096], BF16)
        ow_sb = ar.at(0, "ow_sb", [128, 8 * D], BF16)
        w1g = ar.at(0, "w1g", [128, 8 * 2048], BF16)
        w2g = ar.at(32, "w2g", [128, 16 * 1024], BF16)
        v_all = ar.at(64, "v_all", [128, NT * H * 33], BF16)
        h1T = ar.at(64, "h1T", [128, 16 * 512], BF16)
        mlac = ar.at(80, "mlac", [128, 8 * 512], BF16)
        lnxstage = ar.at(88, "lnxstage", [128, 4096], BF16)
        lnT4 = ar.at(97, "lnT4", [128, 4096], BF16)
        aoT = ar.at(105, "aoT", [128, 8 * 4096], BF16)
        wq_sb = ar.at(105, "wq_sb", [128, 8 * D], BF16)
        wk_sb = ar.at(121, "wk_sb", [128, 8 * D], BF16)
        wv_sb = ar.at(137, "wv_sb", [128, 8 * 512], BF16)
        cm.sq_scr = ar.at(145, "sq_scr", [128, D], F32)
        qstage = ar.at(169, "qstage", [128, 1024], BF16)
        pt = ar.at(171, "pt", [128, 1024], BF16)
        rstg = ar.at(173, "rstg", [2, 1024], F32)
        bcs = ar.at(177, "bcs", [32, 1024], F32)

        sc_ps = ps("sc_ps", [128, 1024])
        oT_ps = ps("oT_ps", [33, 1024])
        y_ps = ps("y_ps", [128, 1024])
        rt_ps = ps("rt_ps", [128, 8])

        s_dma = p.sem("dma")
        s_wgt = p.sem("wgt")
        s_qkvm = p.sem("qkvm")
        s_qkve = p.sem("qkve")
        s_scio = p.sem("scio")
        s_scm = p.sem("scm")
        s_exp = p.sem("exp")
        s_om = p.sem("om")
        s_nrm = p.sem("nrm")
        s_bc = p.sem("bc")
        s_qst = p.sem("qst")
        s_mlm = p.sem("mlm")
        s_mle = p.sem("mle")
        s_rtm = p.sem("rtm")
        s_rte = p.sem("rte")
        s_wg2 = p.sem("wg2")
        s_hm = p.sem("hm")
        s_he = p.sem("he")
        s_ym = p.sem("ym")
        s_ya = p.sem("ya")
        s_xc = p.sem("xc")
        s_xo = p.sem("xo")
        s_lnst = p.sem("lnst")
        s_mcl = p.sem("mcl")

        cm.load_consts(s_dma)
        for _ge in ("tensor", "vector", "scalar"):
            p.add(_ge, None, wait=[w(s_dma)])
        for wdr, wsb_ in ((qwp, wq_sb), (kwp, wk_sb)):
            wsrc = wdr.ap().rearrange("(j p) c -> p j c", p=128)
            p.add("gpsimd", lambda e, a=wsb_, b=wsrc: e.dma_start(
                a[:, :].rearrange("p (j c) -> p j c", j=8), b), inc=[(s_wgt, 16)])
        p.add("gpsimd", lambda e: e.dma_start(
            wv_sb[:, :].rearrange("p (j c) -> p j c", j=8),
            vw.ap().rearrange("(j p) c -> p j c", p=128)), inc=[(s_wgt, 16)])
        p.add("gpsimd", lambda e: e.dma_start(bias_sb[:, 0:8], qbp.ap()[:, :]), inc=[(s_wgt, 16)])
        p.add("gpsimd", lambda e: e.dma_start(bias_sb[:, 8:16], kbp.ap()[:, :]), inc=[(s_wgt, 16)])
        p.add("gpsimd", lambda e: e.dma_start(bias_sb[:, 16:24], obd.ap()[:, :]), inc=[(s_wgt, 16)])
        p.add("gpsimd", lambda e: e.dma_start(vb_sb[:, :], vbd.ap()[:, :]), inc=[(s_wgt, 16)])
        p.add("gpsimd", lambda e: e.dma_start(
            rw_sb[:, :].rearrange("p (j c) -> p j c", j=8),
            rwd.ap().rearrange("(j p) c -> p j c", p=128)), inc=[(s_wgt, 16)])
        p.add("gpsimd", lambda e: e.dma_start(rb_sb[:, :], rbd.ap()[:, :]), inc=[(s_wgt, 16)])
        p.add("vector", lambda e: e.tensor_scalar(bias_sb[:, 0:8], bias_sb[:, 0:8], SC, None, op0=ALU.mult),
              wait=[w(s_wgt)], selfsync=True)
        p.add("vector", lambda e: e.memset(v_all[:, :], 1.0))

        xr = x_in.ap().rearrange("(i p) d -> p i d", p=128)
        xwr = x_out.ap().rearrange("(i p) d -> p i d", p=128)

        for c4 in range(8):
            tiles = [c4 * 4 + k for k in range(4)]
            p.add("sync",
                  lambda e, c4=c4: e.dma_start(
                      cm.x_sb[:, :].rearrange("p (k d) -> p k d", k=4),
                      xr[:, c4 * 4:(c4 + 1) * 4, :]),
                  wait=[w(cm.s_lnbf), w(s_lnst)], inc=[(s_xc, 16)])
            xdep = [w(s_xc)]
            for k, i in enumerate(tiles):
                slot = i % 2
                cm.ln_tile(i, (lambda s=slot: cm.lnbf[:, s * D:(s + 1) * D]), deps=xdep,
                           src_col=k * D, out_wait=[w(s_lnst)])
                p.add("sync", lambda e, i=i, s=slot: e.dma_start(
                    lnx_scr.ap()[i // 4][:, (i % 4) * D:((i % 4) + 1) * D],
                    cm.lnbf[:, s * D:(s + 1) * D]),
                    wait=[w(cm.s_lnbf)], inc=[(s_lnst, 16)])
                dst = lambda k=k: lnT4[:, :].rearrange("p (j t) -> p j t", j=8)[:, :, k * 128:(k + 1) * 128]
                cm.transpose_tile((lambda s=slot: cm.lnbf[:, s * D:(s + 1) * D]),
                                  dst, wait_src=[w(cm.s_lnbf)],
                                  evac_wait=[w(s_qkvm)])
            lnT_ready = [w(cm.s_lnT)]
            for which, wsb_ in (("q", wq_sb), ("k", wk_sb)):
                for ct in range(8):
                    bank = lambda ct=ct: sc_ps[:, (ct % 2) * 512:((ct % 2) + 1) * 512]
                    for j in range(8):
                        p.add("tensor",
                              lambda e, j=j, ct=ct, wsb_=wsb_, bank=bank: e.matmul(
                                  bank(), wsb_[:, j * D + ct * 128: j * D + ct * 128 + 128],
                                  lnT4[:, j * 512:(j + 1) * 512],
                                  start=(j == 0), stop=(j == 7)),
                              wait=(lnT_ready + [w(s_wgt), relw(s_qkve, s_qkvm)]) if j == 0 else (),
                              inc=[(s_qkvm, 1)] if j == 7 else ())
                    if which == "q":
                        p.add("vector",
                              lambda e, ct=ct, bank=bank: e.tensor_scalar(
                                  pt[:, 0:512], bank(), SC, bias_sb[:, ct:ct + 1],
                                  op0=ALU.mult, op1=ALU.add),
                              wait=[w(s_qkvm), w(s_scio), w(s_exp)], inc=[(s_qkve, 1)])
                        p.add("sync", lambda e, ct=ct, c4=c4: e.dma_start(
                            qT_scr.ap()[:, ct * 4096 + c4 * 512: ct * 4096 + (c4 + 1) * 512],
                            pt[:, 0:512]),
                            wait=[w(s_qkve)], inc=[(s_scio, 16)])
                    else:
                        p.add("vector",
                              lambda e, ct=ct, bank=bank, c4=c4: e.tensor_scalar(
                                  kT[:, ct * 4096 + c4 * 512: ct * 4096 + (c4 + 1) * 512],
                                  bank(), bias_sb[:, 8 + ct:9 + ct], None, op0=ALU.add),
                              wait=[w(s_qkvm)], inc=[(s_qkve, 1)])
            for k in range(4):
                bank = lambda k=k: sc_ps[:, (k % 2) * 512:((k % 2) + 1) * 512]
                for j in range(8):
                    p.add("tensor",
                          lambda e, j=j, k=k, bank=bank: e.matmul(
                              bank(), lnT4[:, j * 512 + k * 128: j * 512 + k * 128 + 128],
                              wv_sb[:, j * 512:(j + 1) * 512],
                              start=(j == 0), stop=False),
                          wait=([relw(s_qkve, s_qkvm)]) if j == 0 else ())
                p.add("tensor",
                      lambda e, bank=bank: e.matmul(
                          bank(), cm.ones1[0:1, 0:128], vb_sb[:, :],
                          start=False, stop=True),
                      inc=[(s_qkvm, 1)])
                def evv(e, i=c4 * 4 + k, bank=bank):
                    dstv = v_all[:, :].rearrange("p (i h c) -> p i h c", i=NT, h=H)[:, i, :, 0:32]
                    return e.tensor_copy(dstv, bank().rearrange("p (h c) -> p h c", h=16))
                p.add("vector", lambda e, f=evv: f(e),
                      wait=[w(s_qkvm), w(s_om)], inc=[(s_qkve, 1)])

        p.add("vector", lambda e: e.memset(aoT[32:64, :], 0.0),
              wait=[w(s_qkvm), w(s_qkve)], inc=[(s_nrm, 1)])
        p.add("vector", lambda e: e.memset(aoT[96:128, :], 0.0), inc=[(s_nrm, 1)])
        for h in range(H):
            g, pb = h // 2, (h % 2) * 64
            for tqc in range(4):
                p.add("sync", lambda e, g=g, pb=pb, tqc=tqc: e.dma_start(
                    qstage[pb:pb + 64, :],
                    qT_scr.ap()[pb:pb + 64, g * 4096 + tqc * 1024: g * 4096 + (tqc + 1) * 1024]),
                    wait=[w(s_scm), w(s_scio)], inc=[(s_qst, 16)])
                for j in range(NT):
                    for half in range(2):
                        p.add("tensor",
                              lambda e, j=j, half=half, g=g, pb=pb: e.matmul(
                                  sc_ps[:, half * 512:(half + 1) * 512],
                                  kT[pb:pb + 64, g * 4096 + j * 128: g * 4096 + (j + 1) * 128],
                                  qstage[pb:pb + 64, half * 512:(half + 1) * 512],
                                  start=True, stop=True),
                              wait=([w(s_exp), w(s_qst), w(s_qkve)]) if (half == 0) else (),
                              inc=[(s_scm, 1)] if half == 1 else ())
                    p.add("scalar",
                          lambda e: e.activation(pt[:, :], sc_ps[:, :], AF.Exp, bias=cm.cb(-12.0)),
                          wait=[w(s_scm), w(s_om), w(s_scio)], inc=[(s_exp, 1)])
                    for half in range(2):
                        p.add("tensor",
                              lambda e, j=j, h=h, half=half, first=(j == 0), last=(j == NT - 1): e.matmul(
                                  oT_ps[:, half * 512:(half + 1) * 512],
                                  v_all[:, j * H * 33 + h * 33: j * H * 33 + (h + 1) * 33],
                                  pt[:, half * 512:(half + 1) * 512], start=first, stop=last),
                              wait=([w(s_exp)] + ([w(s_nrm)] if j == 0 else [])) if half == 0 else (),
                              inc=[(s_om, 1)] if half == 1 else ())
                p.add("vector", lambda e: e.reciprocal(rstg[0:1, :], oT_ps[32:33, :]),
                      wait=[w(s_om), w(s_bc)], selfsync=True)
                p.add("vector", lambda e: e.tensor_copy(pt[0:1, :], rstg[0:1, :]),
                      selfsync=True)
                for half in range(2):
                    p.add("tensor", lambda e, half=half: e.matmul(
                        y_ps[0:32, half * 512:(half + 1) * 512], cm.ones1[0:1, 0:32],
                        pt[0:1, half * 512:(half + 1) * 512], start=True, stop=True),
                        wait=[p.selfwait("vector"), w(s_bc)] if half == 0 else (),
                        inc=[(s_mlm, 1)] if half == 1 else ())
                p.add("vector", lambda e: e.tensor_copy(bcs[:, :], y_ps[0:32, :]),
                      wait=[w(s_mlm)], selfsync=True)
                def evo(e, pb=pb, g=g, tqc=tqc):
                    return e.tensor_mul(
                        aoT[pb:pb + 32, g * 4096 + tqc * 1024: g * 4096 + (tqc + 1) * 1024],
                        bcs[0:32, :], oT_ps[0:32, :])
                p.add("vector", lambda e, f=evo: f(e), inc=[(s_nrm, 1)])
                p.add("vector", lambda e: e.engine_nop(), inc=[(s_bc, 1)])

        p.add("gpsimd", lambda e: e.dma_start(
            ow_sb[:, :].rearrange("p (j c) -> p j c", j=8),
            owp.ap().rearrange("(j p) c -> p j c", p=128)),
            wait=[w(s_qkvm), w(s_scm)], inc=[(s_wgt, 16)])
        nml = 0
        for tqc8 in range(8):
            for dt in range(8):
                bank = lambda dt=dt: sc_ps[:, (dt % 2) * 512:((dt % 2) + 1) * 512]
                for j in range(8):
                    p.add("tensor",
                          lambda e, j=j, dt=dt, tqc8=tqc8, bank=bank: e.matmul(
                              bank(), ow_sb[:, j * D + dt * 128: j * D + dt * 128 + 128],
                              aoT[:, j * 4096 + tqc8 * 512: j * 4096 + (tqc8 + 1) * 512],
                              start=(j == 0), stop=(j == 7)),
                          wait=([w(s_nrm), w(s_wgt), w(s_exp), (s_mle, max(0, s_mle.n - 1))]) if j == 0 else (),
                          inc=[(s_mlm, 1)] if j == 7 else ())
                nml += 1
                p.add("vector",
                      lambda e, dt=dt, bank=bank: e.tensor_scalar(
                          pt[:, 0:512], bank(), bias_sb[:, 16 + dt:17 + dt], None, op0=ALU.add),
                      wait=[w(s_mlm), w(s_scio), w(s_exp)], inc=[(s_mle, 1)])
                p.add("sync", lambda e, dt=dt, tqc8=tqc8: e.dma_start(
                    mla_scr.ap()[:, dt * 4096 + tqc8 * 512: dt * 4096 + (tqc8 + 1) * 512],
                    pt[:, 0:512]),
                    wait=[w(s_mle)], inc=[(s_scio, 16)])

        for e_id in range(8):
            war = [w(s_hm), w(s_ym), w(s_scm), w(s_mlm)]
            p.add("gpsimd", lambda e, e_id=e_id: e.dma_start(
                w1g[:, :].rearrange("p (j f) -> p j f", j=8),
                w1.ap()[e_id].rearrange("(j p) f -> p j f", p=128)),
                wait=war, inc=[(s_wg2, 16)])
            p.add("gpsimd", lambda e, e_id=e_id: e.dma_start(
                w2g[:, :].rearrange("p (ft d) -> p ft d", ft=16),
                w2.ap()[e_id].rearrange("(ft p) d -> p ft d", p=128)),
                wait=war, inc=[(s_wg2, 16)])
            p.add("gpsimd", lambda e, e_id=e_id: e.dma_start(b1_sb[:, :], b1d.ap()[e_id]),
                  wait=war, inc=[(s_wg2, 16)])
            p.add("gpsimd", lambda e, e_id=e_id: e.dma_start(b2_sb[:, :], b2d.ap()[e_id]),
                  wait=war, inc=[(s_wg2, 16)])
            for tqc8 in range(8):
                p.add("sync", lambda e, tqc8=tqc8: e.dma_start(
                    mlac[:, :].rearrange("p (dt t) -> p dt t", dt=8),
                    mla_scr.ap().rearrange("p (dt t) -> p dt t", dt=8)[:, :, tqc8 * 512:(tqc8 + 1) * 512]),
                    wait=[w(s_hm), w(s_rtm), w(s_scio), w(s_om)], inc=[(s_mcl, 16)])
                xsrc = xr if e_id == 0 else xwr
                p.add("sync",
                      lambda e, tqc8=tqc8, xsrc=xsrc: e.dma_start(
                          cm.x_sb[:, :].rearrange("p (k d) -> p k d", k=4),
                          xsrc[:, tqc8 * 4:(tqc8 + 1) * 4, :]),
                      wait=[w(s_ya), w(s_xo)], inc=[(s_xc, 16)])
                if e_id == 0:
                    p.add("sync", lambda e, tqc8=tqc8: e.dma_start(lnxstage[:, :], lnx_scr.ap()[tqc8]),
                          wait=[w(s_ya), w(s_om)], inc=[(s_xc, 16)])
                    p.add("vector", lambda e: e.tensor_add(cm.x_sb[:, :], cm.x_sb[:, :], lnxstage[:, :]),
                          wait=[w(s_xc)], selfsync=True)
                    for tt in range(4):
                        for j in range(8):
                            p.add("tensor",
                                  lambda e, j=j, tt=tt: e.matmul(
                                      rt_ps[:, :], mlac[:, j * 512 + tt * 128: j * 512 + tt * 128 + 128],
                                      rw_sb[:, j * 8:(j + 1) * 8], start=(j == 0), stop=False),
                                  wait=([w(s_mcl), w(s_wgt), w(s_rte)]) if j == 0 else ())
                        p.add("tensor",
                              lambda e: e.matmul(rt_ps[:, :], cm.ones1[0:1, 0:128], rb_sb[:, :],
                                                 start=False, stop=True),
                              inc=[(s_rtm, 1)])
                        p.add("vector", lambda e: e.reduce_max(rb[:, 0:1], rt_ps[:, :], axis=AX.X),
                              wait=[w(s_rtm)], selfsync=True)
                        p.add("vector", lambda e: e.tensor_scalar(rb[:, 1:2], rb[:, 0:1], -1.0, None, op0=ALU.mult),
                              selfsync=True)
                        p.add("scalar",
                              lambda e: e.activation(rb[:, 16:24], rt_ps[:, :], AF.Exp,
                                                     bias=rb[:, 1:2], accum_out=rb[:, 2:3]),
                              wait=[p.selfwait("vector")], inc=[(s_rte, 1)])
                        p.add("vector", lambda e: e.reciprocal(rb[:, 3:4], rb[:, 2:3]),
                              wait=[w(s_rte)], selfsync=True)
                        p.add("vector", lambda e: e.tensor_scalar(rb[:, 24:32], rb[:, 16:24], rb[:, 3:4], None, op0=ALU.mult),
                              selfsync=True)
                        p.add("vector", lambda e: e.reduce_max(rb[:, 4:5], rb[:, 24:32], axis=AX.X), selfsync=True)
                        p.add("vector", lambda e: e.tensor_scalar(rb[:, 32:40], rb[:, 24:32], rb[:, 4:5], None, op0=ALU.is_ge), selfsync=True)
                        p.add("vector", lambda e: e.scalar_tensor_tensor(rb[:, 40:48], rb[:, 32:40], -10.0, rb[:, 24:32], op0=ALU.mult, op1=ALU.add), selfsync=True)
                        p.add("vector", lambda e: e.reduce_max(rb[:, 5:6], rb[:, 40:48], axis=AX.X), selfsync=True)
                        p.add("vector", lambda e: e.tensor_scalar(rb[:, 32:40], rb[:, 24:32], rb[:, 5:6], None, op0=ALU.is_ge), selfsync=True)
                        p.add("vector", lambda e: e.tensor_add(rb[:, 6:7], rb[:, 4:5], rb[:, 5:6]), selfsync=True)
                        p.add("vector", lambda e: e.tensor_scalar(rb[:, 7:8], rb[:, 6:7], 1e-8, None, op0=ALU.add), selfsync=True)
                        p.add("vector", lambda e: e.reciprocal(rb[:, 8:9], rb[:, 7:8]), selfsync=True)
                        p.add("vector", lambda e: e.tensor_mul(rb[:, 40:48], rb[:, 24:32], rb[:, 32:40]), selfsync=True)
                        p.add("vector", lambda e, tt=tt, tqc8=tqc8: e.tensor_scalar(
                            c_all[:, (tqc8 * 4 + tt) * 8:(tqc8 * 4 + tt + 1) * 8],
                            rb[:, 40:48], rb[:, 8:9], 0.3, op0=ALU.mult, op1=ALU.mult),
                            selfsync=True)
                for ft in range(16):
                    bank = lambda ft=ft: sc_ps[:, (ft % 2) * 512:((ft % 2) + 1) * 512]
                    for j in range(8):
                        p.add("tensor",
                              lambda e, j=j, ft=ft, bank=bank: e.matmul(
                                  bank(), w1g[:, j * 2048 + ft * 128: j * 2048 + ft * 128 + 128],
                                  mlac[:, j * 512:(j + 1) * 512],
                                  start=(j == 0), stop=(j == 7)),
                              wait=([w(s_mcl), w(s_wg2), w(s_rtm), w(s_mle),
                                     relw(s_he, s_hm)]) if j == 0 else (),
                              inc=[(s_hm, 1)] if j == 7 else ())
                    p.add("scalar",
                          lambda e, ft=ft, bank=bank: e.activation(
                              h1T[:, ft * 512:(ft + 1) * 512], bank(), AF.Gelu,
                              bias=b1_sb[:, ft:ft + 1]),
                          wait=[w(s_hm), w(s_ym), w(s_om)], inc=[(s_he, 1)])
                for tt in range(4):
                    for ft in range(16):
                        for dh in range(2):
                            p.add("tensor",
                                  lambda e, tt=tt, ft=ft, dh=dh: e.matmul(
                                      y_ps[:, dh * 512:(dh + 1) * 512],
                                      h1T[:, ft * 512 + tt * 128: ft * 512 + tt * 128 + 128],
                                      w2g[:, ft * 1024 + dh * 512: ft * 1024 + (dh + 1) * 512],
                                      start=(ft == 0), stop=False),
                                  wait=([w(s_he), w(s_ya), w(s_bc), w(s_mlm)]) if (ft == 0 and dh == 0) else ())
                    for dh in range(2):
                        p.add("tensor",
                              lambda e, dh=dh: e.matmul(
                                  y_ps[:, dh * 512:(dh + 1) * 512],
                                  cm.ones1[0:1, 0:128], b2_sb[:, dh * 512:(dh + 1) * 512],
                                  start=False, stop=True),
                              inc=[(s_ym, 1)] if dh == 1 else ())
                    cap = lambda tt=tt, e_id=e_id, tqc8=tqc8: c_all[:, (tqc8 * 4 + tt) * 8 + e_id: (tqc8 * 4 + tt) * 8 + e_id + 1]
                    p.add("vector", lambda e, tt=tt, cap=cap: e.scalar_tensor_tensor(
                        cm.x_sb[:, tt * D:(tt + 1) * D], y_ps[:, :], cap(),
                        cm.x_sb[:, tt * D:(tt + 1) * D], op0=ALU.mult, op1=ALU.add),
                        wait=[w(s_ym), w(s_xc)], inc=[(s_ya, 1)])
                p.add("sync",
                      lambda e, tqc8=tqc8: e.dma_start(xwr[:, tqc8 * 4:(tqc8 + 1) * 4, :],
                                                       cm.x_sb[:, :].rearrange("p (k d) -> p k d", k=4)),
                      wait=[w(s_ya)], inc=[(s_xo, 16)])

        p.add("sync", lambda e, v=s_xo.n: e.wait_ge(s_xo.h, v))


def l1_ffn_inmap(inputs, b, x):
    qw = np.asarray(inputs["L1_ffn_q_w"], np.float32)
    kw = np.asarray(inputs["L1_ffn_k_w"], np.float32)
    qb = np.asarray(inputs["L1_ffn_q_b"], np.float32)
    kb = np.asarray(inputs["L1_ffn_k_b"], np.float32)
    ow = np.asarray(inputs["L1_ffn_o_w"], np.float32)
    obv = np.asarray(inputs["L1_ffn_o_b"], np.float32)
    qwp = np.zeros((1024, 1024), np.float32)
    kwp = np.zeros((1024, 1024), np.float32)
    qbp = np.zeros((1024,), np.float32)
    kbp = np.zeros((1024,), np.float32)
    owp = np.zeros((1024, 1024), np.float32)
    for h in range(16):
        dstc = (h // 2) * 128 + (h % 2) * 64
        qwp[:, dstc:dstc + 32] = qw[:, h * 64:h * 64 + 32]
        kwp[:, dstc:dstc + 32] = kw[:, h * 32:h * 32 + 32]
        qbp[dstc:dstc + 32] = qb[h * 64:h * 64 + 32]
        kbp[dstc:dstc + 32] = kb[h * 32:h * 32 + 32]
        owp[dstc:dstc + 32, :] = ow[h * 32:h * 32 + 32, :]
    return {
        "x_in": np.ascontiguousarray(x[b]),
        "qwp": qwp, "kwp": kwp,
        "vw": hC(inputs["L1_ffn_v_w"]),
        "qbp": np.ascontiguousarray(qbp.reshape(8, 128).T),
        "kbp": np.ascontiguousarray(kbp.reshape(8, 128).T),
        "vbd": np.asarray(inputs["L1_ffn_v_b"], np.float32).reshape(1, 512),
        "owp": owp,
        "obd": np.ascontiguousarray(obv.reshape(8, 128).T),
        "rwd": hC(inputs["L1_ffn_router_w"]),
        "rbd": np.asarray(inputs["L1_ffn_router_b"], np.float32).reshape(1, 8),
        "w1": hC(inputs["L1_ffn_w1"]),
        "b1d": np.ascontiguousarray(np.asarray(inputs["L1_ffn_b1"], np.float32).reshape(8, 16, 128).transpose(0, 2, 1)),
        "w2": hC(inputs["L1_ffn_w2"]),
        "b2d": np.asarray(inputs["L1_ffn_b2"], np.float32).reshape(8, 1, 1024),
    }





def _alibi_slopes(n):
    def pow2(n):
        start = 2 ** (-2 ** (-(math.log2(n) - 3)))
        return [start * start ** i for i in range(n)]
    if math.log2(n).is_integer():
        return pow2(n)
    c = 2 ** math.floor(math.log2(n))
    return pow2(c) + _alibi_slopes(2 * c)[0::2][:n - c]


def emit_L2_attn(fz, pfx, x_in, x_out, upto=99, debug=False):
    nc, p, cm = fz.nc, fz.p, fz.cm
    wqkv = nc.dram_tensor(pfx + "wqkv", [D, 3 * D], F32, kind="ExternalInput")
    bqkv2 = nc.dram_tensor(pfx + "bqkv2", [128, 16], F32, kind="ExternalInput")
    bv_row = nc.dram_tensor(pfx + "bv_row", [1, D], F32, kind="ExternalInput")
    wproj = nc.dram_tensor(pfx + "wproj", [D, D], F32, kind="ExternalInput")
    bp_row = nc.dram_tensor(pfx + "bp_row", [1, D], F32, kind="ExternalInput")
    qT_scr = nc.dram_tensor(pfx + "qT_scr", [128, 8 * 4096], BF16, kind="Internal")
    aoT_scr = nc.dram_tensor(pfx + "aoT_scr", [128, 8 * 4096], BF16, kind="Internal")
    lnT_scr = nc.dram_tensor(pfx + "lnT_scr", [8, 128, 4096], BF16, kind="Internal")

    H = 16
    slopes = _alibi_slopes(H)
    CB = 30.0
    ab = np.zeros((128, H * 32), np.float32)
    for h in range(H):
        for dlt in range(32):
            ab[:, h * 32 + dlt] = slopes[h] * np.arange(128) - slopes[h] * 128.0 * dlt - CB
    dmax = [min(31, int(math.ceil(150.0 / (128.0 * s)))) for s in slopes]
    dmc = np.zeros((128, H * 128), np.float32)
    tkv = np.arange(128)[:, None]
    tqv = np.arange(128)[None, :]
    for h in range(H):
        dmc[:, h * 128:(h + 1) * 128] = np.where(tkv > tqv, -8.0 * slopes[h] * (tkv - tqv), 0.0)
    qxc = np.zeros((128, 8 * 512), np.float32)
    for h in range(H):
        qxc[(h % 2) * 64, (h // 2) * 512:(h // 2 + 1) * 512] = -8.0 * slopes[h] * (np.arange(512) % 128)

    if debug:
        dbg_kT = nc.dram_tensor("dbg_kT", [128, 8 * 4096], F32, kind="ExternalOutput")
        dbg_v = nc.dram_tensor("dbg_v", [128, 4 * H * 65], F32, kind="ExternalOutput")
        dbg_pt = nc.dram_tensor("dbg_pt", [128, 512], F32, kind="ExternalOutput")
        dbg_ao = nc.dram_tensor("dbg_ao", [128, 512], F32, kind="ExternalOutput")
    with contextlib.ExitStack() as stack:
        cm.cb(-30.0)
        sb = lambda name, shape, dt=BF16: stack.enter_context(nc.sbuf_tensor(pfx + name, shape, dt))
        ps = lambda name, shape, dt=F32: stack.enter_context(nc.psum_tensor(pfx + name, shape, dt))
        cm.sq_scr = sb("sq_scr2", [128, D], F32)

        kT = sb("kT", [128, 8 * 4096])
        v_all = sb("v_all", [128, NT * H * 65])
        ar2 = Arena(nc, stack, 16 * 1024, name="arena2", pfx=pfx)
        lnT4 = ar2.at(0, "lnT4", [128, 4096], BF16)
        qstage = sb("qstage", [128, 512])
        pt = sb("pt", [128, 512])
        ostage = sb("ostage", [128, 512])
        rbf = sb("rbf", [128, 512])
        wsb = sb("wsb", [128, 8 * 1024])
        aoc = sb("aoc", [128, 8 * 128])
        x1 = sb("x1", [128, 1024], F32)
        rstg = sb("rstg", [128, 512], F32)
        bcs = sb("bcs", [128, 512], F32)
        dmscr = sb("dmscr", [128, 128], F32)
        bias_sb = sb("bias_sb", [128, 16], F32)
        alcst = sb("alcst", [128, H * 32], F32)
        dmcs = ar2.at(0, "dmcs", [128, H * 128], F32)
        qxs = ar2.at(8, "qxs", [128, 8 * 512], BF16)
        ones_a = sb("ones_a", [128, 128])
        bvr = sb("bvr", [1, D])
        bpr = sb("bpr", [1, D])

        qk_ps = [ps("qk_ps0", [128, 512]), ps("qk_ps1", [128, 512])]
        sc_ps = ps("sc_ps", [128, 512])
        oT_ps = ps("oT_ps", [128, 512])
        rs_ps = ps("rs_ps", [128, 512])
        pr_ps = ps("pr_ps", [128, 1024])

        abd = nc.inline_tensor(ab, name=pfx + "alibic")
        dmd = nc.inline_tensor(dmc, name=pfx + "dmcc")
        qxd = nc.inline_tensor(qxc.astype(bfdt), name=pfx + "qxcc")

        s_dma = p.sem("dma")
        s_wgt = p.sem("wgt")
        s_qkvm = p.sem("qkvm")
        s_qkve = p.sem("qkve")
        s_scio = p.sem("scio")
        s_ltio = p.sem("ltio")
        s_qst = p.sem("qst")
        s_scm = p.sem("scm")
        s_msk = p.sem("msk")
        s_exp = p.sem("exp")
        s_om = p.sem("om")
        s_nrm = p.sem("nrm")
        s_bcm = p.sem("bcm")
        s_aocl = p.sem("aocl")
        s_prm = p.sem("prm")
        s_res = p.sem("res")
        s_xc = p.sem("xc")
        s_xo = p.sem("xo")

        cm.load_consts(s_dma)
        p.add("sync", lambda e: e.dma_start(alcst[:, :], abd.ap()[:, :]), inc=[(s_dma, 16)])
        p.add("gpsimd", lambda e: e.dma_start(bias_sb[:, :], bqkv2.ap()[:, :]), inc=[(s_wgt, 16)])
        p.add("gpsimd", lambda e: e.dma_start(bvr[:, :], bv_row.ap()[:, :]), inc=[(s_wgt, 16)])
        p.add("gpsimd", lambda e: e.dma_start(bpr[:, :], bp_row.ap()[:, :]), inc=[(s_wgt, 16)])
        p.add("vector", lambda e: e.memset(v_all[:, :], 1.0))
        p.add("vector", lambda e: e.memset(ones_a[:, :], 1.0))

        xr = x_in.ap().rearrange("(i p) d -> p i d", p=128)
        xwr = x_out.ap().rearrange("(i p) d -> p i d", p=128)

        for c4 in range(8):
            tiles = [c4 * 4 + k for k in range(4)]
            p.add("sync",
                  lambda e, c4=c4: e.dma_start(
                      cm.x_sb[:, :].rearrange("p (k d) -> p k d", k=4),
                      xr[:, c4 * 4:(c4 + 1) * 4, :]),
                  wait=[w(cm.s_lnbf)], inc=[(s_xc, 16)])
            xdep = [w(s_xc)]
            for k, i in enumerate(tiles):
                slot = i % 2
                cm.ln_tile(i, (lambda s=slot: cm.lnbf[:, s * D:(s + 1) * D]), deps=xdep,
                           src_col=k * D)
                dst = lambda k=k: lnT4[:, :].rearrange("p (j t) -> p j t", j=8)[:, :, k * 128:(k + 1) * 128]
                cm.transpose_tile((lambda s=slot: cm.lnbf[:, s * D:(s + 1) * D]),
                                  dst, wait_src=[w(cm.s_lnbf)],
                                  evac_wait=[w(s_ltio)])
            p.add("sync", lambda e, c4=c4: e.dma_start(lnT_scr.ap()[c4], lnT4[:, :]),
                  wait=[w(cm.s_lnT)], inc=[(s_ltio, 16)])

        for which, cols0 in (("q", 0), ("k", D), ("v", 2 * D)):
            p.add("gpsimd", lambda e, cols0=cols0: e.dma_start(
                wsb[:, :].rearrange("p (j c) -> p j c", j=8),
                wqkv.ap().rearrange("(j p) c -> p j c", p=128)[:, :, cols0:cols0 + D]),
                wait=[w(s_qkvm)], inc=[(s_wgt, 16)])
            wdep = [w(s_wgt)]
            for c4 in range(8):
                p.add("sync", lambda e, c4=c4: e.dma_start(lnT4[:, :], lnT_scr.ap()[c4]),
                      wait=[w(s_qkvm), w(s_ltio), w(cm.s_lnT)], inc=[(s_ltio, 16)])
                ldep = [w(s_ltio)]
                if which in ("q", "k"):
                    for ct in range(8):
                        bank = qk_ps[ct % 2]
                        for j in range(8):
                            p.add("tensor",
                                  lambda e, j=j, ct=ct, bank=bank: e.matmul(
                                      bank[:, :], wsb[:, j * D + ct * 128: j * D + ct * 128 + 128],
                                      lnT4[:, j * 512:(j + 1) * 512],
                                      start=(j == 0), stop=(j == 7)),
                                  wait=(wdep + ldep + [relw(s_qkve, s_qkvm)]) if j == 0 else (),
                                  inc=[(s_qkvm, 1)] if j == 7 else ())
                        if which == "q":
                            p.add("vector",
                                  lambda e, ct=ct, bank=bank: e.tensor_scalar(
                                      pt[:, 0:512], bank[:, :], bias_sb[:, ct:ct + 1], None, op0=ALU.add),
                                  wait=[w(s_qkvm), w(s_scio)], inc=[(s_qkve, 1)])
                            p.add("sync", lambda e, ct=ct, c4=c4: e.dma_start(
                                qT_scr.ap()[:, ct * 4096 + c4 * 512: ct * 4096 + (c4 + 1) * 512],
                                pt[:, 0:512]),
                                wait=[w(s_qkve)], inc=[(s_scio, 16)])
                        else:
                            p.add("vector",
                                  lambda e, ct=ct, bank=bank, c4=c4: e.tensor_scalar(
                                      kT[:, ct * 4096 + c4 * 512: ct * 4096 + (c4 + 1) * 512],
                                      bank[:, :], bias_sb[:, 8 + ct:9 + ct], None, op0=ALU.add),
                                  wait=[w(s_qkvm)], inc=[(s_qkve, 1)])
                else:
                    for k in range(4):
                        for half in range(2):
                            bank = qk_ps[half]
                            for j in range(8):
                                p.add("tensor",
                                      lambda e, j=j, k=k, half=half, bank=bank: e.matmul(
                                          bank[:, :],
                                          lnT4[:, j * 512 + k * 128: j * 512 + k * 128 + 128],
                                          wsb[:, j * D + half * 512: j * D + (half + 1) * 512],
                                          start=(j == 0), stop=False),
                                      wait=(wdep + ldep + [relw(s_qkve, s_qkvm)]) if j == 0 else ())
                            p.add("tensor",
                                  lambda e, half=half, bank=bank: e.matmul(
                                      bank[:, :], cm.ones1[0:1, 0:128],
                                      bvr[:, half * 512:(half + 1) * 512],
                                      start=False, stop=True),
                                  inc=[(s_qkvm, 1)])
                            def evv(e, i=c4 * 4 + k, half=half, bank=bank):
                                dstv = v_all[:, :].rearrange("p (i h c) -> p i h c", i=NT, h=H)[
                                    :, i, half * 8:(half + 1) * 8, 0:64]
                                return e.tensor_copy(dstv, bank[:, :].rearrange("p (h c) -> p h c", h=8))
                            p.add("vector", lambda e, f=evv: f(e),
                                  wait=[w(s_qkvm), w(s_om)], inc=[(s_qkve, 1)])

        if upto not in (24, 26):
            p.add("sync", lambda e: e.dma_start(dmcs[:, :], dmd.ap()[:, :]),
                  wait=[w(s_qkvm), w(s_ltio)], inc=[(s_dma, 16)])
            p.add("sync", lambda e: e.dma_start(qxs[:, :], qxd.ap()[:, :]),
                  wait=[w(s_qkvm), w(s_ltio)], inc=[(s_dma, 16)])
            p.add("tensor", None, wait=[w(s_dma)])
            p.add("vector", None, wait=[w(s_dma)])
            p.add("scalar", None, wait=[w(s_dma)])
        s_dbg = p.sem("dbg")
        if debug:
            p.add("gpsimd", lambda e: e.dma_start(dbg_kT.ap()[:, :], kT[:, :]),
                  wait=[w(s_qkve)], inc=[(s_dbg, 16)])
            p.add("gpsimd", lambda e: e.dma_start(dbg_v.ap()[:, :], v_all[:, 0:4 * H * 65]),
                  wait=[w(s_qkve)], inc=[(s_dbg, 16)])
        for h in range(H if upto >= 2 else 0):
            g, pb = h // 2, (h % 2) * 64
            for i4 in range(8):
                jlo = max(0, i4 * 4 - dmax[h])
                p.add("sync", lambda e, g=g, pb=pb, i4=i4: e.dma_start(
                    qstage[pb:pb + 64, :],
                    qT_scr.ap()[pb:pb + 64, g * 4096 + i4 * 512: g * 4096 + (i4 + 1) * 512]),
                    wait=[w(s_scm), w(s_scio)], inc=[(s_qst, 16)])
                first = True
                for jt in range(jlo, NT):
                    fully_past = (jt < i4 * 4)
                    fully_future = (jt > i4 * 4 + 3)
                    sp1 = fully_future or (upto == 22)
                    p.add("tensor",
                          lambda e, jt=jt, g=g, pb=pb, sp=sp1: e.matmul(
                              sc_ps[:, :],
                              kT[pb:pb + 64, g * 4096 + jt * 128: g * 4096 + (jt + 1) * 128],
                              qstage[pb:pb + 64, :],
                              start=True, stop=sp),
                          wait=[w(s_exp), w(s_msk), w(s_qst), w(s_qkve)],
                          inc=[(s_scm, 1)] if fully_future else ())
                    if upto == 22:
                        if not fully_future:
                            p.add("tensor", lambda e: e.nop(), inc=[(s_scm, 1)])
                    elif fully_past:
                        p.add("tensor",
                              lambda e, g=g, pb=pb: e.matmul(
                                  sc_ps[:, :], ones_a[pb:pb + 1, 0:128],
                                  qxs[pb:pb + 1, g * 512:(g + 1) * 512],
                                  start=False, stop=True),
                              inc=[(s_scm, 1)])
                    elif not fully_future:
                        subs = [s2 for s2 in range(4) if i4 * 4 + s2 >= jt]
                        for n2, s2 in enumerate(subs):
                            p.add("tensor",
                                  lambda e, g=g, pb=pb, s2=s2, i4=i4, last=(n2 == len(subs) - 1): e.matmul(
                                      sc_ps[:, s2 * 128:(s2 + 1) * 128], ones_a[pb:pb + 1, 0:128],
                                      qxs[pb:pb + 1, g * 512 + s2 * 128: g * 512 + (s2 + 1) * 128],
                                      start=False, stop=last),
                                  inc=[(s_scm, 1)] if n2 == len(subs) - 1 else ())
                    if fully_future and upto != 23:
                        p.add("scalar",
                              lambda e: e.activation(pt[:, :], sc_ps[:, :], AF.Exp,
                                                     scale=0.125, bias=cm.cb(-30.0)),
                              wait=[w(s_scm), w(s_om)], inc=[(s_exp, 1)])
                    else:
                        subops = []
                        for sub in range(4):
                            isub = i4 * 4 + sub
                            dlt = isub - jt
                            if dlt > dmax[h]:
                                p.add("vector",
                                      lambda e, sub=sub: e.memset(pt[:, sub * 128:(sub + 1) * 128], 0.0),
                                      wait=[w(s_om), w(s_scm)], inc=[(s_msk, 1)])
                                subops.append(("v", sub))
                                continue
                            diag = (dlt == 0)
                            if diag:
                                p.add("vector",
                                      lambda e, sub=sub, h=h: e.tensor_add(
                                          dmscr[:, :],
                                          sc_ps[:, sub * 128:(sub + 1) * 128],
                                          dmcs[:, h * 128:(h + 1) * 128]),
                                      wait=[w(s_scm)], inc=[(s_msk, 1)])
                            bias_ap = (lambda h=h, dlt=dlt: alcst[:, h * 32 + dlt:h * 32 + dlt + 1]) if dlt >= 0 else (lambda: cm.cb(-30.0))
                            is_last_scalar = all((i4 * 4 + s2 - jt > dmax[h]) for s2 in range(sub + 1, 4))
                            src_ap = (lambda: dmscr[:, :]) if diag else (
                                lambda sub=sub: sc_ps[:, sub * 128:(sub + 1) * 128])
                            p.add("scalar",
                                  lambda e, sub=sub, b=bias_ap, sa=src_ap: e.activation(
                                      pt[:, sub * 128:(sub + 1) * 128],
                                      sa(), AF.Exp,
                                      scale=0.125, bias=b()),
                                  wait=[w(s_scm), w(s_om), w(s_msk)],
                                  inc=[(s_exp, 1)] if is_last_scalar else ())
                    last = (jt == NT - 1)
                    p.add("tensor",
                          lambda e, jt=jt, h=h, pb=pb, first=first, last=last: e.matmul(
                              oT_ps[pb:pb + 64, :],
                              v_all[:, jt * H * 65 + h * 65: jt * H * 65 + h * 65 + 64],
                              pt[:, :], start=first, stop=last),
                          wait=[w(s_exp), w(s_msk)] + ([w(s_nrm)] if first else []),
                          inc=[(s_om, 1)])
                    p.add("tensor",
                          lambda e, jt=jt, h=h, pb=pb, first=first, last=last: e.matmul(
                              rs_ps[pb:pb + 1, :],
                              v_all[:, jt * H * 65 + h * 65 + 64: jt * H * 65 + h * 65 + 65],
                              pt[:, :], start=first, stop=last),
                          inc=[(s_om, 1)])
                    first = False
                    if debug and h == 2 and i4 == 1 and jt == 4:
                        p.add("gpsimd", lambda e: e.dma_start(dbg_pt.ap()[:, :], pt[:, :]),
                              wait=[w(s_exp)], inc=[(s_dbg, 16)])
                p.add("vector", lambda e, pb=pb: e.reciprocal(rstg[pb:pb + 1, :], rs_ps[pb:pb + 1, :]),
                      wait=[w(s_om), w(s_bcm)], selfsync=True)
                p.add("vector", lambda e, pb=pb: e.tensor_copy(rbf[pb:pb + 1, :], rstg[pb:pb + 1, :]),
                      selfsync=True)
                p.add("tensor", lambda e, pb=pb: e.matmul(
                    pr_ps[pb:pb + 64, 0:512], ones_a[pb:pb + 1, 0:64], rbf[pb:pb + 1, :],
                    start=True, stop=True),
                    wait=[p.selfwait("vector"), w(s_prm), w(s_res)], inc=[(s_bcm, 1)])
                p.add("vector", lambda e, pb=pb: e.tensor_copy(bcs[pb:pb + 64, :], pr_ps[pb:pb + 64, 0:512]),
                      wait=[w(s_bcm)], selfsync=True)
                p.add("vector", lambda e, pb=pb: e.tensor_mul(ostage[pb:pb + 64, :], bcs[pb:pb + 64, :],
                                                              oT_ps[pb:pb + 64, :]),
                      inc=[(s_nrm, 1)])
                p.add("sync", lambda e, g=g, pb=pb, i4=i4: e.dma_start(
                    aoT_scr.ap()[pb:pb + 64, g * 4096 + i4 * 512: g * 4096 + (i4 + 1) * 512],
                    ostage[pb:pb + 64, :]),
                    wait=[w(s_nrm)], inc=[(s_scio, 16)])
                if debug and h == 2 and i4 == 1:
                    p.add("gpsimd", lambda e, pb=pb: e.dma_start(dbg_ao.ap()[0:64, :], ostage[pb:pb + 64, :]),
                          wait=[w(s_nrm)], inc=[(s_dbg, 16)])

        if upto < 3:
            p.add("sync", lambda e, v=s_scio.n: e.wait_ge(s_scio.h, v))
            return
        p.add("gpsimd", lambda e: e.dma_start(
            wsb[:, :].rearrange("p (j c) -> p j c", j=8),
            wproj.ap().rearrange("(j p) c -> p j c", p=128)),
            wait=[w(s_qkvm), w(s_scm)], inc=[(s_wgt, 16)])
        for i in range(NT if upto >= 4 else 1):
            p.add("sync", lambda e, i=i: e.dma_start(
                aoc[:, :].rearrange("p (g t) -> p g t", g=8),
                aoT_scr.ap().rearrange("p (g t) -> p g t", g=8)[:, :, i * 128:(i + 1) * 128]),
                wait=[w(s_prm), w(s_scio)], inc=[(s_aocl, 16)])
            p.add("sync", lambda e, i=i: e.dma_start(x1[:, :], xr[:, i, :]),
                  wait=[w(s_res), w(s_xo)], inc=[(s_aocl, 16)])
            if upto == 31:
                p.add("sync", lambda e, i=i: e.dma_start(xwr[:, i, :], x1[:, :]),
                      wait=[w(s_aocl)], inc=[(s_xo, 16)])
                break
            for g in range(8 if upto != 32 else 1):
                for dh in range(2):
                    p.add("tensor",
                          lambda e, g=g, dh=dh: e.matmul(
                              pr_ps[:, dh * 512:(dh + 1) * 512],
                              aoc[:, g * 128:(g + 1) * 128],
                              wsb[:, g * 1024 + dh * 512: g * 1024 + (dh + 1) * 512],
                              start=(g == 0), stop=False),
                          wait=([w(s_aocl), w(s_wgt), w(s_res), w(s_bcm), w(s_nrm)]) if (g == 0 and dh == 0) else ())
            for dh in range(2):
                p.add("tensor",
                      lambda e, dh=dh: e.matmul(
                          pr_ps[:, dh * 512:(dh + 1) * 512],
                          cm.ones1[0:1, 0:128], bpr[:, dh * 512:(dh + 1) * 512],
                          start=False, stop=True),
                      inc=[(s_prm, 1)] if dh == 1 else ())
            p.add("vector", lambda e: e.tensor_add(x1[:, :], x1[:, :], pr_ps[:, :]),
                  wait=[w(s_prm), w(s_aocl)], inc=[(s_res, 1)])
            p.add("sync", lambda e, i=i: e.dma_start(xwr[:, i, :], x1[:, :]),
                  wait=[w(s_res)], inc=[(s_xo, 16)])

        p.add("sync", lambda e, v=s_xo.n: e.wait_ge(s_xo.h, v))
        if s_dbg.n:
            p.add("sync", lambda e, v=s_dbg.n: e.wait_ge(s_dbg.h, v))


def l2_attn_inmap(inputs, b, x):
    bq = np.asarray(inputs["L2_attn_b_qkv"], np.float32)
    bqk = np.concatenate([bq[0:1024].reshape(8, 128).T, bq[1024:2048].reshape(8, 128).T], axis=1)
    return {
        "x_in": np.ascontiguousarray(x[b]),
        "wqkv": hC(inputs["L2_attn_W_qkv"]),
        "bqkv2": np.ascontiguousarray(bqk),
        "bv_row": bq[2048:3072].reshape(1, 1024).copy(),
        "wproj": hC(inputs["L2_attn_W_proj"]),
        "bp_row": np.asarray(inputs["L2_attn_b_proj"], np.float32).reshape(1, 1024).copy(),
    }





def emit_L3_attn(fz, pfx, x_in, x_out, debug=False):
    nc, p, cm = fz.nc, fz.p, fz.cm
    wqkv = nc.dram_tensor(pfx + "wqkv", [D, 3 * D], F32, kind="ExternalInput")
    bq_row = nc.dram_tensor(pfx + "bq_row", [1, 3 * D], F32, kind="ExternalInput")
    wln = nc.dram_tensor(pfx + "wln", [D, D], F32, kind="ExternalInput")
    bl_row = nc.dram_tensor(pfx + "bl_row", [1, D], F32, kind="ExternalInput")
    skind = "ExternalOutput" if debug else "Internal"
    v_scr = nc.dram_tensor(pfx + "v_scr", [8, 128, 4096], BF16, kind=skind)
    q2_scr = nc.dram_tensor(pfx + "q2_scr", [8, 128, 4096], BF16, kind=skind)
    if debug:
        dbg_kv = nc.dram_tensor("dbg_kv", [128, 8 * D], F32, kind="ExternalOutput")
        dbg_k2 = nc.dram_tensor("dbg_k2", [128, 4 * D], F32, kind="ExternalOutput")
        dbg_rcn = nc.dram_tensor("dbg_rcn", [128, 8], F32, kind="ExternalOutput")
        dbg_cq = nc.dram_tensor("dbg_cq", [1, D], F32, kind="ExternalOutput")

    with contextlib.ExitStack() as stack:
        sb = lambda name, shape, dt=BF16: stack.enter_context(nc.sbuf_tensor(pfx + name, shape, dt))
        ps = lambda name, shape, dt=F32: stack.enter_context(nc.psum_tensor(pfx + name, shape, dt))
        cm.sq_scr = sb("sq_scr2", [128, D], F32)

        ar = Arena(nc, stack, 48 * 1024, pfx=pfx)
        wqkv_sb = ar.at(0, "wqkv_sb", [128, 8 * 3 * D], BF16)
        kv_bf = ar.at(0, "kv_bf", [128, 8 * D], BF16)
        wln_sb = ar.at(16, "wln_sb", [128, 8 * D], BF16)
        k2 = sb("k2", [128, NT * D])
        lnT4 = sb("lnT4", [128, 4096])
        qkvbuf = sb("qkvbuf", [128, 3 * D], F32)
        chunkb = sb("chunkb", [128, 2 * 4096])
        vstage = sb("vstage", [128, D])
        z_bf = sb("z_bf", [128, D])
        zT = sb("zT", [128, D])
        q2T = sb("q2T", [128, D])
        sqs = sb("sqs", [128, D])
        bqr = sb("bqr", [1, 3 * D])
        blr = sb("blr", [1, D])
        colsq = sb("colsq", [1, D], F32)
        s_csa = None
        rcn = sb("rcn", [128, 8], F32)
        rcnb = sb("rcnb", [1, D])
        onesc = sb("onesc", [128, 1])
        st2 = sb("st2", [128, 16], F32)
        x1 = sb("x1", [128, D], F32)

        kv_ps = ps("kv_ps", [128, 2048])
        cs_ps = [ps("cs_ps0", [128, 512]), ps("cs_ps1", [128, 512])]
        qb = [lambda: kv_ps[:, 0:512], lambda: kv_ps[:, 512:1024]]

        s_dma = p.sem("dma")
        s_wgt = p.sem("wgt")
        s_qm = p.sem("qm")
        s_qe = p.sem("qe")
        s_ch = p.sem("ch")
        s_io = p.sem("io")
        s_cs = p.sem("cs")
        s_csa2 = p.sem("csa2")
        s_vld = p.sem("vld")
        s_kvm = p.sem("kvm")
        s_kve = p.sem("kve")
        s_rt = p.sem("rt")
        s_t2m = p.sem("t2m")
        s_zb = p.sem("zb")
        s_lnm = p.sem("lnm")
        s_res = p.sem("res")
        s_xc = p.sem("xc")
        s_xo = p.sem("xo")
        s_chld = p.sem("chld")

        cm.load_consts(s_dma)
        for _ge in ("tensor", "vector", "scalar"):
            p.add(_ge, None, wait=[w(s_dma)])
        p.add("gpsimd", lambda e: e.dma_start(
            wqkv_sb[:, :].rearrange("p (j c) -> p j c", j=8),
            wqkv.ap().rearrange("(j p) c -> p j c", p=128)), inc=[(s_wgt, 16)])
        p.add("gpsimd", lambda e: e.dma_start(bqr[:, :], bq_row.ap()[:, :]), inc=[(s_wgt, 16)])
        p.add("gpsimd", lambda e: e.dma_start(blr[:, :], bl_row.ap()[:, :]), inc=[(s_wgt, 16)])
        p.add("vector", lambda e: e.memset(onesc[:, :], 1.0))
        p.add("vector", lambda e: e.memset(colsq[0:1, :], 0.0))

        xr = x_in.ap().rearrange("(i p) d -> p i d", p=128)
        xwr = x_out.ap().rearrange("(i p) d -> p i d", p=128)

        for c4 in range(8):
            tiles = [c4 * 4 + k for k in range(4)]
            p.add("sync",
                  lambda e, c4=c4: e.dma_start(
                      cm.x_sb[:, :].rearrange("p (k d) -> p k d", k=4),
                      xr[:, c4 * 4:(c4 + 1) * 4, :]),
                  wait=[w(cm.s_lnbf)], inc=[(s_xc, 16)])
            xdep = [w(s_xc)]
            for k, i in enumerate(tiles):
                slot = i % 2
                cm.ln_tile(i, (lambda s=slot: cm.lnbf[:, s * D:(s + 1) * D]), deps=xdep,
                           src_col=k * D)
                dst = lambda k=k: lnT4[:, :].rearrange("p (j t) -> p j t", j=8)[:, :, k * 128:(k + 1) * 128]
                cm.transpose_tile((lambda s=slot: cm.lnbf[:, s * D:(s + 1) * D]),
                                  dst, wait_src=[w(cm.s_lnbf)],
                                  evac_wait=[w(s_qm)])
            for k, i in enumerate(tiles):
                for c6 in range(6):
                    bank = qb[c6 % 2]
                    for j in range(8):
                        p.add("tensor",
                              lambda e, j=j, k=k, c6=c6, bank=bank: e.matmul(
                                  bank(), lnT4[:, j * 512 + k * 128: j * 512 + k * 128 + 128],
                                  wqkv_sb[:, j * 3 * D + c6 * 512: j * 3 * D + (c6 + 1) * 512],
                                  start=(j == 0), stop=False),
                              wait=([w(cm.s_lnT), w(s_wgt), relw(s_qe, s_qm)]) if j == 0 else ())
                    p.add("tensor",
                          lambda e, c6=c6, bank=bank: e.matmul(
                              bank(), cm.ones1[0:1, 0:128], bqr[:, c6 * 512:(c6 + 1) * 512],
                              start=False, stop=True),
                          inc=[(s_qm, 1)])
                    p.add("vector",
                          lambda e, c6=c6, bank=bank: e.tensor_copy(qkvbuf[:, c6 * 512:(c6 + 1) * 512], bank()),
                          wait=[w(s_qm), w(s_ch)], inc=[(s_qe, 1)])
                p.add("vector", lambda e: e.reduce_sum(st2[:, 0:1], qkvbuf[:, 0:D], axis=AX.X),
                      wait=[w(s_qe)], selfsync=True)
                p.add("vector", lambda e: e.tensor_scalar(st2[:, 1:2], st2[:, 0:1], 1.0 / D, None, op0=ALU.mult), selfsync=True)
                p.add("vector", lambda e: e.tensor_scalar(qkvbuf[:, 0:D], qkvbuf[:, 0:D], st2[:, 1:2], None, op0=ALU.subtract), selfsync=True)
                p.add("scalar", lambda e: e.activation(qkvbuf[:, 0:D], qkvbuf[:, 0:D], AF.Square,
                                                       accum_out=st2[:, 2:3]),
                      wait=[p.selfwait("vector")], inc=[(s_ch, 1)])
                p.add("vector", lambda e: e.tensor_scalar(st2[:, 3:4], st2[:, 2:3], 1e-7, None, op0=ALU.add),
                      wait=[w(s_ch)], selfsync=True)
                p.add("vector", lambda e: e.reciprocal(st2[:, 4:5], st2[:, 3:4]), selfsync=True)
                p.add("vector", lambda e: e.tensor_scalar(qkvbuf[:, 0:D], qkvbuf[:, 0:D], st2[:, 4:5], None, op0=ALU.mult), selfsync=True)
                p.add("scalar", lambda e, sq=cm.sq_scr: e.activation(sq[:, :], qkvbuf[:, 0:D], AF.Square,
                                                       accum_out=st2[:, 5:6]),
                      wait=[p.selfwait("vector")], inc=[(s_ch, 1)])
                p.add("scalar", lambda e: e.activation(st2[:, 6:7], st2[:, 5:6], AF.Ln), inc=[(s_ch, 1)])
                p.add("scalar", lambda e: e.activation(st2[:, 7:8], st2[:, 6:7], AF.Exp, scale=0.5), inc=[(s_ch, 1)])
                p.add("vector", lambda e: e.tensor_scalar_max(st2[:, 8:9], st2[:, 7:8], 1e-12),
                      wait=[w(s_ch)], selfsync=True)
                p.add("vector", lambda e: e.reciprocal(st2[:, 9:10], st2[:, 8:9]), selfsync=True)
                p.add("vector", lambda e: e.tensor_scalar(st2[:, 10:11], st2[:, 9:10], 1.0 / 64.0, None, op0=ALU.mult), selfsync=True)
                p.add("vector", lambda e: e.tensor_scalar(z_bf[:, :], qkvbuf[:, 0:D], st2[:, 10:11], None, op0=ALU.mult),
                      inc=[(s_ch, 1)])
                p.add("sync", lambda e, i=i: e.dma_start(q2_scr.ap()[i // 4][:, (i % 4) * D:((i % 4) + 1) * D], z_bf[:, :]),
                      wait=[w(s_ch)], inc=[(s_io, 16)])
                p.add("vector", lambda e: e.reduce_sum(st2[:, 0:1], qkvbuf[:, D:2 * D], axis=AX.X),
                      wait=[w(s_io)], selfsync=True)
                p.add("vector", lambda e: e.tensor_scalar(st2[:, 1:2], st2[:, 0:1], 1.0 / D, None, op0=ALU.mult), selfsync=True)
                p.add("vector", lambda e: e.tensor_scalar(qkvbuf[:, D:2 * D], qkvbuf[:, D:2 * D], st2[:, 1:2], None, op0=ALU.subtract), selfsync=True)
                p.add("scalar", lambda e: e.activation(qkvbuf[:, D:2 * D], qkvbuf[:, D:2 * D], AF.Square,
                                                       accum_out=st2[:, 2:3]),
                      wait=[p.selfwait("vector")], inc=[(s_ch, 1)])
                p.add("vector", lambda e: e.tensor_scalar(st2[:, 3:4], st2[:, 2:3], 1e-7, None, op0=ALU.add),
                      wait=[w(s_ch)], selfsync=True)
                p.add("vector", lambda e: e.reciprocal(st2[:, 4:5], st2[:, 3:4]), selfsync=True)
                p.add("vector", lambda e, i=i: e.tensor_scalar(k2[:, i * D:(i + 1) * D], qkvbuf[:, D:2 * D], st2[:, 4:5], None, op0=ALU.mult),
                      inc=[(s_ch, 1)])
                p.add("scalar", lambda e, i=i: e.activation(sqs[:, :], k2[:, i * D:(i + 1) * D], AF.Square),
                      wait=[w(s_ch), w(s_cs)], inc=[(s_ch, 1)])
                for half in range(2):
                    p.add("tensor",
                          lambda e, half=half: e.matmul(
                              cs_ps[half][0:1, :], onesc[:, :],
                              sqs[:, half * 512:(half + 1) * 512],
                              start=True, stop=True),
                          wait=[w(s_ch), w(s_csa2)] if half == 0 else (),
                          inc=[(s_cs, 1)] if half == 1 else ())
                for half in range(2):
                    p.add("vector",
                          lambda e, half=half: e.tensor_add(
                              colsq[0:1, half * 512:(half + 1) * 512],
                              colsq[0:1, half * 512:(half + 1) * 512],
                              cs_ps[half][0:1, :]),
                          wait=[w(s_cs)] if half == 0 else (),
                          inc=[(s_csa2, 1)] if half == 1 else ())
                p.add("vector", lambda e: e.tensor_copy(vstage[:, :], qkvbuf[:, 2 * D:3 * D]),
                      wait=[w(s_io)], inc=[(s_ch, 1)])
                p.add("sync", lambda e, i=i: e.dma_start(v_scr.ap()[i // 4][:, (i % 4) * D:((i % 4) + 1) * D], vstage[:, :]),
                      wait=[w(s_ch)], inc=[(s_io, 16)])

        p.add("scalar", lambda e: e.activation(colsq[0:1, :], colsq[0:1, :], AF.Ln),
              wait=[w(s_csa2)], inc=[(s_ch, 1)])
        p.add("scalar", lambda e: e.activation(colsq[0:1, :], colsq[0:1, :], AF.Exp, scale=0.5),
              inc=[(s_ch, 1)])
        p.add("vector", lambda e: e.tensor_scalar_max(colsq[0:1, :], colsq[0:1, :], 1e-12),
              wait=[w(s_ch)], selfsync=True)
        p.add("vector", lambda e: e.reciprocal(colsq[0:1, :], colsq[0:1, :]), selfsync=True)
        p.add("vector", lambda e: e.tensor_copy(rcnb[0:1, :], colsq[0:1, :]),
              inc=[(s_ch, 1)])
        for ct in range(8):
            p.add("tensor", lambda e, ct=ct: e.matmul(
                cs_ps[0][:, 0:1], rcnb[0:1, ct * 128:(ct + 1) * 128], cm.ones1[0:1, 0:1],
                start=True, stop=True),
                wait=[w(s_ch), w(s_csa2)] + ([w(s_cs)] if ct == 0 else []) + ([w(s_rt)] if ct > 0 else []),
                inc=[(s_rt, 1)])
            p.add("vector", lambda e, ct=ct: e.tensor_copy(rcn[:, ct:ct + 1], cs_ps[0][:, 0:1]),
                  wait=[w(s_rt)], inc=[(s_rt, 1)])

        for chalf in range(4):
            for i in range(NT):
                p.add("sync", lambda e, i=i: e.dma_start(vstage[:, :], v_scr.ap()[i // 4][:, (i % 4) * D:((i % 4) + 1) * D]),
                      wait=[w(s_kvm), w(s_io), w(s_ch)], inc=[(s_vld, 16)])
                for cc in range(2):
                    ct = chalf * 2 + cc
                    for dh in range(2):
                        p.add("tensor",
                              lambda e, i=i, ct=ct, cc=cc, dh=dh, first=(i == 0), last=(i == NT - 1): e.matmul(
                                  kv_ps[:, cc * 1024 + dh * 512: cc * 1024 + (dh + 1) * 512],
                                  k2[:, i * D + ct * 128: i * D + ct * 128 + 128],
                                  vstage[:, dh * 512:(dh + 1) * 512],
                                  start=first, stop=last),
                              wait=([w(s_vld), w(s_qe), w(s_rt), w(s_kve)]) if (cc == 0 and dh == 0) else (),
                              inc=[(s_kvm, 1)] if (cc == 1 and dh == 1) else ())
            for cc in range(2):
                ct = chalf * 2 + cc
                p.add("vector", lambda e, ct=ct, cc=cc: e.tensor_scalar(
                    kv_bf[:, ct * D:(ct + 1) * D], kv_ps[:, cc * 1024:(cc + 1) * 1024],
                    rcn[:, ct:ct + 1], None, op0=ALU.mult),
                    wait=[w(s_kvm), w(s_wgt)], inc=[(s_kve, 1)])

        if debug:
            s_dbg3 = p.sem("dbg3")
            p.add("gpsimd", lambda e: e.dma_start(dbg_rcn.ap()[:, :], rcn[:, :]),
                  wait=[w(s_rt)], inc=[(s_dbg3, 16)])
            p.add("gpsimd", lambda e: e.dma_start(dbg_cq.ap()[:, :], colsq[0:1, :]),
                  wait=[w(s_rt)], inc=[(s_dbg3, 16)])
            p.add("gpsimd", lambda e: e.dma_start(dbg_kv.ap()[:, :], kv_bf[:, :]),
                  wait=[w(s_kve)], inc=[(s_dbg3, 16)])
            p.add("gpsimd", lambda e: e.dma_start(dbg_k2.ap()[:, :], k2[:, 0:4 * D]),
                  wait=[w(s_kve)], inc=[(s_dbg3, 16)])
            p.add("sync", lambda e, v=s_dbg3.n: e.wait_ge(s_dbg3.h, v))
        p.add("gpsimd", lambda e: e.dma_start(
            wln_sb[:, :].rearrange("p (j c) -> p j c", j=8),
            wln.ap().rearrange("(j p) c -> p j c", p=128)),
            wait=[w(s_qm)], inc=[(s_wgt, 16)])

        for c4 in range(8):
            p.add("sync", lambda e, c4=c4: e.dma_start(chunkb[:, 0:4096], v_scr.ap()[c4]),
                  wait=[w(s_zb), w(s_kvm)], inc=[(s_chld, 16)])
            p.add("sync", lambda e, c4=c4: e.dma_start(chunkb[:, 4096:8192], q2_scr.ap()[c4]),
                  wait=[w(s_t2m)], inc=[(s_chld, 16)])
            for k in range(4):
                for j in range(8):
                    p.add("tensor",
                          lambda e, j=j, k=k: e.transpose(
                              cm.tr_ps[:, j * 128:(j + 1) * 128],
                              chunkb[:, 4096 + k * D + j * 128: 4096 + k * D + (j + 1) * 128],
                              cm.ident[:, :]),
                          wait=([w(s_chld), w(cm.s_lnT)]) if j == 0 else (),
                          inc=[(cm.s_tr, 1)] if j == 7 else ())
                p.add("vector", lambda e: e.tensor_copy(q2T[:, :], cm.tr_ps[:, :]),
                      wait=[w(cm.s_tr), w(s_t2m)], inc=[(cm.s_lnT, 1)])
                for j in range(8):
                    for dh in range(2):
                        p.add("tensor",
                              lambda e, j=j, dh=dh: e.matmul(
                                  kv_ps[:, dh * 512:(dh + 1) * 512],
                                  q2T[:, j * 128:(j + 1) * 128],
                                  kv_bf[:, j * D + dh * 512: j * D + (dh + 1) * 512],
                                  start=(j == 0), stop=(j == 7)),
                              wait=([w(cm.s_lnT), w(s_kve), w(s_zb)]) if (j == 0 and dh == 0) else (),
                              inc=[(s_t2m, 1)] if (j == 7 and dh == 1) else ())
                p.add("vector", lambda e, k=k: e.tensor_add(z_bf[:, :], chunkb[:, k * D:(k + 1) * D], kv_ps[:, 0:1024]),
                      wait=[w(s_t2m)], inc=[(s_zb, 1)])
                for j in range(8):
                    p.add("tensor",
                          lambda e, j=j: e.transpose(
                              cm.tr_ps[:, j * 128:(j + 1) * 128],
                              z_bf[:, j * 128:(j + 1) * 128],
                              cm.ident[:, :]),
                          wait=([w(s_zb), w(cm.s_lnT)]) if j == 0 else (),
                          inc=[(cm.s_tr, 1)] if j == 7 else ())
                p.add("vector", lambda e: e.tensor_copy(zT[:, :], cm.tr_ps[:, :]),
                      wait=[w(cm.s_tr), w(s_lnm)], inc=[(cm.s_lnT, 1)])
                for j in range(8):
                    for dh in range(2):
                        p.add("tensor",
                              lambda e, j=j, dh=dh: e.matmul(
                                  kv_ps[:, 1024 + dh * 512: 1024 + (dh + 1) * 512],
                                  zT[:, j * 128:(j + 1) * 128],
                                  wln_sb[:, j * D + dh * 512: j * D + (dh + 1) * 512],
                                  start=(j == 0), stop=False),
                              wait=([w(cm.s_lnT), w(s_wgt), w(s_res)]) if (j == 0 and dh == 0) else ())
                for dh in range(2):
                    p.add("tensor",
                          lambda e, dh=dh: e.matmul(
                              kv_ps[:, 1024 + dh * 512: 1024 + (dh + 1) * 512],
                              cm.ones1[0:1, 0:128], blr[:, dh * 512:(dh + 1) * 512],
                              start=False, stop=True),
                          inc=[(s_lnm, 1)] if dh == 1 else ())
                i = c4 * 4 + k
                p.add("sync", lambda e, i=i: e.dma_start(x1[:, :], xr[:, i, :]),
                      wait=[w(s_res), w(s_xo)], inc=[(s_xc, 16)])
                p.add("vector", lambda e: e.tensor_add(x1[:, :], x1[:, :], kv_ps[:, 1024:2048]),
                      wait=[w(s_lnm), w(s_xc)], inc=[(s_res, 1)])
                p.add("sync", lambda e, i=i: e.dma_start(xwr[:, i, :], x1[:, :]),
                      wait=[w(s_res)], inc=[(s_xo, 16)])

        p.add("sync", lambda e, v=s_xo.n: e.wait_ge(s_xo.h, v))


def l3_attn_inmap(inputs, b, x):
    return {
        "x_in": np.ascontiguousarray(x[b]),
        "wqkv": hC(inputs["L3_attn_Wqkv"]),
        "bq_row": np.asarray(inputs["L3_attn_bqkv"], np.float32).reshape(1, 3 * 1024).copy(),
        "wln": hC(inputs["L3_attn_Wln"]),
        "bl_row": np.asarray(inputs["L3_attn_bln"], np.float32).reshape(1, 1024).copy(),
    }





def emit_L2_ffn(fz, pfx, x_in, x_out, debug=False):
    nc, p, cm = fz.nc, fz.p, fz.cm
    wg = nc.dram_tensor(pfx + "wg", [2816, D], F32, kind="ExternalInput")
    wu = nc.dram_tensor(pfx + "wu", [2816, D], F32, kind="ExternalInput")
    wd = nc.dram_tensor(pfx + "wd", [D, 2816], F32, kind="ExternalInput")
    I2816 = 2816

    with contextlib.ExitStack() as stack:
        sb = lambda name, shape, dt=BF16: stack.enter_context(nc.sbuf_tensor(pfx + name, shape, dt))
        ps = lambda name, shape, dt=F32: stack.enter_context(nc.psum_tensor(pfx + name, shape, dt))
        cm.sq_scr = sb("sq_scr2", [128, D], F32)

        wgT = sb("wgT", [128, 8 * I2816])
        wuT = sb("wuT", [128, 8 * I2816])
        wdT = sb("wdT", [128, 22 * D])
        rmax = sb("rmax", [128, 176], F32)
        sbuf_s = sb("sbuf_s", [128, 176], F32)
        s16 = sb("s16", [128, 176])
        xn = sb("xn", [128, I2816], F32)
        ci = sb("ci", [128, I2816], I32)
        c1f = sb("c1f", [128, I2816])
        c2f = sb("c2f", [128, I2816])
        gA = sb("gA", [128, I2816])
        gB = sb("gB", [128, I2816])
        ar = Arena(nc, stack, 11 * 1024 + 256, name="arena_l2f", pfx=pfx,
                   at_addr=nc.lookup_mloc(cm.x_sb).addr + 4 * D)
        wstage = ar.at(0, "wstage", [128, I2816], F32)
        lnf = ar.at(0, "lnf", [128, D], F32)
        xqT = ar.at(4, "xqT", [128, D], BF16)
        h32 = ar.at(6, "h32", [128, 512], F32)
        sg = ar.at(8, "sg", [128, 512], F32)
        hqT = ar.at(10, "hqT", [128, 512], BF16)
        qout = sb("qout", [128, I2816])

        gate_ps = ps("gate_ps", [128, 512])
        up_ps = ps("up_ps", [128, 512])
        out_ps = ps("out_ps", [128, 1024])

        if debug:
            dbg_xq = nc.dram_tensor("dbg_xq", [128, D], F32, kind="ExternalOutput")
            dbg_hq = nc.dram_tensor("dbg_hq", [128, 512], F32, kind="ExternalOutput")
            dbg_h = nc.dram_tensor("dbg_h", [128, 512], F32, kind="ExternalOutput")
            s_dbg = p.sem("dbgq")
        s_dma = p.sem("dma")
        s_wld = p.sem("wld")
        s_qv = p.sem("qv")
        s_qs = p.sem("qs")
        s_tr2 = p.sem("tr2")
        s_te = p.sem("te")
        s_gm = p.sem("gm")
        s_sil = p.sem("sil")
        s_dm = p.sem("dm")
        s_res = p.sem("res")
        s_xc = p.sem("xc")
        s_xo = p.sem("xo")

        cm.load_consts(s_dma)
        for _ge in ("tensor", "vector", "scalar"):
            p.add(_ge, None, wait=[w(s_dma)])
        xr = x_in.ap().rearrange("(i p) d -> p i d", p=128)
        xwr = x_out.ap().rearrange("(i p) d -> p i d", p=128)

        def emit_quant(src_fn, dst_fn, W, waits):
            nb = W // 16
            v3 = lambda a, n: a.rearrange("p (g s) -> p g s", g=n)
            p.add("vector", lambda e: e.reduce_max(
                v3(rmax[:, 0:nb], nb), v3(src_fn(), nb), axis=AX.X,
                apply_absolute_value=True),
                wait=list(waits) + [w(s_qs), w(s_tr2)], selfsync=True)
            p.add("vector", lambda e: e.tensor_scalar(
                sbuf_s[:, 0:nb], rmax[:, 0:nb], 1.0 / 6.0, 1e-12, op0=ALU.mult, op1=ALU.max),
                selfsync=True)
            p.add("vector", lambda e: e.reciprocal(rmax[:, 0:nb], sbuf_s[:, 0:nb]),
                  selfsync=True)
            rs0 = lambda: bass.AP(rmax.ap().tensor, 0, [[176, 128], [1, nb], [0, 16]])
            p.add("vector", lambda e: e.tensor_tensor(
                v3(xn[:, 0:W], nb), v3(src_fn(), nb).rearrange("p g s -> p g s"),
                rs0(), op=ALU.mult),
                inc=[(s_qv, 1)])
            p.add("scalar", lambda e: e.activation(ci[:, 0:W], xn[:, 0:W], AF.Copy, scale=2.0),
                  wait=[w(s_qv)])
            p.add("scalar", lambda e: e.activation(c1f[:, 0:W], ci[:, 0:W], AF.Copy))
            p.add("scalar", lambda e: e.activation(ci[:, 0:W], xn[:, 0:W], AF.Copy))
            p.add("scalar", lambda e: e.activation(c2f[:, 0:W], ci[:, 0:W], AF.Copy),
                  inc=[(s_qs, 1)])
            p.add("vector", lambda e: e.tensor_scalar(gA[:, 0:W], c1f[:, 0:W], -4.0, 4.0, op0=ALU.max, op1=ALU.min),
                  wait=[w(s_qs)])
            p.add("vector", lambda e: e.tensor_scalar(gB[:, 0:W], c2f[:, 0:W], 2.0, 4.0, op0=ALU.max, op1=ALU.min))
            p.add("vector", lambda e: e.scalar_tensor_tensor(gA[:, 0:W], gA[:, 0:W], 0.5, gB[:, 0:W], op0=ALU.mult, op1=ALU.add))
            p.add("vector", lambda e: e.tensor_scalar(gB[:, 0:W], c2f[:, 0:W], -4.0, -2.0, op0=ALU.max, op1=ALU.min))
            p.add("vector", lambda e: e.tensor_add(gA[:, 0:W], gA[:, 0:W], gB[:, 0:W]))
            p.add("vector", lambda e: e.tensor_single_scalar(gB[:, 0:W], xn[:, 0:W], 5.0, op=ALU.is_gt))
            p.add("vector", lambda e: e.scalar_tensor_tensor(gA[:, 0:W], gB[:, 0:W], 2.0, gA[:, 0:W], op0=ALU.mult, op1=ALU.add))
            p.add("vector", lambda e: e.tensor_single_scalar(gB[:, 0:W], xn[:, 0:W], -5.0, op=ALU.is_lt))
            p.add("vector", lambda e: e.scalar_tensor_tensor(gA[:, 0:W], gB[:, 0:W], -2.0, gA[:, 0:W], op0=ALU.mult, op1=ALU.add))
            p.add("vector", lambda e: e.tensor_copy(s16[:, 0:nb], sbuf_s[:, 0:nb]), selfsync=True)
            s0 = lambda: bass.AP(s16.ap().tensor, 0, [[176, 128], [1, nb], [0, 16]])
            p.add("vector", lambda e: e.tensor_tensor(
                v3(dst_fn(), nb), v3(gA[:, 0:W], nb), s0(), op=ALU.mult),
                inc=[(s_qv, 1)])

        for wi, (wdr, ntile, wT, wTcols, tpr) in enumerate((
                (wg, 22, wgT, I2816, 8), (wu, 22, wuT, I2816, 8), (wd, 8, wdT, D, 22))):
            Wq = 1024 if tpr == 8 else I2816
            for r in range(ntile):
                p.add("sync", lambda e, wdr=wdr, r=r, Wq=Wq: e.dma_start(
                    wstage[:, 0:Wq], wdr.ap()[r * 128:(r + 1) * 128, :]),
                    wait=[w(s_qv)], inc=[(s_wld, 16)])
                emit_quant(lambda Wq=Wq: wstage[:, 0:Wq], lambda Wq=Wq: qout[:, 0:Wq], Wq,
                           [w(s_wld)])
                nrounds = (tpr + 7) // 8
                for rd in range(nrounds):
                    j0 = rd * 8
                    jn = min(8, tpr - j0)
                    for j in range(jn):
                        p.add("tensor",
                              lambda e, j=j, j0=j0: e.transpose(
                                  cm.tr_ps[:, j * 128:(j + 1) * 128],
                                  qout[:, (j0 + j) * 128:(j0 + j + 1) * 128],
                                  cm.ident[:, :]),
                              wait=([w(s_qv), w(s_te)]) if j == 0 else (),
                              inc=[(s_tr2, 1)] if j == jn - 1 else ())
                    def evt(e, r=r, j0=j0, jn=jn, wT=wT, wTcols=wTcols):
                        dstv = bass.AP(wT.ap().tensor, j0 * wTcols + r * 128,
                                       [[wT.shape[1], 128], [wTcols, jn], [1, 128]])
                        return e.tensor_copy(dstv, cm.tr_ps[:, 0:jn * 128].rearrange("p (j t) -> p j t", j=jn))
                    p.add("vector", lambda e, f=evt: f(e),
                          wait=[w(s_tr2)], inc=[(s_te, 1)])

        CHK = [(0, 512), (512, 512), (1024, 512), (1536, 512), (2048, 512), (2560, 256)]
        for i in range(NT):
            p.add("sync", lambda e, i=i: e.dma_start(cm.x_sb[:, 0:D], xr[:, i, :]),
                  wait=[w(s_res), w(cm.s_lnbf), w(s_xo)], inc=[(s_xc, 16)])
            cm.ln_tile(i, (lambda: lnf[:, :]), deps=[w(s_xc)], src_col=0,
                       out_wait=[w(s_qv)])
            emit_quant(lambda: lnf[:, :], lambda: qout[:, 0:D], D, [w(cm.s_lnbf)])
            if debug and i == 3:
                p.add("gpsimd", lambda e: e.dma_start(dbg_xq.ap()[:, :], qout[:, 0:D]),
                      wait=[w(s_qv)], inc=[(s_dbg, 16)])
            for j in range(8):
                p.add("tensor",
                      lambda e, j=j: e.transpose(
                          cm.tr_ps[:, j * 128:(j + 1) * 128],
                          qout[:, j * 128:(j + 1) * 128], cm.ident[:, :]),
                      wait=([w(s_qv), w(s_te)]) if j == 0 else (),
                      inc=[(s_tr2, 1)] if j == 7 else ())
            p.add("vector", lambda e: e.tensor_copy(xqT[:, :], cm.tr_ps[:, 0:1024]),
                  wait=[w(s_tr2), w(s_gm)], inc=[(s_te, 1)])
            for ci_, (k0, kw) in enumerate(CHK):
                for which, ps_, wT in ((0, gate_ps, wgT), (1, up_ps, wuT)):
                    for j in range(8):
                        p.add("tensor",
                              lambda e, j=j, k0=k0, kw=kw, ps_=ps_, wT=wT: e.matmul(
                                  ps_[:, 0:kw], xqT[:, j * 128:(j + 1) * 128],
                                  wT[:, j * I2816 + k0: j * I2816 + k0 + kw],
                                  start=(j == 0), stop=(j == 7)),
                              wait=([w(s_te), w(s_wld), w(s_sil)]) if j == 0 else (),
                              inc=[(s_gm, 1)] if j == 7 else ())
                p.add("scalar", lambda e, kw=kw: e.activation(sg[:, 0:kw], gate_ps[:, 0:kw], AF.Silu),
                      wait=[w(s_gm)], inc=[(s_qs, 1)])
                p.add("vector", lambda e, kw=kw: e.tensor_mul(h32[:, 0:kw], sg[:, 0:kw], up_ps[:, 0:kw]),
                      wait=[w(s_qs)], inc=[(s_sil, 1)])
                if debug and i == 3 and ci_ == 0:
                    p.add("gpsimd", lambda e: e.dma_start(dbg_h.ap()[:, :], h32[:, 0:512]),
                          wait=[w(s_sil)], inc=[(s_dbg, 16)])
                emit_quant(lambda kw=kw: h32[:, 0:kw], lambda kw=kw: qout[:, 0:kw], kw,
                           [w(s_sil)])
                if debug and i == 3 and ci_ == 0:
                    p.add("gpsimd", lambda e: e.dma_start(dbg_hq.ap()[:, :], qout[:, 0:512]),
                          wait=[w(s_qv)], inc=[(s_dbg, 16)])
                nkt = kw // 128
                for j in range(nkt):
                    p.add("tensor",
                          lambda e, j=j: e.transpose(
                              cm.tr_ps[:, j * 128:(j + 1) * 128],
                              qout[:, j * 128:(j + 1) * 128], cm.ident[:, :]),
                          wait=([w(s_qv), w(s_te)]) if j == 0 else (),
                          inc=[(s_tr2, 1)] if j == nkt - 1 else ())
                p.add("vector", lambda e, nkt=nkt: e.tensor_copy(hqT[:, 0:nkt * 128], cm.tr_ps[:, 0:nkt * 128]),
                      wait=[w(s_tr2), w(s_dm)], inc=[(s_te, 1)])
                for j in range(nkt):
                    kt = k0 // 128 + j
                    for dh in range(2):
                        p.add("tensor",
                              lambda e, j=j, kt=kt, dh=dh, st=(ci_ == 0 and j == 0),
                                     sp=(ci_ == len(CHK) - 1 and j == nkt - 1): e.matmul(
                                  out_ps[:, dh * 512:(dh + 1) * 512],
                                  hqT[:, j * 128:(j + 1) * 128],
                                  wdT[:, kt * D + dh * 512: kt * D + (dh + 1) * 512],
                                  start=st, stop=sp),
                              wait=([w(s_te), w(s_res)]) if (j == 0 and dh == 0) else (),
                              inc=[(s_dm, 1)] if (j == nkt - 1 and dh == 1) else ())
            p.add("vector", lambda e: e.tensor_add(cm.x_sb[:, 0:D], cm.x_sb[:, 0:D], out_ps[:, :]),
                  wait=[w(s_dm), w(s_xc)], inc=[(s_res, 1)])
            p.add("sync", lambda e, i=i: e.dma_start(xwr[:, i, :], cm.x_sb[:, 0:D]),
                  wait=[w(s_res)], inc=[(s_xo, 16)])

        p.add("sync", lambda e, v=s_xo.n: e.wait_ge(s_xo.h, v))
        if debug:
            p.add("sync", lambda e, v=s_dbg.n: e.wait_ge(s_dbg.h, v))


def l2_ffn_inmap(inputs, b, x):
    return {
        "x_in": np.ascontiguousarray(x[b]),
        "wg": hC(inputs["L2_ffn_w_gate"]),
        "wu": hC(inputs["L2_ffn_w_up"]),
        "wd": hC(inputs["L2_ffn_w_down"]),
    }





def emit_L0_attn(fz, pfx, x_in, x_out, debug=False):
    nc, p, cm = fz.nc, fz.p, fz.cm
    wqkvT = nc.dram_tensor(pfx + "wqkvT", [D, 3 * D], F32, kind="ExternalInput")
    wgbT = nc.dram_tensor(pfx + "wgbT", [D, 64], F32, kind="ExternalInput")
    bgb = nc.dram_tensor(pfx + "bgb", [1, 64], F32, kind="ExternalInput")
    woutT = nc.dram_tensor(pfx + "woutT", [D, D], F32, kind="ExternalInput")
    dk = "ExternalOutput" if debug else "Internal"
    qT_scr = nc.dram_tensor(pfx + "qT_scr", [128, 8 * 4096], BF16, kind=dk)
    kT_scr = nc.dram_tensor(pfx + "kT_scr", [128, 8 * 4096], BF16, kind=dk)
    kn_scr = nc.dram_tensor(pfx + "kn_scr", [NT, 128, D], BF16, kind=dk)
    vn_scr = nc.dram_tensor(pfx + "vn_scr", [NT, 128, D], BF16, kind=dk)
    oT_scr = nc.dram_tensor(pfx + "oT_scr", [8, 128, 4096], BF16, kind=dk)
    if debug:
        gdbg = nc.dram_tensor("gdbg", [128, 4 * NT * 16], F32, kind="ExternalOutput")

    H = 16
    SQK = 6

    r_ = np.arange(128)
    UT = (r_[:, None] <= r_[None, :]).astype(bfdt)
    LTs = (r_[:, None] > r_[None, :]).astype(bfdt)
    Bext = np.zeros((128, 130), bfdt)
    Bext[:, 0] = 1.0
    Bext[:, 1:129] = (r_[:, None] > r_[None, :]).astype(bfdt)
    maskc = np.zeros((128, 130), np.float32)
    maskc[:, 1:129] = np.where(r_[None, :] <= r_[:, None], 0.0, -1e30)
    onemI = (1.0 - np.eye(128)).astype(bfdt)

    with contextlib.ExitStack() as stack:
        cm.cb(1.0)
        sb = lambda name, shape, dt=BF16: stack.enter_context(nc.sbuf_tensor(pfx + name, shape, dt))
        ps = lambda name, shape, dt=F32: stack.enter_context(nc.psum_tensor(pfx + name, shape, dt))
        cm.sq_scr = sb("sq_scr2", [128, D], F32)

        ar = Arena(nc, stack, 48 * 1024, pfx=pfx)
        wq_sb = ar.at(0, "wq_sb", [128, 8 * 3 * D], BF16)
        woutT_sb = ar.at(0, "woutT_sb", [128, 8 * D], BF16)
        lnT4 = sb("lnT4", [128, 4096])
        qkvbuf = sb("qkvbuf", [128, 3 * D], F32)
        wgb_sb = sb("wgb_sb", [128, 8 * 64])
        bgb_sb = sb("bgb_sb", [1, 64])
        lgs = sb("lgs", [128, 64], F32)
        ab_sc = sb("ab_sc", [128, 80], F32)
        la_all = sb("la_all", [128, NT * H], F32)
        bt_all = sb("bt_all", [128, NT * H], F32)
        nbt_all = sb("nbt_all", [128, NT * H], F32)
        lah = sb("lah", [128, NT * H], F32)
        lal = sb("lal", [128, NT * H], F32)
        bscr = sb("bscr", [128, 16])
        nrm = sb("nrm", [128, 64], F32)
        qn_bf = sb("qn_bf", [128, D])
        kn_buf = sb("kn_buf", [128, D])
        vn_buf = sb("vn_buf", [128, D])
        qTst = sb("qTst", [128, 8 * 128])
        qTh = sb("qTh", [128, 4096])
        kTh = sb("kTh", [128, 4096])
        kv_sb = sb("kv_sb", [128, 256])
        utc = sb("utc", [128, 128])
        lts = sb("lts", [128, 128])
        bxc = sb("bxc", [128, 130])
        mkc = sb("mkc", [128, 130], F32)
        om1 = sb("om1", [128, 128])
        onescol = sb("onescol", [128, 1])
        A_sb = sb("A_sb", [128, 256])
        A2_sb = sb("A2_sb", [128, 256])
        Mm_sb = sb("Mm_sb", [128, 130], F32)
        Dg = sb("Dg", [128, 130])
        Dl = sb("Dl", [128, 128])
        DT = sb("DT", [128, 128])
        At_sb = sb("At_sb", [128, 128])
        Pn = [sb(f"Pn{k}", [128, 128]) for k in range(SQK + 1)]
        Pt = [sb(f"Pt{k}", [128, 128]) for k in range(SQK)]
        Rt_sb = sb("Rt_sb", [128, 128])
        bV_sb = sb("bV_sb", [128, 64])
        W2in = sb("W2in", [128, 64])
        Wv_sb = sb("Wv_sb", [128, 64])
        U1_sb = sb("U1_sb", [128, 64])
        W2n_sb = sb("W2n_sb", [128, 64])
        U2n_sb = sb("U2n_sb", [128, 64])
        Zn_sb = sb("Zn_sb", [128, 64])
        AW2nt = sb("AW2nt", [128, 128])
        O2c = sb("O2c", [128, 64])
        Obuf = sb("Obuf", [128, 64])
        St32 = sb("St32", [128, 64], F32)
        St_bf = sb("St_bf", [128, 64])
        scol = sb("scol", [128, 8], F32)
        gf = sb("gf", [128, 4], F32)
        gneg = sb("gneg", [128, 1], F32)
        oTh = sb("oTh", [128, 4096])
        oTW = sb("oTW", [128, 8 * 128])
        x1 = sb("x1", [128, D], F32)

        ps_q = ps("ps_q", [128, 512])
        ps_q2 = ps("ps_q2", [128, 512])
        ps_m = ps("ps_m", [128, 132])
        ps_e = ps("ps_e", [128, 256])
        ps_sq = ps("ps_sq", [128, 256])
        ps_w = ps("ps_w", [128, 320])
        ps_o = ps("ps_o", [128, 128])

        utd = nc.inline_tensor(UT, name=pfx + "utc0")
        ltd = nc.inline_tensor(LTs, name=pfx + "lts0")
        bxd = nc.inline_tensor(Bext, name=pfx + "bxc0")
        mkd = nc.inline_tensor(maskc, name=pfx + "mkc0")
        omd = nc.inline_tensor(onemI, name=pfx + "om10")

        s_dma = p.sem("dma")
        s_wgt = p.sem("wgt")
        s_qm = p.sem("qm")
        s_qe = p.sem("qe")
        s_ab = p.sem("ab")
        s_nv = p.sem("nv")
        s_nr = p.sem("nr")
        s_io = p.sem("io")
        s_knv = p.sem("knv")
        s_xc = p.sem("xc")
        s_hl = p.sem("hl")
        s_kv = p.sem("kv")
        s_tm = p.sem("tm")
        s_ve = p.sem("ve")
        s_se = p.sem("se")
        s_oio = p.sem("oio")
        s_wm = p.sem("wm")
        s_res = p.sem("res")
        s_xo = p.sem("xo")

        cm.load_consts(s_dma)
        for dsts, srcs in ((utc, utd), (lts, ltd), (bxc, bxd), (mkc, mkd), (om1, omd)):
            p.add("sync", lambda e, a=dsts, b=srcs: e.dma_start(a[:, :], b.ap()[:, :]), inc=[(s_dma, 16)])
        for _ge in ("tensor", "vector", "scalar"):
            p.add(_ge, None, wait=[w(s_dma)])
        p.add("gpsimd", lambda e: e.dma_start(
            wq_sb[:, :].rearrange("p (j c) -> p j c", j=8),
            wqkvT.ap().rearrange("(j p) c -> p j c", p=128)), inc=[(s_wgt, 16)])
        p.add("gpsimd", lambda e: e.dma_start(
            wgb_sb[:, :].rearrange("p (j c) -> p j c", j=8),
            wgbT.ap().rearrange("(j p) c -> p j c", p=128)), inc=[(s_wgt, 16)])
        p.add("gpsimd", lambda e: e.dma_start(bgb_sb[:, :], bgb.ap()[:, :]), inc=[(s_wgt, 16)])
        p.add("vector", lambda e: e.memset(onescol[:, :], 1.0))

        xr = x_in.ap().rearrange("(i p) d -> p i d", p=128)
        xwr = x_out.ap().rearrange("(i p) d -> p i d", p=128)

        for c4 in range(8):
            tiles = [c4 * 4 + kk for kk in range(4)]
            p.add("sync",
                  lambda e, c4=c4: e.dma_start(
                      cm.x_sb[:, :].rearrange("p (k d) -> p k d", k=4),
                      xr[:, c4 * 4:(c4 + 1) * 4, :]),
                  wait=[w(cm.s_lnbf)], inc=[(s_xc, 16)])
            xdep = [w(s_xc), w(s_nr)]
            for kk, i in enumerate(tiles):
                slot = i % 2
                cm.ln_tile(i, (lambda s=slot: cm.lnbf[:, s * D:(s + 1) * D]), deps=xdep,
                           src_col=kk * D)
                dst = lambda kk=kk: lnT4[:, :].rearrange("p (j t) -> p j t", j=8)[:, :, kk * 128:(kk + 1) * 128]
                cm.transpose_tile((lambda s=slot: cm.lnbf[:, s * D:(s + 1) * D]),
                                  dst, wait_src=[w(cm.s_lnbf)],
                                  evac_wait=[w(s_qm)])
            for kk, i in enumerate(tiles):
                for c6 in range(6):
                    bank = (ps_q if c6 % 2 == 0 else ps_q2)
                    for j in range(8):
                        p.add("tensor",
                              lambda e, j=j, kk=kk, c6=c6, bank=bank: e.matmul(
                                  bank[:, :], lnT4[:, j * 512 + kk * 128: j * 512 + kk * 128 + 128],
                                  wq_sb[:, j * 3 * D + c6 * 512: j * 3 * D + (c6 + 1) * 512],
                                  start=(j == 0), stop=(j == 7)),
                              wait=([w(cm.s_lnT), w(s_wgt), (s_qe, max(0, s_qe.n - 1))]) if j == 0 else (),
                              inc=[(s_qm, 1)] if j == 7 else ())
                    p.add("vector",
                          lambda e, c6=c6, bank=bank: e.tensor_copy(qkvbuf[:, c6 * 512:(c6 + 1) * 512], bank[:, :]),
                          wait=[w(s_qm), w(s_nv), w(s_ab)], inc=[(s_qe, 1)])
                for j in range(8):
                    p.add("tensor",
                          lambda e, j=j, kk=kk: e.matmul(
                              ps_m[:, 0:64], lnT4[:, j * 512 + kk * 128: j * 512 + kk * 128 + 128],
                              wgb_sb[:, j * 64:(j + 1) * 64],
                              start=(j == 0), stop=False),
                          wait=([(s_qe, max(0, s_qe.n - 1))]) if j == 0 else ())
                p.add("tensor",
                      lambda e: e.matmul(ps_m[:, 0:64], cm.ones1[0:1, 0:128], bgb_sb[0:1, :],
                                         start=False, stop=True),
                      inc=[(s_qm, 1)])
                p.add("vector", lambda e: e.tensor_copy(lgs[:, :], ps_m[:, 0:64]),
                      wait=[w(s_qm), w(s_ab)], inc=[(s_qe, 1)])
                p.add("scalar", lambda e: e.activation(ab_sc[:, 0:16], lgs[:, 32:48], AF.Exp, scale=-1.0),
                      wait=[w(s_qe), w(s_nv)], inc=[(s_ab, 1)])
                p.add("vector", lambda e: e.tensor_scalar(ab_sc[:, 16:32], ab_sc[:, 0:16], 1.0, None, op0=ALU.add),
                      wait=[w(s_ab)], selfsync=True)
                p.add("vector", lambda e, i=i: e.reciprocal(bt_all[:, i * H:(i + 1) * H], ab_sc[:, 16:32]),
                      selfsync=True)
                p.add("vector", lambda e, i=i: e.tensor_scalar(nbt_all[:, i * H:(i + 1) * H], bt_all[:, i * H:(i + 1) * H], -1.0, None, op0=ALU.mult),
                      inc=[(s_nv, 1)])
                p.add("scalar", lambda e: e.activation(ab_sc[:, 32:48], lgs[:, 0:16], AF.Abs),
                      wait=[w(s_nv)], inc=[(s_ab, 1)])
                p.add("scalar", lambda e: e.activation(ab_sc[:, 48:64], ab_sc[:, 32:48], AF.Exp, scale=-1.0),
                      inc=[(s_ab, 1)])
                p.add("scalar", lambda e: e.activation(ab_sc[:, 64:80], ab_sc[:, 48:64], AF.Ln, bias=cm.cb(1.0)),
                      inc=[(s_ab, 1)])
                p.add("vector", lambda e: e.tensor_scalar_min(ab_sc[:, 32:48], lgs[:, 0:16], 0.0),
                      wait=[w(s_ab)], selfsync=True)
                p.add("vector", lambda e, i=i: e.tensor_sub(la_all[:, i * H:(i + 1) * H], ab_sc[:, 32:48], ab_sc[:, 64:80]),
                      selfsync=True)
                p.add("vector", lambda e, i=i: e.tensor_copy(bscr[:, :], la_all[:, i * H:(i + 1) * H]),
                      selfsync=True)
                p.add("vector", lambda e, i=i: e.tensor_copy(lah[:, i * H:(i + 1) * H], bscr[:, :]),
                      selfsync=True)
                p.add("vector", lambda e, i=i: e.tensor_sub(lal[:, i * H:(i + 1) * H], la_all[:, i * H:(i + 1) * H], lah[:, i * H:(i + 1) * H]),
                      inc=[(s_nv, 1)])
                for which, col0 in ((0, 0), (1, D)):
                    p.add("scalar", lambda e, col0=col0, sq=cm.sq_scr: e.activation(
                        sq[:, :], qkvbuf[:, col0:col0 + D], AF.Square),
                        wait=[w(s_qe), w(s_nr)], inc=[(s_ab, 1)])
                    p.add("vector", lambda e, sq=cm.sq_scr: e.reduce_sum(
                        nrm[:, 0:16].rearrange("p (h o) -> p h o", h=16),
                        sq[:, :].rearrange("p (h d) -> p h d", h=16), axis=AX.X),
                        wait=[w(s_ab)], inc=[(s_nr, 1)])
                    p.add("scalar", lambda e: e.activation(nrm[:, 16:32], nrm[:, 0:16], AF.Ln),
                          wait=[w(s_nr)], inc=[(s_ab, 1)])
                    p.add("scalar", lambda e: e.activation(nrm[:, 32:48], nrm[:, 16:32], AF.Exp, scale=0.5),
                          inc=[(s_ab, 1)])
                    p.add("vector", lambda e: e.tensor_scalar_max(nrm[:, 32:48], nrm[:, 32:48], 1e-12),
                          wait=[w(s_ab)], selfsync=True)
                    p.add("vector", lambda e: e.reciprocal(nrm[:, 48:64], nrm[:, 32:48]),
                          selfsync=True)
                    rv = lambda: bass.AP(nrm.ap().tensor, 48, [[64, 128], [1, 16], [0, 64]])
                    dbuf = (lambda: qn_bf[:, :]) if which == 0 else (lambda: kn_buf[:, :])
                    p.add("vector", lambda e, col0=col0, rv=rv, dbuf=dbuf: e.tensor_tensor(
                        dbuf()[:, :].rearrange("p (h d) -> p h d", h=16),
                        qkvbuf[:, col0:col0 + D].rearrange("p (h d) -> p h d", h=16),
                        rv(), op=ALU.mult),
                        wait=[w(cm.s_tr), w(s_knv), w(s_qe)], inc=[(s_nv, 1)])
                    if which == 1:
                        p.add("sync", lambda e, i=i: e.dma_start(kn_scr.ap()[i][:, :], kn_buf[:, :]),
                              wait=[w(s_nv)], inc=[(s_knv, 16)])
                    for j in range(8):
                        p.add("tensor",
                              lambda e, j=j, dbuf=dbuf: e.transpose(
                                  cm.tr_ps[:, j * 128:(j + 1) * 128],
                                  dbuf()[:, j * 128:(j + 1) * 128], cm.ident[:, :]),
                              wait=([w(s_nv), w(cm.s_lnT)]) if j == 0 else (),
                              inc=[(cm.s_tr, 1)] if j == 7 else ())
                    p.add("vector", lambda e: e.tensor_copy(qTst[:, :], cm.tr_ps[:, 0:1024]),
                          wait=[w(cm.s_tr), w(s_io)], inc=[(cm.s_lnT, 1)])
                    scr = qT_scr if which == 0 else kT_scr
                    p.add("sync", lambda e, i=i, scr=scr: e.dma_start(
                        scr.ap().rearrange("p (g t) -> p g t", g=8)[:, :, i * 128:(i + 1) * 128],
                        qTst[:, :].rearrange("p (g t) -> p g t", g=8)),
                        wait=[w(cm.s_lnT)], inc=[(s_io, 16)])
                p.add("vector", lambda e: e.tensor_copy(vn_buf[:, :], qkvbuf[:, 2 * D:3 * D]),
                      wait=[w(s_qe), w(s_knv)], inc=[(s_nv, 1)])
                p.add("sync", lambda e, i=i: e.dma_start(vn_scr.ap()[i][:, :], vn_buf[:, :]),
                      wait=[w(s_nv)], inc=[(s_knv, 16)])

        if debug:
            for nmi, buf in enumerate((la_all, bt_all, lah, lal)):
                p.add("sync", lambda e, nmi=nmi, buf=buf: e.dma_start(
                    gdbg.ap()[:, nmi * NT * H:(nmi + 1) * NT * H], buf[:, :]),
                    wait=[w(s_nv)], inc=[(s_io, 16)])
        for h in range(H):
            g, pb = h // 2, (h % 2) * 64
            p.add("sync", lambda e, g=g, pb=pb: e.dma_start(
                qTh[0:64, :], qT_scr.ap()[pb:pb + 64, g * 4096:(g + 1) * 4096]),
                wait=[w(s_tm), w(s_io)], inc=[(s_hl, 16)])
            p.add("sync", lambda e, g=g, pb=pb: e.dma_start(
                kTh[0:64, :], kT_scr.ap()[pb:pb + 64, g * 4096:(g + 1) * 4096]),
                wait=[w(s_tm), w(s_io)], inc=[(s_hl, 16)])
            p.add("sync", lambda e, h=h: e.dma_start(
                kv_sb[:, 0:64], kn_scr.ap()[0][:, h * 64:(h + 1) * 64]),
                wait=[w(s_tm), w(s_knv)], inc=[(s_kv, 16)])
            p.add("sync", lambda e, h=h: e.dma_start(
                kv_sb[:, 64:128], vn_scr.ap()[0][:, h * 64:(h + 1) * 64]),
                wait=[w(s_tm), w(s_knv)], inc=[(s_kv, 16)])
            p.add("vector", lambda e: e.memset(St32[:, :], 0.0), wait=[w(s_tm), w(s_ve)], selfsync=True)
            p.add("vector", lambda e: e.memset(St_bf[:, :], 0.0), inc=[(s_ve, 1)])
            for c in range(NT):
                sl = (c % 2) * 128
                lahc = lambda c=c, h=h: lah[:, c * H + h: c * H + h + 1]
                lalc = lambda c=c, h=h: lal[:, c * H + h: c * H + h + 1]
                btc = lambda c=c, h=h: bt_all[:, c * H + h: c * H + h + 1]
                nbtc = lambda c=c, h=h: nbt_all[:, c * H + h: c * H + h + 1]
                qTc = lambda c=c: qTh[0:64, c * 128:(c + 1) * 128]
                kTc = lambda c=c: kTh[0:64, c * 128:(c + 1) * 128]
                knc = lambda sl=sl: kv_sb[:, sl:sl + 64]
                vnc = lambda sl=sl: kv_sb[:, sl + 64:sl + 128]
                p.add("vector", lambda e, lahc=lahc: e.tensor_scalar(A_sb[:, 0:128], utc[:, :], lahc(), None, op0=ALU.mult),
                      wait=[w(s_tm), w(s_se)], selfsync=True)
                p.add("vector", lambda e, lalc=lalc: e.tensor_scalar(A_sb[:, 128:256], utc[:, :], lalc(), None, op0=ALU.mult),
                      selfsync=True)
                p.add("vector", lambda e, lahc=lahc: e.tensor_scalar(A2_sb[:, 0:128], lts[:, :], lahc(), None, op0=ALU.mult),
                      selfsync=True)
                p.add("vector", lambda e, lalc=lalc: e.tensor_scalar(A2_sb[:, 128:256], lts[:, :], lalc(), None, op0=ALU.mult),
                      inc=[(s_ve, 1)])
                p.add("tensor", lambda e: e.matmul(ps_m[:, 0:130], A_sb[:, 0:128], bxc[:, :], start=True, stop=False),
                      wait=[w(s_ve), w(s_se)])
                p.add("tensor", lambda e: e.matmul(ps_m[:, 0:130], A_sb[:, 128:256], bxc[:, :], start=False, stop=True))
                p.add("tensor", lambda e: e.matmul(ps_m[:, 130:131], A2_sb[:, 0:128], onescol[:, :], start=True, stop=False))
                p.add("tensor", lambda e: e.matmul(ps_m[:, 130:131], A2_sb[:, 128:256], onescol[:, :], start=False, stop=True),
                      inc=[(s_tm, 1)])
                p.add("vector", lambda e: e.tensor_add(Mm_sb[:, :], mkc[:, :], ps_m[:, 0:130]),
                      wait=[w(s_tm), w(s_se)], selfsync=True)
                p.add("vector", lambda e: e.tensor_copy(scol[:, 0:1], ps_m[:, 130:131]), selfsync=True)
                p.add("vector", lambda e: e.tensor_copy(scol[:, 1:2], Mm_sb[:, 0:1]), selfsync=True)
                p.add("vector", lambda e: e.tensor_add(scol[:, 2:3], scol[:, 0:1], scol[:, 1:2]),
                      inc=[(s_ve, 1)])
                p.add("scalar", lambda e: e.activation(Dg[:, :], Mm_sb[:, :], AF.Exp),
                      wait=[w(s_ve)], inc=[(s_se, 1)])
                p.add("scalar", lambda e: e.activation(gf[:, 0:3], scol[:, 0:3], AF.Exp),
                      inc=[(s_se, 1)])
                p.add("vector", lambda e: e.tensor_mul(Dl[:, :], Dg[:, 1:129], om1[:, :]),
                      wait=[w(s_se)], selfsync=True)
                p.add("vector", lambda e: e.tensor_scalar(gneg[:, :], gf[:, 0:1], -1.0, None, op0=ALU.mult),
                      selfsync=True)
                p.add("vector", lambda e, btc=btc: e.tensor_mul(scol[:, 4:5], btc(), gf[:, 1:2]),
                      inc=[(s_ve, 1)])
                p.add("tensor", lambda e, kTc=kTc: e.matmul(ps_e[:, 0:128], kTc(), kTc(), start=True, stop=True),
                      wait=[w(s_ve), w(s_hl), w(s_kv)])
                p.add("tensor", lambda e, qTc=qTc, kTc=kTc: e.matmul(ps_e[:, 128:256], kTc(), qTc(), start=True, stop=True))
                p.add("tensor", lambda e: e.transpose(cm.tr_ps[:, 0:128], Dg[:, 1:129], cm.ident[:, :]),
                      inc=[(s_tm, 1)])
                p.add("vector", lambda e: e.tensor_copy(DT[:, :], cm.tr_ps[:, 0:128]),
                      wait=[w(s_tm)], selfsync=True)
                p.add("vector", lambda e: e.tensor_mul(At_sb[:, :], ps_e[:, 128:256], DT[:, :]),
                      selfsync=True)
                p.add("vector", lambda e: e.tensor_mul(Pn[0][:, :], ps_e[:, 0:128], Dl[:, :]),
                      selfsync=True)
                p.add("vector", lambda e, nbtc=nbtc: e.tensor_scalar(Pn[0][:, :], Pn[0][:, :], nbtc(), None, op0=ALU.mult),
                      inc=[(s_ve, 1)])
                p.add("tensor", lambda e: e.transpose(cm.tr_ps[:, 128:256], Pn[0][:, :], cm.ident[:, :]),
                      wait=[w(s_ve)], inc=[(s_tm, 1)])
                p.add("vector", lambda e: e.tensor_copy(Pt[0][:, :], cm.tr_ps[:, 128:256]),
                      wait=[w(s_tm)], inc=[(s_ve, 1)])
                for kq in range(1, SQK + 1):
                    bank = lambda kq=kq: ps_sq[:, (kq % 2) * 128:((kq % 2) + 1) * 128]
                    p.add("tensor", lambda e, kq=kq, bank=bank: e.matmul(
                        bank(), Pt[kq - 1][:, :], Pn[kq - 1][:, :], start=True, stop=True),
                        wait=[w(s_ve)], inc=[(s_tm, 1)])
                    p.add("vector", lambda e, kq=kq, bank=bank: e.tensor_copy(Pn[kq][:, :], bank()),
                          wait=[w(s_tm)], inc=[(s_ve, 1)])
                    if kq < SQK:
                        p.add("tensor", lambda e, kq=kq, bank=bank: e.matmul(
                            bank(), Pn[kq - 1][:, :], Pt[kq - 1][:, :], start=True, stop=True),
                            wait=[w(s_ve)], inc=[(s_tm, 1)])
                        p.add("vector", lambda e, kq=kq, bank=bank: e.tensor_copy(Pt[kq][:, :], bank()),
                              wait=[w(s_tm)], inc=[(s_ve, 1)])
                p.add("vector", lambda e: e.tensor_add(Rt_sb[:, :], cm.ident[:, :], Pt[0][:, :]),
                      wait=[w(s_ve)], inc=[(s_ve, 1)])
                for kq in range(1, SQK + 1):
                    p.add("tensor", lambda e, kq=kq: e.matmul(
                        ps_sq[:, 0:128], Pn[kq][:, :], Rt_sb[:, :], start=True, stop=True),
                        wait=[w(s_ve)], inc=[(s_tm, 1)])
                    p.add("vector", lambda e: e.tensor_add(Rt_sb[:, :], Rt_sb[:, :], ps_sq[:, 0:128]),
                          wait=[w(s_tm)], inc=[(s_ve, 1)])
                p.add("vector", lambda e, vnc=vnc, btc=btc: e.tensor_scalar(bV_sb[:, :], vnc(), btc(), None, op0=ALU.mult),
                      selfsync=True)
                p.add("vector", lambda e, knc=knc: e.tensor_scalar(W2in[:, :], knc(), scol[:, 4:5], None, op0=ALU.mult),
                      wait=[w(s_ve)], inc=[(s_ve, 1)])
                p.add("tensor", lambda e: e.matmul(ps_w[:, 0:64], Rt_sb[:, :], bV_sb[:, :], start=True, stop=True),
                      wait=[w(s_ve)])
                p.add("tensor", lambda e: e.matmul(ps_w[:, 64:128], Rt_sb[:, :], W2in[:, :], start=True, stop=True),
                      inc=[(s_tm, 1)])
                p.add("vector", lambda e: e.tensor_copy(Wv_sb[:, :], ps_w[:, 0:64]),
                      wait=[w(s_tm)], selfsync=True)
                p.add("vector", lambda e: e.tensor_scalar(U1_sb[:, :], ps_w[:, 0:64], gf[:, 0:1], None, op0=ALU.mult),
                      selfsync=True)
                p.add("vector", lambda e: e.tensor_scalar(W2n_sb[:, :], ps_w[:, 64:128], -1.0, None, op0=ALU.mult),
                      selfsync=True)
                p.add("vector", lambda e: e.tensor_scalar(U2n_sb[:, :], ps_w[:, 64:128], gneg[:, :], None, op0=ALU.mult),
                      inc=[(s_ve, 1)])
                p.add("tensor", lambda e, knc=knc: e.matmul(ps_w[0:64, 128:192], U2n_sb[:, :], knc(), start=True, stop=True),
                      wait=[w(s_ve)])
                p.add("tensor", lambda e: e.matmul(ps_w[0:64, 192:320], W2n_sb[:, :], At_sb[:, :], start=True, stop=True),
                      inc=[(s_tm, 1)])
                p.add("vector", lambda e: e.tensor_copy(Zn_sb[:, :], ps_w[:, 128:192]),
                      wait=[w(s_tm)], selfsync=True)
                p.add("vector", lambda e: e.tensor_copy(AW2nt[:, :], ps_w[:, 192:320]),
                      inc=[(s_ve, 1)])
                p.add("tensor", lambda e, qTc=qTc: e.matmul(ps_o[:, 0:64], qTc(), St_bf[0:64, :], start=True, stop=True),
                      wait=[w(s_ve)])
                p.add("tensor", lambda e: e.matmul(ps_o[:, 64:128], At_sb[:, :], Wv_sb[:, :], start=True, stop=False))
                p.add("tensor", lambda e: e.matmul(ps_o[:, 64:128], AW2nt[0:64, :], St_bf[0:64, :], start=False, stop=True),
                      inc=[(s_tm, 1)])
                p.add("vector", lambda e: e.tensor_copy(O2c[:, :], ps_o[:, 64:128]),
                      wait=[w(s_tm)], selfsync=True)
                p.add("vector", lambda e: e.scalar_tensor_tensor(Obuf[:, :], ps_o[:, 0:64], gf[:, 1:2], O2c[:, :], op0=ALU.mult, op1=ALU.add),
                      inc=[(s_ve, 1)])
                p.add("tensor", lambda e: e.transpose(cm.tr_ps[0:64, 256:384], Obuf[:, :], cm.ident[:, :]),
                      wait=[w(s_ve)], inc=[(s_tm, 1)])
                p.add("vector", lambda e, c=c: e.tensor_copy(oTh[0:64, c * 128:(c + 1) * 128], cm.tr_ps[0:64, 256:384]),
                      wait=([w(s_tm), w(s_oio)] if c == 0 else [w(s_tm)]), inc=[(s_ve, 1)])
                p.add("tensor", lambda e, knc=knc: e.matmul(ps_m[0:64, 0:64], knc(), U1_sb[:, :], start=True, stop=False),
                      wait=[w(s_ve)])
                p.add("tensor", lambda e: e.matmul(ps_m[0:64, 0:64], Zn_sb[0:64, :], St_bf[0:64, :], start=False, stop=True),
                      inc=[(s_tm, 1)])
                p.add("vector", lambda e: e.scalar_tensor_tensor(St32[0:64, :], St32[0:64, :], gf[0:64, 2:3], ps_m[0:64, 0:64], op0=ALU.mult, op1=ALU.add),
                      wait=[w(s_tm)], selfsync=True)
                p.add("vector", lambda e: e.tensor_copy(St_bf[0:64, :], St32[0:64, :]),
                      inc=[(s_ve, 1)])
                if c + 1 < NT:
                    sln = ((c + 1) % 2) * 128
                    p.add("sync", lambda e, c=c, h=h, sln=sln: e.dma_start(
                        kv_sb[:, sln:sln + 64], kn_scr.ap()[c + 1][:, h * 64:(h + 1) * 64]),
                        wait=[w(s_tm)], inc=[(s_kv, 16)])
                    p.add("sync", lambda e, c=c, h=h, sln=sln: e.dma_start(
                        kv_sb[:, sln + 64:sln + 128], vn_scr.ap()[c + 1][:, h * 64:(h + 1) * 64]),
                        inc=[(s_kv, 16)])
            p.add("sync", lambda e, g=g, pb=pb: e.dma_start(
                oT_scr.ap()[g][pb:pb + 64, :], oTh[0:64, :]),
                wait=[w(s_ve)], inc=[(s_oio, 16)])

        p.add("gpsimd", lambda e: e.dma_start(
            woutT_sb[:, :].rearrange("p (j c) -> p j c", j=8),
            woutT.ap().rearrange("(j p) c -> p j c", p=128)),
            wait=[w(s_qm)], inc=[(s_wgt, 16)])
        for i in range(NT):
            p.add("sync", lambda e, i=i: e.dma_start(
                oTW[:, :].rearrange("p (g t) -> p g t", g=8),
                bass.AP(oT_scr.ap().tensor, i * 128, [[4096, 128], [128 * 4096, 8], [1, 128]])),
                wait=[w(s_wm), w(s_oio)], inc=[(s_hl, 16)])
            p.add("sync", lambda e, i=i: e.dma_start(x1[:, :], xr[:, i, :]),
                  wait=[w(s_res), w(s_xo)], inc=[(s_hl, 16)])
            for dh in range(2):
                bank = ps_q if dh == 0 else ps_q2
                for gq in range(8):
                    p.add("tensor",
                          lambda e, gq=gq, dh=dh, bank=bank: e.matmul(
                              bank[:, :],
                              oTW[:, gq * 128:(gq + 1) * 128],
                              woutT_sb[:, gq * D + dh * 512: gq * D + (dh + 1) * 512],
                              start=(gq == 0), stop=(gq == 7)),
                          wait=([w(s_hl), w(s_wgt), w(s_res), w(s_qe), w(s_tm), w(s_ve)]) if (gq == 0 and dh == 0) else (),
                          inc=[(s_wm, 1)] if gq == 7 else ())
            p.add("vector", lambda e: e.tensor_add(x1[:, 0:512], x1[:, 0:512], ps_q[:, :]),
                  wait=[w(s_wm), w(s_hl)], selfsync=True)
            p.add("vector", lambda e: e.tensor_add(x1[:, 512:1024], x1[:, 512:1024], ps_q2[:, :]),
                  inc=[(s_res, 1)])
            p.add("sync", lambda e, i=i: e.dma_start(xwr[:, i, :], x1[:, :]),
                  wait=[w(s_res)], inc=[(s_xo, 16)])

        p.add("sync", lambda e, v=s_xo.n: e.wait_ge(s_xo.h, v))


def l0_attn_inmap(inputs, b, x):
    wg = np.asarray(inputs["L0_attn_Wg"], np.float32)
    wb = np.asarray(inputs["L0_attn_Wb"], np.float32)
    bg = np.asarray(inputs["L0_attn_bg"], np.float32)
    bb = np.asarray(inputs["L0_attn_bb"], np.float32)
    wgb = np.zeros((1024, 64), np.float32)
    wgb[:, 0:16] = wg.T
    wgb[:, 32:48] = wb.T
    bgbv = np.zeros((1, 64), np.float32)
    bgbv[0, 0:16] = bg
    bgbv[0, 32:48] = bb
    return {
        "x_in": np.ascontiguousarray(x[b]),
        "wqkvT": hT(inputs["L0_attn_Wqkv"]),
        "wgbT": wgb,
        "bgb": bgbv,
        "woutT": hT(inputs["L0_attn_Wout"]),
    }


SUBLAYERS = [
    ("L0_attn", emit_L0_attn, l0_attn_inmap),
    ("L0_ffn", emit_L0_ffn, l0_ffn_inmap),
    ("L1_attn", emit_L1_attn, l1_attn_inmap),
    ("L1_ffn", emit_L1_ffn, l1_ffn_inmap),
    ("L2_attn", emit_L2_attn, l2_attn_inmap),
    ("L2_ffn", emit_L2_ffn, l2_ffn_inmap),
    ("L3_attn", emit_L3_attn, l3_attn_inmap),
    ("L3_ffn", emit_L3_ffn, l3_ffn_inmap),
]


class Fz:

    def __init__(self, nc, p, cm):
        self.nc = nc
        self.p = p
        self.cm = cm


def barrier(p, cm, mode="full"):
    if mode == "none":
        return
    b = p.sem("bar")
    p.add("vector", lambda e: e.tensor_copy(cm.stat[:, 14:15], cm.stat[:, 0:1]),
          inc=[(b, 1)])
    p.add("scalar", lambda e: e.activation(cm.stat[:, 13:14], cm.stat[:, 0:1], AF.Copy),
          inc=[(b, 1)])
    if mode == "joins":
        return
    tgt = b.n
    engines = ("vector", "scalar", "tensor") if mode == "nq" else ENGINES
    for e in engines:
        p.add(e, None, wait=[(b, tgt)])


def build_fused(chain=None, bar_mode="full", expose_chain=False):
    subs = SUBLAYERS if chain is None else [SUBLAYERS[i] for i in chain]
    n = len(subs)
    nc = bass.Bass("TRN2", target_bir_lowering=False)
    xs = [nc.dram_tensor("x_in", [T, D], F32, kind="ExternalInput")]
    for k in range(n - 1):
        xs.append(nc.dram_tensor(f"xchain{k}", [T, D], F32,
                                 kind="ExternalOutput" if expose_chain else "Internal"))
    xs.append(nc.dram_tensor("x_out", [T, D], F32, kind="ExternalOutput"))
    with contextlib.ExitStack() as gstack:
        p = Prog(nc, gstack)
        cm = Common(p)
        fz = Fz(nc, p, cm)
        for k, (name, emit_fn, mkmap) in enumerate(subs):
            emit_fn(fz, f"S{k}_", xs[k], xs[k + 1])
            if k < n - 1:
                barrier(p, cm, mode=bar_mode)
        cm.finalize_consts()
        p.emit()
    return nc


def kernel(**inputs):
    x = np.asarray(inputs["L0_attn_x"], np.float32)
    B = x.shape[0]
    for chain in ([0, 1, 2, 3], [4, 5, 6, 7]):
        nc = build_fused(chain=chain)
        in_maps = []
        for b in range(B):
            m = {"x_in": np.ascontiguousarray(x[b])}
            for k, idx in enumerate(chain):
                name, emit_fn, mkmap = SUBLAYERS[idx]
                sm = mkmap(inputs, b, x)
                sm.pop("x_in", None)
                for kk, v in sm.items():
                    m[f"S{k}_" + kk] = v
            in_maps.append(m)
        res = run_bass_kernel_spmd(nc, in_maps, core_ids=list(range(B)))
        x = np.stack([np.asarray(r["x_out"]) for r in res.results], 0)
    return x
